# version 12
# baseline (speedup 1.0000x reference)
import contextlib
import numpy as np
import concourse.bass as bass
import concourse.mybir as mybir
from concourse.bass_utils import run_bass_kernel_spmd

F32 = mybir.dt.float32
BF16 = mybir.dt.bfloat16
U32 = mybir.dt.uint32
AF = mybir.ActivationFunctionType
ALU = mybir.AluOpType
AX = mybir.AxisListType

NCORES = 8
T = 2048
D = 1024
CH = 8
SUB = 512
NSUB = T // SUB
N_SC = 512
NG = 4
L = 64
CJ = T // L
PI = float(np.pi)
TWO_PI = float(2 * np.pi)

ENGINES = ("sync", "tensor", "vector", "scalar", "gpsimd")


class Sem:
    def __init__(self, handle):
        self.h = handle
        self.v = 0

    def bump(self, n):
        self.v += n
        return self.v


class Prog:

    def __init__(self, nc, es):
        self.nc = nc
        self.es = es
        self.q = {e: [] for e in ENGINES}
        self._semnames = {}
        self._last = {}

    def sem(self, name):
        if name not in self._semnames:
            self._semnames[name] = Sem(self.es.enter_context(self.nc.semaphore(name)))
        return self._semnames[name]

    def emit(self, eng, fn):
        self.q[eng].append(fn)

    def wait_last(self, eng, producer):
        last = self._last.get(producer)
        if last is not None:
            self.wait(eng, last[0], last[1])

    def wait(self, eng, sem, v=None):
        if v is None:
            v = sem.v
        self.emit(eng, lambda e, s=sem.h, vv=v: e.wait_ge(s, vv))

    SERIAL = ("vector", "scalar", "gpsimd")

    def op(self, eng, name, *args, sem=None, inc=1, **kw):
        if eng in self.SERIAL:
            last = self._last.get(eng)
            if last is not None:
                self.wait(eng, last[0], last[1])
            if sem is None:
                sem = self.sem("_chain_" + eng)
                inc = 1
            sem.bump(inc)
            self._last[eng] = (sem, sem.v)

            def fn(e, n=name, a=args, k=kw, s=sem.h, i=inc):
                getattr(e, n)(*a, **k).then_inc(s, i)
        elif sem is not None:
            sem.bump(inc)

            def fn(e, n=name, a=args, k=kw, s=sem.h, i=inc):
                getattr(e, n)(*a, **k).then_inc(s, i)
        else:
            def fn(e, n=name, a=args, k=kw):
                getattr(e, n)(*a, **k)
        self.emit(eng, fn)

    def dma(self, eng, out, in_, sem, **kw):
        sem.bump(16)
        self.emit(eng, lambda e, o=out, i=in_, s=sem.h, k=kw: e.dma_start(out=o, in_=i, **k).then_inc(s, 16))

    def global_barrier(self):
        for eng in ENGINES:
            for s in self._semnames.values():
                if s.v > 0:
                    self.wait(eng, s, s.v)

    def transpose_to(self, ps_tile, in_ap, id_t, dest_ap, mm, sem=None):
        self.wait_last("tensor", "vector")
        self.op("tensor", "transpose", ps_tile[:], in_ap, id_t[:], sem=mm)
        self.wait("vector", mm)
        self.op("vector", "tensor_copy", dest_ap, ps_tile[:], sem=sem)

    def finalize(self):
        block = self.es.enter_context(self.nc.Block())
        for eng in ENGINES:
            getattr(block, eng)(lambda e, q=tuple(self.q[eng]): [f(e) for f in q])


def make_dram(nc, rename=None, internal=(), cache=None):
    cache = {} if cache is None else cache
    rename = rename or {}

    def dram(name, shape, dtype, io):
        canon = rename.get(name, name)
        if canon in cache:
            return cache[canon]
        if canon in internal:
            kind = "Internal"
        else:
            kind = "ExternalInput" if io == "in" else "ExternalOutput"
        ap = nc.dram_tensor(canon, list(shape), dtype, kind=kind).ap()
        cache[canon] = ap
        return ap

    return dram


def emit_A(p, nc, dram, stages=6):
    es = contextlib.ExitStack()

    xT = dram("xT", [D, T], F32, "in")
    BreT = dram("BreT", [D, N_SC], F32, "in")
    BimT = dram("BimT", [D, N_SC], F32, "in")
    lamp = dram("lamp", [128, 12], F32, "in")
    iotaL = dram("iotaL", [128, L], F32, "in")
    cvec = dram("cvec", [128, 8], F32, "in")
    o_cre = dram("o_cre", [128, NG * T], F32, "out")
    o_cim = dram("o_cim", [128, NG * T], F32, "out")
    o_bsum = dram("o_bsum", [128, 8], F32, "out")

    sb = lambda name, shape, dt=F32: es.enter_context(nc.sbuf_tensor(name, shape, dt, allow_name_mangling=True))
    ps = lambda name, shape, dt=F32: es.enter_context(nc.psum_tensor(name, shape, dt, allow_name_mangling=True))

    lam_t = sb("lam_t", [128, 12])
    cvec_t = sb("cvec_t", [128, 8])
    iotaL_t = sb("iotaL_t", [128, L])
    c4 = sb("c4", [128, 4 * 16])
    tabs = sb("tabs", [128, NG, 4 * L])
    ang = sb("ang", [128, NG, L])
    s_k = sb("s_k", [128, NG, L])
    c_k = sb("c_k", [128, NG, L])
    m_k = sb("m_k", [128, NG, L])
    t_k = sb("t_k", [128, NG, L])
    w_re = sb("w_re", [128, NG, T])
    w_im = sb("w_im", [128, NG, T])
    ev1 = sb("ev1", [128, SUB])
    ev2 = sb("ev2", [128, SUB])
    sA_re = sb("sA_re", [128, NG, CJ])
    sA_im = sb("sA_im", [128, NG, CJ])
    sB_re = sb("sB_re", [128, NG, CJ])
    sB_im = sb("sB_im", [128, NG, CJ])
    t32a = sb("t32a", [128, NG, CJ])
    t32b = sb("t32b", [128, NG, CJ])
    hsr = sb("hsr", [128, NG, 5])
    hsi = sb("hsi", [128, NG, 5])
    l63r_all = sb("l63r_all", [128, NG])
    l63i_all = sb("l63i_all", [128, NG])
    bsum_t = sb("bsum_t", [128, 8])
    qpr = sb("qpr", [128, NG, 6])
    qpi = sb("qpi", [128, NG, 6])
    qnr = sb("qnr", [128, NG, 6])
    qni = sb("qni", [128, NG, 6])

    es1 = contextlib.ExitStack()
    sb1 = lambda name, shape, dt=F32: es1.enter_context(nc.sbuf_tensor(name, shape, dt, allow_name_mangling=True))
    x_bf = sb1("x_bf", [128, CH, T], BF16)
    Bt_re = sb1("Bt_re", [128, CH, N_SC], BF16)
    Bt_im = sb1("Bt_im", [128, CH, N_SC], BF16)

    ps_re = ps("ps_re", [128, SUB])
    ps_im = ps("ps_im", [128, SUB])
    ps_re2 = ps("ps_re2", [128, SUB])
    ps_im2 = ps("ps_im2", [128, SUB])

    ld = p.sem("ld")
    cm = p.sem("cm")
    mm = p.sem("mm")
    sc = p.sem("sc")
    st = p.sem("st")

    p.dma("sync", lam_t[:], lamp, ld)
    p.dma("sync", iotaL_t[:], iotaL, ld)
    p.dma("sync", cvec_t[:], cvec, ld)
    ld_small = ld.v
    ldg = p.sem("ldg")
    p.dma("gpsimd", x_bf[:], xT.rearrange("(c p) t -> p c t", p=128), ldg)
    p.dma("gpsimd", Bt_re[:], BreT.rearrange("(c p) n -> p c n", p=128), ldg)
    p.dma("gpsimd", Bt_im[:], BimT.rearrange("(c p) n -> p c n", p=128), ldg)
    ld_big = ldg.v

    if stages < 2:
        p.wait("sync", ld, ld_small)
        p.wait("sync", ldg, ld_big)
        p.dma("sync", o_bsum, bsum_t[:], st)
        p.wait("sync", st)
        es1.close(); es.close(); return

    def slot(i):
        return c4[:, 4 * i:4 * i + 4]
    NU, TH, GA, ENU, RHO, THR, SINT, COST, LR, LI, LAr, LAi, TMP1, TMP2, TMP3, TMP4 = range(16)

    p.wait("scalar", ld, ld_small)
    p.op("scalar", "activation", slot(ENU), lamp_slice(lam_t, 0), AF.Exp)
    p.op("scalar", "activation", slot(RHO), slot(ENU), AF.Exp, scale=-1.0)
    p.op("scalar", "activation", slot(TH), lamp_slice(lam_t, 1), AF.Exp)
    p.op("scalar", "activation", slot(GA), lamp_slice(lam_t, 2), AF.Exp, sem=sc)
    p.wait("vector", sc)
    p.op("vector", "tensor_scalar", slot(TMP1), slot(TH), -PI, None, ALU.add)
    p.op("vector", "tensor_scalar", slot(TMP1), slot(TMP1), 1e9, None, ALU.mult)
    p.op("vector", "tensor_scalar", slot(TMP1), slot(TMP1), 0.0, None, ALU.max)
    p.op("vector", "tensor_scalar", slot(TMP1), slot(TMP1), 1.0, None, ALU.min)
    p.op("vector", "scalar_tensor_tensor", slot(THR), slot(TMP1), -TWO_PI, slot(TH), ALU.mult, ALU.add, sem=cm)
    p.wait("scalar", cm)
    p.op("scalar", "activation", slot(SINT), slot(THR), AF.Sin, sem=sc)
    p.wait("vector", sc)
    p.op("vector", "tensor_scalar", slot(TMP1), slot(THR), -1.0, None, ALU.mult)
    p.op("vector", "tensor_tensor", slot(TMP1), slot(TMP1), slot(THR), ALU.max, sem=cm)
    p.wait("scalar", cm)
    p.op("scalar", "activation", slot(COST), slot(TMP1), AF.Sin, scale=-1.0, bias=cvec_t[:, 0:1], sem=sc)
    p.wait("vector", sc)
    p.op("vector", "tensor_tensor", slot(LR), slot(RHO), slot(COST), ALU.mult)
    p.op("vector", "tensor_tensor", slot(LI), slot(RHO), slot(SINT), ALU.mult, sem=cm)

    def Tn_re(g):
        return tabs[:, g, 0 * L:1 * L]

    def Tn_im(g):
        return tabs[:, g, 1 * L:2 * L]

    def Tp_re(g):
        return tabs[:, g, 2 * L:3 * L]

    def Tp_im(g):
        return tabs[:, g, 3 * L:4 * L]

    NLr, NLi = NU, TH
    p.op("vector", "tensor_tensor", slot(TMP2), slot(RHO), slot(RHO), ALU.mult)
    p.op("vector", "reciprocal", slot(TMP2), slot(TMP2))
    p.op("vector", "tensor_tensor", slot(NLr), slot(LR), slot(TMP2), ALU.mult)
    p.op("vector", "tensor_tensor", slot(NLi), slot(LI), slot(TMP2), ALU.mult)
    p.op("vector", "tensor_scalar", slot(NLi), slot(NLi), -1.0, None, ALU.mult)

    def cplx_mulv(ar, ai, br, bi, outr, outi, tmp1, tmp2):
        p.op("vector", "tensor_tensor", tmp1, ar, br, ALU.mult)
        p.op("vector", "tensor_tensor", tmp2, ai, bi, ALU.mult)
        p.op("vector", "tensor_tensor", outr, tmp1, tmp2, ALU.subtract)
        p.op("vector", "tensor_tensor", tmp1, ar, bi, ALU.mult)
        p.op("vector", "tensor_tensor", tmp2, ai, br, ALU.mult)
        p.op("vector", "tensor_tensor", outi, tmp1, tmp2, ALU.add)

    p.op("vector", "tensor_copy", qpr[:, :, 0], slot(LR))
    p.op("vector", "tensor_copy", qpi[:, :, 0], slot(LI))
    p.op("vector", "tensor_copy", qnr[:, :, 0], slot(NLr))
    p.op("vector", "tensor_copy", qni[:, :, 0], slot(NLi))
    for m in range(1, 6):
        cplx_mulv(qpr[:, :, m - 1], qpi[:, :, m - 1], qpr[:, :, m - 1], qpi[:, :, m - 1],
                  qpr[:, :, m], qpi[:, :, m], slot(TMP1), slot(TMP2))
        cplx_mulv(qnr[:, :, m - 1], qni[:, :, m - 1], qnr[:, :, m - 1], qni[:, :, m - 1],
                  qnr[:, :, m], qni[:, :, m], slot(TMP1), slot(TMP2))
    p.op("vector", "memset", tabs[:, :, 0 * L:0 * L + 1], 1.0)
    p.op("vector", "memset", tabs[:, :, 1 * L:1 * L + 1], 0.0)
    p.op("vector", "memset", tabs[:, :, 2 * L:2 * L + 1], 1.0)
    p.op("vector", "memset", tabs[:, :, 3 * L:3 * L + 1], 0.0)
    for m in range(6):
        d = 1 << m
        for g in range(NG):
            for (tr, ti, qr_, qi_) in ((Tp_re(g), Tp_im(g), qpr, qpi), (Tn_re(g), Tn_im(g), qnr, qni)):
                t1 = ang[:, g, 0:d]
                t2 = s_k[:, g, 0:d]
                p.op("vector", "tensor_scalar", t1, tr[:, 0:d], qr_[:, g, m:m + 1], None, ALU.mult)
                p.op("vector", "tensor_scalar", t2, ti[:, 0:d], qi_[:, g, m:m + 1], None, ALU.mult)
                p.op("vector", "tensor_tensor", tr[:, d:2 * d], t1, t2, ALU.subtract)
                p.op("vector", "tensor_scalar", t1, tr[:, 0:d], qi_[:, g, m:m + 1], None, ALU.mult)
                p.op("vector", "tensor_scalar", t2, ti[:, 0:d], qr_[:, g, m:m + 1], None, ALU.mult)
                p.op("vector", "tensor_tensor", ti[:, d:2 * d], t1, t2, ALU.add)
    for g in range(NG):
        p.op("vector", "tensor_scalar", Tn_re(g), Tn_re(g), slot(GA)[:, g:g + 1], None, ALU.mult)
        p.op("vector", "tensor_scalar", Tn_im(g), Tn_im(g), slot(GA)[:, g:g + 1], None,
             ALU.mult, sem=(cm if g == NG - 1 else None))
    TABS_DONE = cm.v

    if stages < 3:
        p.wait("sync", cm, TABS_DONE)
        p.dma("sync", o_bsum, bsum_t[:], st)
        p.wait("sync", st)
        es1.close(); es.close(); return

    p.wait("tensor", ldg, ld_big)
    p.wait("tensor", cm, TABS_DONE)
    psums = [(ps_re, ps_im), (ps_re2, ps_im2)]
    n_evac = 0
    for g in range(NG):
        for s in range(NSUB):
            i = g * NSUB + s
            pre, pim = psums[i % 2]
            if i >= 2:
                p.wait("tensor", cm, TABS_DONE + i - 1)
            for ic in range(CH):
                p.op("tensor", "matmul", pre[:], Bt_re[:, ic, g * 128:(g + 1) * 128],
                     x_bf[:, ic, s * SUB:(s + 1) * SUB], start=(ic == 0), stop=(ic == CH - 1))
            for ic in range(CH):
                p.op("tensor", "matmul", pim[:], Bt_im[:, ic, g * 128:(g + 1) * 128],
                     x_bf[:, ic, s * SUB:(s + 1) * SUB], start=(ic == 0), stop=(ic == CH - 1),
                     sem=(mm if ic == CH - 1 else None))
            p.wait("vector", mm)
            pre3 = pre[:].rearrange("p (j k) -> p j k", k=L)
            pim3 = pim[:].rearrange("p (j k) -> p j k", k=L)
            ev1_3 = ev1[:].rearrange("p (j k) -> p j k", k=L)
            ev2_3 = ev2[:].rearrange("p (j k) -> p j k", k=L)
            wre3 = w_re[:, g, s * SUB:(s + 1) * SUB].rearrange("p (j k) -> p j k", k=L)
            wim3 = w_im[:, g, s * SUB:(s + 1) * SUB].rearrange("p (j k) -> p j k", k=L)
            tnr = Tn_re(g).unsqueeze(1).broadcast_to([128, SUB // L, L])
            tni = Tn_im(g).unsqueeze(1).broadcast_to([128, SUB // L, L])
            p.op("vector", "tensor_tensor", ev1_3, pre3, tnr, ALU.mult)
            p.op("vector", "tensor_tensor", ev2_3, pim3, tni, ALU.mult)
            p.op("vector", "tensor_tensor", wre3, ev1_3, ev2_3, ALU.subtract)
            p.op("vector", "tensor_tensor", ev1_3, pre3, tni, ALU.mult)
            p.op("vector", "tensor_tensor", ev2_3, pim3, tnr, ALU.mult)
            p.op("vector", "tensor_tensor", wim3, ev1_3, ev2_3, ALU.add, sem=cm)
            n_evac += 1
    es1.close()
    cum_re = sb("cum_re_s", [128, NG, T])
    cum_im = sb("cum_im_s", [128, NG, T])
    msk = sb("msk", [128, NG * T])

    if stages < 4:
        p.wait("sync", cm)
        p.dma("sync", o_cre, w_re[:].rearrange("p g t -> p (g t)"), st)
        p.dma("sync", o_cim, w_im[:].rearrange("p g t -> p (g t)"), st)
        p.dma("sync", o_cre[:, 5000:5064], c4[:], st)
        p.dma("sync", o_cre[:, 6000:6000 + NG * 4 * L], tabs[:].rearrange("p g k -> p (g k)"), st)
        p.dma("sync", o_cre[:, 7100:7100 + NG * 6], qnr[:].rearrange("p g m -> p (g m)"), st)
        p.dma("sync", o_cre[:, 7200:7200 + NG * 6], qpr[:].rearrange("p g m -> p (g m)"), st)
        p.dma("sync", o_bsum[:, 0:4], tabs[:, 0, 0:4], st)
        p.wait("sync", st)
        es1.close(); es.close(); return

    p.op("vector", "memset", msk[:], 1.0)
    p.op("vector", "memset", msk[:].rearrange("p (c k) -> p c k", k=L)[:, :, 0:1], 0.0)
    cre_f = cum_re[:].rearrange("p g t -> p (g t)")
    cim_f = cum_im[:].rearrange("p g t -> p (g t)")
    wre_f = w_re[:].rearrange("p g t -> p (g t)")
    wim_f = w_im[:].rearrange("p g t -> p (g t)")
    p.op("vector", "tensor_tensor_scan", cre_f, msk[:], wre_f, 0.0, ALU.mult, ALU.add)
    p.op("vector", "tensor_tensor_scan", cim_f, msk[:], wim_f, 0.0, ALU.mult, ALU.add, sem=cm)

    if stages < 5:
        p.wait("sync", cm)
        p.dma("sync", o_cre, cre_f, st)
        p.dma("sync", o_cim, cim_f, st)
        p.wait("sync", st)
        es1.close(); es.close(); return

    cre4 = cum_re[:].rearrange("p g (j k) -> p g j k", k=L)
    cim4 = cum_im[:].rearrange("p g (j k) -> p g j k", k=L)
    p.op("vector", "tensor_copy", sA_re[:], cre4[:, :, :, L - 1])
    p.op("vector", "tensor_copy", sA_im[:], cim4[:, :, :, L - 1])
    for g in range(NG):
        p.op("vector", "tensor_copy", l63r_all[:, g:g + 1], Tp_re(g)[:, L - 1:L])
        p.op("vector", "tensor_copy", l63i_all[:, g:g + 1], Tp_im(g)[:, L - 1:L])
    for g in range(NG):
        p.op("vector", "tensor_scalar", t32a[:, g, :], sA_re[:, g, :], l63r_all[:, g:g + 1], None, ALU.mult)
        p.op("vector", "tensor_scalar", t32b[:, g, :], sA_im[:, g, :], l63i_all[:, g:g + 1], None, ALU.mult)
        p.op("vector", "tensor_scalar", sB_re[:, g, :], sA_re[:, g, :], l63i_all[:, g:g + 1], None, ALU.mult)
        p.op("vector", "tensor_scalar", sB_im[:, g, :], sA_im[:, g, :], l63r_all[:, g:g + 1], None, ALU.mult)
    p.op("vector", "tensor_tensor", sA_re[:], t32a[:], t32b[:], ALU.subtract)
    p.op("vector", "tensor_tensor", sA_im[:], sB_re[:], sB_im[:], ALU.add)

    def cplx_mul(ar, ai, br, bi, outr, outi, tmp1, tmp2):
        p.op("vector", "tensor_tensor", tmp1, ar, br, ALU.mult)
        p.op("vector", "tensor_tensor", tmp2, ai, bi, ALU.mult)
        p.op("vector", "tensor_tensor", outr, tmp1, tmp2, ALU.subtract)
        p.op("vector", "tensor_tensor", tmp1, ar, bi, ALU.mult)
        p.op("vector", "tensor_tensor", tmp2, ai, br, ALU.mult)
        p.op("vector", "tensor_tensor", outi, tmp1, tmp2, ALU.add)

    cplx_mul(l63r_all[:], l63i_all[:], slot(LR), slot(LI), slot(LAr), slot(LAi), slot(TMP1), slot(TMP2))
    p.op("vector", "tensor_copy", hsr[:, :, 0], slot(LAr))
    p.op("vector", "tensor_copy", hsi[:, :, 0], slot(LAi))
    for i in range(1, 5):
        cplx_mul(hsr[:, :, i - 1], hsi[:, :, i - 1], hsr[:, :, i - 1], hsi[:, :, i - 1],
                 hsr[:, :, i], hsi[:, :, i], slot(TMP1), slot(TMP2))
    cur_r, cur_i, oth_r, oth_i = sA_re, sA_im, sB_re, sB_im
    for pw in range(5):
        d = 1 << pw
        mr = hsr[:, :, pw:pw + 1].broadcast_to([128, NG, CJ - d])
        mi = hsi[:, :, pw:pw + 1].broadcast_to([128, NG, CJ - d])
        p.op("vector", "tensor_copy", oth_r[:, :, 0:d], cur_r[:, :, 0:d])
        p.op("vector", "tensor_copy", oth_i[:, :, 0:d], cur_i[:, :, 0:d])
        p.op("vector", "tensor_tensor", t32a[:, :, 0:CJ - d], cur_r[:, :, 0:CJ - d], mr, ALU.mult)
        p.op("vector", "tensor_tensor", t32b[:, :, 0:CJ - d], cur_i[:, :, 0:CJ - d], mi, ALU.mult)
        p.op("vector", "tensor_tensor", t32a[:, :, 0:CJ - d], t32a[:, :, 0:CJ - d], t32b[:, :, 0:CJ - d], ALU.subtract)
        p.op("vector", "tensor_tensor", oth_r[:, :, d:], cur_r[:, :, d:], t32a[:, :, 0:CJ - d], ALU.add)
        p.op("vector", "tensor_tensor", t32a[:, :, 0:CJ - d], cur_r[:, :, 0:CJ - d], mi, ALU.mult)
        p.op("vector", "tensor_tensor", t32b[:, :, 0:CJ - d], cur_i[:, :, 0:CJ - d], mr, ALU.mult)
        p.op("vector", "tensor_tensor", t32a[:, :, 0:CJ - d], t32a[:, :, 0:CJ - d], t32b[:, :, 0:CJ - d], ALU.add)
        p.op("vector", "tensor_tensor", oth_i[:, :, d:], cur_i[:, :, d:], t32a[:, :, 0:CJ - d], ALU.add)
        cur_r, cur_i, oth_r, oth_i = oth_r, oth_i, cur_r, cur_i
    p.op("vector", "tensor_copy", bsum_t[:, 0:4], cur_r[:, :, CJ - 1])
    p.op("vector", "tensor_copy", bsum_t[:, 4:8], cur_i[:, :, CJ - 1])
    for g in range(NG):
        p.op("vector", "tensor_scalar", oth_r[:, g, :], cur_r[:, g, :], slot(LR)[:, g:g + 1], None, ALU.mult)
        p.op("vector", "tensor_scalar", oth_i[:, g, :], cur_i[:, g, :], slot(LR)[:, g:g + 1], None, ALU.mult)
        p.op("vector", "tensor_scalar", t32a[:, g, :], cur_i[:, g, :], slot(LI)[:, g:g + 1], None, ALU.mult)
        p.op("vector", "tensor_scalar", t32b[:, g, :], cur_r[:, g, :], slot(LI)[:, g:g + 1], None, ALU.mult)
    p.op("vector", "tensor_tensor", t32a[:], oth_r[:], t32a[:], ALU.subtract)
    p.op("vector", "tensor_tensor", t32b[:], t32b[:], oth_i[:], ALU.add)
    cre4w = cum_re[:].rearrange("p g (j k) -> p g j k", k=L)
    cim4w = cum_im[:].rearrange("p g (j k) -> p g j k", k=L)
    p.op("vector", "tensor_tensor", cre4w[:, :, 1:, :], cre4w[:, :, 1:, :],
         t32a[:, :, 0:CJ - 1].unsqueeze(3).broadcast_to([128, NG, CJ - 1, L]), ALU.add)
    p.op("vector", "tensor_tensor", cim4w[:, :, 1:, :], cim4w[:, :, 1:, :],
         t32b[:, :, 0:CJ - 1].unsqueeze(3).broadcast_to([128, NG, CJ - 1, L]), ALU.add, sem=cm)
    DONE = cm.v

    p.wait("sync", cm, DONE)
    p.dma("sync", o_cre, cre_f, st)
    p.dma("sync", o_cim, cim_f, st)
    p.dma("sync", o_bsum, bsum_t[:], st)
    p.wait("sync", st)

    es.close()



def emit_B(p, nc, dram):
    es = contextlib.ExitStack()

    xT = dram("xT", [D, T], F32, "in")
    i_cre = dram("i_cre", [128, NG * T], F32, "in")
    i_cim = dram("i_cim", [128, NG * T], F32, "in")
    ball = dram("ball", [NCORES * 128, 8], F32, "in")
    selmask = dram("selmask", [128, 8], F32, "in")
    lamp = dram("lamp", [128, 12], F32, "in")
    cvec = dram("cvec", [128, 8], F32, "in")
    CreT = dram("CreT", [N_SC, D], F32, "in")
    CimT = dram("CimT", [N_SC, D], F32, "in")
    DT = dram("DT", [D, D], F32, "in")
    o_x1 = dram("o_x1", [D, T], F32, "out")

    sb = lambda name, shape, dt=F32: es.enter_context(nc.sbuf_tensor(name, shape, dt, allow_name_mangling=True))
    ps = lambda name, shape, dt=F32: es.enter_context(nc.psum_tensor(name, shape, dt, allow_name_mangling=True))

    lam_t = sb("lam_t", [128, 12])
    cvec_t = sb("cvec_t", [128, 8])
    c4 = sb("c4", [128, 4 * 16])
    tp_t = sb("tp_t", [128, NG, L])
    tp_i = sb("tp_i", [128, NG, L])
    lj_r = sb("lj_r", [128, NG, CJ])
    lj_i = sb("lj_i", [128, NG, CJ])
    qpr = sb("qpr", [128, NG, 6])
    qpi = sb("qpi", [128, NG, 6])
    ball_t = sb("ball_t", [128, 64])
    mask_t = sb("mask_t", [128, 8])
    sgr = sb("sgr", [128, NG])
    sgi = sb("sgi", [128, NG])
    ggr = sb("ggr", [128, NG])
    ggi = sb("ggi", [128, NG])
    c1r = sb("c1r", [128, NG, CJ])
    c1i = sb("c1i", [128, NG, CJ])
    x_bf = sb("x_bf", [128, CH, T], BF16)
    Ct_re = sb("Ct_re", [128, NG, D], BF16)
    Ct_imn = sb("Ct_imn", [128, NG, D], BF16)
    Dt_bf = sb("Dt_bf", [128, CH, D], BF16)
    s_re_bf = sb("s_re_bf", [128, NG, T], BF16)
    s_im_bf = sb("s_im_bf", [128, NG, T], BF16)
    evA = sb("evA", [128, T])
    evB = sb("evB", [128, T])
    ones_colb = sb("ones_colb", [128, 1], BF16)
    ones_rowf = sb("ones_rowf", [1, 128])
    row_sc = sb("row_sc", [1, 8 * SUB])

    es1 = contextlib.ExitStack()
    cum_re = es1.enter_context(nc.sbuf_tensor("cum_re_s", [128, NG, T], F32, allow_name_mangling=True))
    cum_im = es1.enter_context(nc.sbuf_tensor("cum_im_s", [128, NG, T], F32, allow_name_mangling=True))

    ps_y = [ps("ps_y0", [128, SUB]), ps("ps_y1", [128, SUB])]
    ps_sum = ps("ps_sum", [1, SUB])
    ps_sq = ps("ps_sq", [1, SUB])
    ps_a = ps("ps_a", [128, SUB])
    ps_b = ps("ps_b", [128, SUB])

    ld = p.sem("ld")
    ldg = p.sem("ldg")
    ldx = p.sem("ldx")
    cm = p.sem("cm")
    mm = p.sem("mm")
    sc = p.sem("sc")
    st = p.sem("st")

    p.dma("sync", lam_t[:], lamp, ld)
    p.dma("sync", cvec_t[:], cvec, ld)
    p.dma("sync", ball_t[:].rearrange("p (j e) -> p j e", e=8), ball.rearrange("(j p) e -> p j e", p=128), ld)
    p.dma("sync", mask_t[:], selmask, ld)
    ld_small = ld.v
    p.dma("sync", cum_re[:].rearrange("p g t -> p (g t)"), i_cre, ld)
    p.dma("sync", cum_im[:].rearrange("p g t -> p (g t)"), i_cim, ld)
    ld_cum = ld.v
    p.dma("gpsimd", x_bf[:], xT.rearrange("(c p) t -> p c t", p=128), ldg)
    p.dma("gpsimd", Ct_re[:], CreT.rearrange("(g p) o -> p g o", p=128), ldg)
    p.dma("gpsimd", Ct_imn[:], CimT.rearrange("(g p) o -> p g o", p=128), ldg)
    p.dma("gpsimd", Dt_bf[:], DT.rearrange("(c p) o -> p c o", p=128), ldg)
    ld_big = ldg.v

    def slot(i):
        return c4[:, 4 * i:4 * i + 4]
    NU, TH, GA, ENU, RHO, THR, SINT, COST, LR, LI, LTr, LTi, TMP1, TMP2, TMP3, TMP4 = range(16)

    p.wait("vector", ldg, ld_big)
    p.op("vector", "tensor_scalar", Ct_imn[:].rearrange("p g o -> p (g o)"),
         Ct_imn[:].rearrange("p g o -> p (g o)"), -1.0, None, ALU.mult)

    p.wait("scalar", ld, ld_small)
    p.op("scalar", "activation", slot(ENU), lamp_slice(lam_t, 0), AF.Exp)
    p.op("scalar", "activation", slot(RHO), slot(ENU), AF.Exp, scale=-1.0)
    p.op("scalar", "activation", slot(TH), lamp_slice(lam_t, 1), AF.Exp, sem=sc)
    p.wait("vector", sc)
    p.op("vector", "tensor_scalar", slot(TMP1), slot(TH), -PI, None, ALU.add)
    p.op("vector", "tensor_scalar", slot(TMP1), slot(TMP1), 1e9, None, ALU.mult)
    p.op("vector", "tensor_scalar", slot(TMP1), slot(TMP1), 0.0, None, ALU.max)
    p.op("vector", "tensor_scalar", slot(TMP1), slot(TMP1), 1.0, None, ALU.min)
    p.op("vector", "scalar_tensor_tensor", slot(THR), slot(TMP1), -TWO_PI, slot(TH), ALU.mult, ALU.add, sem=cm)
    p.wait("scalar", cm)
    p.op("scalar", "activation", slot(SINT), slot(THR), AF.Sin, sem=sc)
    p.wait("vector", sc)
    p.op("vector", "tensor_scalar", slot(TMP1), slot(THR), -1.0, None, ALU.mult)
    p.op("vector", "tensor_tensor", slot(TMP1), slot(TMP1), slot(THR), ALU.max, sem=cm)
    p.wait("scalar", cm)
    p.op("scalar", "activation", slot(COST), slot(TMP1), AF.Sin, scale=-1.0, bias=cvec_t[:, 0:1], sem=sc)
    p.wait("vector", sc)
    p.op("vector", "tensor_tensor", slot(LR), slot(RHO), slot(COST), ALU.mult)
    p.op("vector", "tensor_tensor", slot(LI), slot(RHO), slot(SINT), ALU.mult)

    def cplx_mul(ar, ai, br, bi, outr, outi, tmp1, tmp2):
        p.op("vector", "tensor_tensor", tmp1, ar, br, ALU.mult)
        p.op("vector", "tensor_tensor", tmp2, ai, bi, ALU.mult)
        p.op("vector", "tensor_tensor", outr, tmp1, tmp2, ALU.subtract)
        p.op("vector", "tensor_tensor", tmp1, ar, bi, ALU.mult)
        p.op("vector", "tensor_tensor", tmp2, ai, br, ALU.mult)
        p.op("vector", "tensor_tensor", outi, tmp1, tmp2, ALU.add)

    def build_pow_table(tr, ti, base_r, base_i, n, nlev):
        p.op("vector", "tensor_copy", qpr[:, :, 0], base_r)
        p.op("vector", "tensor_copy", qpi[:, :, 0], base_i)
        for m in range(1, nlev):
            cplx_mul(qpr[:, :, m - 1], qpi[:, :, m - 1], qpr[:, :, m - 1], qpi[:, :, m - 1],
                     qpr[:, :, m], qpi[:, :, m], slot(TMP1), slot(TMP2))
        p.op("vector", "memset", tr[:, :, 0:1], 1.0)
        p.op("vector", "memset", ti[:, :, 0:1], 0.0)
        for m in range(nlev):
            d = 1 << m
            if d >= n:
                break
            w = min(d, n - d)
            for g in range(NG):
                t1 = evA[:, 0:w]
                t2 = evB[:, 0:w]
                p.op("vector", "tensor_scalar", t1, tr[:, g, 0:w], qpr[:, g, m:m + 1], None, ALU.mult)
                p.op("vector", "tensor_scalar", t2, ti[:, g, 0:w], qpi[:, g, m:m + 1], None, ALU.mult)
                p.op("vector", "tensor_tensor", tr[:, g, d:d + w], t1, t2, ALU.subtract)
                p.op("vector", "tensor_scalar", t1, tr[:, g, 0:w], qpi[:, g, m:m + 1], None, ALU.mult)
                p.op("vector", "tensor_scalar", t2, ti[:, g, 0:w], qpr[:, g, m:m + 1], None, ALU.mult)
                p.op("vector", "tensor_tensor", ti[:, g, d:d + w], t1, t2, ALU.add)

    build_pow_table(tp_t, tp_i, slot(LR), slot(LI), L, 6)
    cplx_mul(tp_t[:, :, L - 1], tp_i[:, :, L - 1], slot(LR), slot(LI), slot(LTr), slot(LTi), slot(TMP1), slot(TMP2))
    build_pow_table(lj_r, lj_i, slot(LTr), slot(LTi), CJ, 5)
    LTTr, LTTi = TMP3, TMP4
    cplx_mul(lj_r[:, :, 16], lj_i[:, :, 16], lj_r[:, :, 16], lj_i[:, :, 16], slot(LTTr), slot(LTTi), slot(TMP1), slot(TMP2))

    p.op("vector", "memset", sgr[:], 0.0)
    p.op("vector", "memset", sgi[:], 0.0)
    p.op("vector", "memset", ggr[:], 0.0)
    p.op("vector", "memset", ggi[:], 0.0)
    for j in range(NCORES):
        br = ball_t[:, 8 * j:8 * j + 4]
        bi = ball_t[:, 8 * j + 4:8 * j + 8]
        p.op("vector", "tensor_tensor", slot(TMP1), sgr[:], slot(LTTr), ALU.mult)
        p.op("vector", "tensor_tensor", slot(TMP2), sgi[:], slot(LTTi), ALU.mult)
        p.op("vector", "tensor_tensor", slot(TMP1), slot(TMP1), slot(TMP2), ALU.subtract)
        p.op("vector", "tensor_tensor", slot(TMP2), sgr[:], slot(LTTi), ALU.mult)
        p.op("vector", "tensor_tensor", sgi[:], sgi[:], slot(LTTr), ALU.mult)
        p.op("vector", "tensor_tensor", sgi[:], sgi[:], slot(TMP2), ALU.add)
        p.op("vector", "tensor_tensor", sgi[:], sgi[:], bi, ALU.add)
        p.op("vector", "tensor_tensor", sgr[:], slot(TMP1), br, ALU.add)
        p.op("vector", "tensor_scalar", slot(TMP1), sgr[:], mask_t[:, j:j + 1], None, ALU.mult)
        p.op("vector", "tensor_tensor", ggr[:], ggr[:], slot(TMP1), ALU.add)
        p.op("vector", "tensor_scalar", slot(TMP1), sgi[:], mask_t[:, j:j + 1], None, ALU.mult)
        p.op("vector", "tensor_tensor", ggi[:], ggi[:], slot(TMP1), ALU.add)
    glr, gli = sgr, sgi
    cplx_mul(ggr[:], ggi[:], slot(LR), slot(LI), glr[:], gli[:], slot(TMP1), slot(TMP2))
    for g in range(NG):
        p.op("vector", "tensor_scalar", c1r[:, g, :], lj_r[:, g, :], glr[:, g:g + 1], None, ALU.mult)
        p.op("vector", "tensor_scalar", evA[:, 0:CJ], lj_i[:, g, :], gli[:, g:g + 1], None, ALU.mult)
        p.op("vector", "tensor_tensor", c1r[:, g, :], c1r[:, g, :], evA[:, 0:CJ], ALU.subtract)
        p.op("vector", "tensor_scalar", c1i[:, g, :], lj_r[:, g, :], gli[:, g:g + 1], None, ALU.mult)
        p.op("vector", "tensor_scalar", evA[:, 0:CJ], lj_i[:, g, :], glr[:, g:g + 1], None, ALU.mult)
        p.op("vector", "tensor_tensor", c1i[:, g, :], c1i[:, g, :], evA[:, 0:CJ], ALU.add)

    p.wait("vector", ld, ld_cum)
    cre4 = cum_re[:].rearrange("p g (j k) -> p g j k", k=L)
    cim4 = cum_im[:].rearrange("p g (j k) -> p g j k", k=L)
    p.op("vector", "tensor_tensor", cre4, cre4, c1r[:].unsqueeze(3).broadcast_to([128, NG, CJ, L]), ALU.add)
    p.op("vector", "tensor_tensor", cim4, cim4, c1i[:].unsqueeze(3).broadcast_to([128, NG, CJ, L]), ALU.add)
    for g in range(NG):
        cr = cum_re[:, g, :].rearrange("p (j k) -> p j k", k=L)
        ci = cum_im[:, g, :].rearrange("p (j k) -> p j k", k=L)
        tpr = tp_t[:, g, :].unsqueeze(1).broadcast_to([128, CJ, L])
        tpi = tp_i[:, g, :].unsqueeze(1).broadcast_to([128, CJ, L])
        eA = evA[:].rearrange("p (j k) -> p j k", k=L)
        eB = evB[:].rearrange("p (j k) -> p j k", k=L)
        p.op("vector", "tensor_tensor", eA, cr, tpr, ALU.mult)
        p.op("vector", "tensor_tensor", eB, ci, tpi, ALU.mult)
        p.op("vector", "tensor_tensor", s_re_bf[:, g, :].rearrange("p (j k) -> p j k", k=L), eA, eB, ALU.subtract)
        p.op("vector", "tensor_tensor", eA, cr, tpi, ALU.mult)
        p.op("vector", "tensor_tensor", eB, ci, tpr, ALU.mult)
        p.op("vector", "tensor_tensor", s_im_bf[:, g, :].rearrange("p (j k) -> p j k", k=L), eA, eB,
             ALU.add, sem=(cm if g == NG - 1 else None))
    STATES_DONE = cm.v
    es1.close()
    s_sub = sb("s_sub", [128, CH, SUB])
    x_sub = sb("x_sub", [128, CH, SUB])
    sbfc = sb("sbfc", [128, CH, SUB], BF16)
    sqbf = sb("sqbf", [128, CH, SUB], BF16)
    x1_t = sb("x1_t", [128, CH, SUB])

    p.op("vector", "memset", ones_colb[:], 1.0)
    p.op("vector", "memset", ones_rowf[:], 1.0, sem=cm)
    CONSTS_DONE = cm.v

    p.wait("tensor", ldg, ld_big)
    p.wait("tensor", cm, STATES_DONE)
    evac_done = [0] * NSUB
    stats_read = [0] * NSUB
    apply_done = [0] * NSUB
    sq_done = [0] * NSUB
    store_done = [0] * NSUB
    evacv = []
    for s in range(NSUB):
        p.wait("sync", cm, evac_done[s - 1] if s >= 1 else STATES_DONE)
        p.dma("sync", x_sub[:], xT.rearrange("(c p) t -> p c t", p=128)[:, :, s * SUB:(s + 1) * SUB], ldx)
        ldx_s = ldx.v
        for oc in range(CH):
            k = s * CH + oc
            pyt = ps_y[k % 2]
            if k >= 2:
                p.wait("tensor", cm, evacv[k - 2])
            first = True
            for g in range(NG):
                p.op("tensor", "matmul", pyt[:], Ct_re[:, g, oc * 128:(oc + 1) * 128],
                     s_re_bf[:, g, s * SUB:(s + 1) * SUB], start=first, stop=False)
                first = False
            for g in range(NG):
                p.op("tensor", "matmul", pyt[:], Ct_imn[:, g, oc * 128:(oc + 1) * 128],
                     s_im_bf[:, g, s * SUB:(s + 1) * SUB], start=False, stop=False)
            for ic in range(CH):
                p.op("tensor", "matmul", pyt[:], Dt_bf[:, ic, oc * 128:(oc + 1) * 128],
                     x_bf[:, ic, s * SUB:(s + 1) * SUB], start=False, stop=(ic == CH - 1),
                     sem=(mm if ic == CH - 1 else None))
            mm_oc = mm.v
            p.wait("vector", mm, mm_oc)
            if oc == 0:
                p.wait("vector", ldx, ldx_s)
            p.op("vector", "tensor_tensor", s_sub[:, oc, :], pyt[:], x_sub[:, oc, :], ALU.add, sem=cm)
            evacv.append(cm.v)
            p.op("vector", "tensor_copy", sbfc[:, oc, :], s_sub[:, oc, :])
        evac_done[s] = evacv[-1]
        p.wait("scalar", cm, evac_done[s])
        for oc in range(CH):
            p.op("scalar", "activation", sqbf[:, oc, :], s_sub[:, oc, :], AF.Square,
                 sem=(sc if oc == CH - 1 else None))
        sq_done[s] = sc.v
        p.wait("tensor", sc, sq_done[s])
        p.wait_last("tensor", "vector")
        if s >= 1:
            p.wait("tensor", cm, stats_read[s - 1])
        for oc in range(CH):
            p.op("tensor", "matmul", ps_sum[:], ones_colb[:], sbfc[:, oc, :], start=(oc == 0), stop=(oc == CH - 1))
        for oc in range(CH):
            p.op("tensor", "matmul", ps_sq[:], ones_colb[:], sqbf[:, oc, :], start=(oc == 0), stop=(oc == CH - 1),
                 sem=(mm if oc == CH - 1 else None))
        MEAN = row_sc[0:1, 0:SUB]
        EX2 = row_sc[0:1, SUB:2 * SUB]
        VAR = row_sc[0:1, 2 * SUB:3 * SUB]
        RSTD = row_sc[0:1, 3 * SUB:4 * SUB]
        BB = row_sc[0:1, 4 * SUB:5 * SUB]
        p.wait("vector", mm)
        p.op("vector", "tensor_scalar", MEAN, ps_sum[:], 1.0 / D, None, ALU.mult)
        p.op("vector", "tensor_scalar", EX2, ps_sq[:], 1.0 / D, None, ALU.mult)
        p.op("vector", "tensor_tensor", VAR, MEAN, MEAN, ALU.mult)
        p.op("vector", "tensor_tensor", VAR, EX2, VAR, ALU.subtract, sem=cm)
        stats_read[s] = cm.v
        p.wait("scalar", cm, stats_read[s])
        p.op("scalar", "activation", RSTD, VAR, AF.Ln, bias=cvec_t[0:1, 1:2])
        p.op("scalar", "activation", RSTD, RSTD, AF.Exp, scale=-0.5, sem=sc)
        p.wait("vector", sc)
        p.op("vector", "scalar_tensor_tensor", BB, MEAN, -1.0, RSTD, ALU.mult, ALU.mult, sem=cm)
        bb_done = cm.v
        p.wait("tensor", cm, bb_done)
        if s >= 1:
            p.wait("tensor", cm, apply_done[s - 1])
        p.op("tensor", "matmul", ps_a[:], ones_rowf[:], RSTD, start=True, stop=True)
        p.op("tensor", "matmul", ps_b[:], ones_rowf[:], BB, start=True, stop=True, sem=mm)
        p.wait("vector", mm)
        if s >= 1:
            p.wait("vector", st, store_done[s - 1])
        for oc in range(CH):
            p.op("vector", "tensor_tensor", x1_t[:, oc, :], s_sub[:, oc, :], ps_a[:], ALU.mult)
            p.op("vector", "tensor_tensor", x1_t[:, oc, :], x1_t[:, oc, :], ps_b[:], ALU.add,
                 sem=(cm if oc == CH - 1 else None))
        apply_done[s] = cm.v
        p.wait("sync", cm, apply_done[s])
        p.dma("sync", o_x1.rearrange("(c p) t -> p c t", p=128)[:, :, s * SUB:(s + 1) * SUB], x1_t[:], st)
        store_done[s] = st.v
    p.wait("sync", st)

    es.close()


def host_inputs_B(inputs, core, a_results):
    x = np.asarray(inputs["L0_attn_x"])[0]
    xT = np.ascontiguousarray(np.asarray(x).T)
    shard = np.ascontiguousarray(xT[:, core * T:(core + 1) * T])
    lamp = np.concatenate([
        np.asarray(inputs["L0_attn_nu_log"]).reshape(NG, 128).T,
        np.asarray(inputs["L0_attn_theta_log"]).reshape(NG, 128).T,
        np.asarray(inputs["L0_attn_gamma_log"]).reshape(NG, 128).T,
    ], axis=1).astype(np.float32)
    cv = np.zeros((128, 8), dtype=np.float32)
    cv[:, 0] = PI / 2
    cv[:, 1] = 1e-5
    cv[:, 2] = 1e-6
    ball = np.concatenate([a_results[j]["o_bsum"] for j in range(NCORES)], axis=0).astype(np.float32)
    selmask = np.zeros((128, 8), dtype=np.float32)
    if core > 0:
        selmask[:, core - 1] = 1.0
    return {
        "xT": shard,
        "i_cre": np.ascontiguousarray(a_results[core]["o_cre"]),
        "i_cim": np.ascontiguousarray(a_results[core]["o_cim"]),
        "ball": np.ascontiguousarray(ball),
        "selmask": selmask,
        "lamp": np.ascontiguousarray(lamp),
        "cvec": cv,
        "CreT": np.ascontiguousarray(np.asarray(inputs["L0_attn_C_re"]).T),
        "CimT": np.ascontiguousarray(np.asarray(inputs["L0_attn_C_im"]).T),
        "DT": np.ascontiguousarray(np.asarray(inputs["L0_attn_D"]).T),
    }



class LNKit:
    def __init__(self, p, sb, ps, cvec_t):
        self.p = p
        self.cvec_t = cvec_t
        self.ones_colb = sb("ln_ones_colb", [128, 1], BF16)
        self.ones_rowf = sb("ln_ones_rowf", [1, 128])
        self.rows = [sb(f"ln_row{i}", [1, SUB]) for i in range(5)]
        self.ps_sum = ps("ln_ps_sum", [1, SUB])
        self.ps_sq = ps("ln_ps_sq", [1, SUB])
        self.ps_a = ps("ln_ps_a", [128, SUB])
        self.ps_b = ps("ln_ps_b", [128, SUB])
        self.prev_stats_read = None
        self.prev_apply_done = None
        self.inited = False

    def init_consts(self, cm):
        p = self.p
        p.op("vector", "memset", self.ones_colb[:], 1.0)
        p.op("vector", "memset", self.ones_rowf[:], 1.0, sem=cm)
        self.inited = True

    def emit(self, s_sub, sbfc, sqbf, out_t, cm, mm, sc, evac_done_v, out_guard=None):
        p = self.p
        p.wait("scalar", cm, evac_done_v)
        for oc in range(CH):
            p.op("scalar", "activation", sqbf[:, oc, :], s_sub[:, oc, :], AF.Square,
                 sem=(sc if oc == CH - 1 else None))
        sq_v = sc.v
        p.wait("tensor", sc, sq_v)
        p.wait_last("tensor", "vector")
        if self.prev_stats_read is not None:
            p.wait("tensor", cm, self.prev_stats_read)
        for oc in range(CH):
            p.op("tensor", "matmul", self.ps_sum[:], self.ones_colb[:], sbfc[:, oc, :],
                 start=(oc == 0), stop=(oc == CH - 1))
        for oc in range(CH):
            p.op("tensor", "matmul", self.ps_sq[:], self.ones_colb[:], sqbf[:, oc, :],
                 start=(oc == 0), stop=(oc == CH - 1), sem=(mm if oc == CH - 1 else None))
        MEAN = self.rows[0][:]
        EX2 = self.rows[1][:]
        VAR = self.rows[2][:]
        RSTD = self.rows[3][:]
        BB = self.rows[4][:]
        p.wait("vector", mm)
        p.op("vector", "tensor_scalar", MEAN, self.ps_sum[:], 1.0 / D, None, ALU.mult)
        p.op("vector", "tensor_scalar", EX2, self.ps_sq[:], 1.0 / D, None, ALU.mult)
        p.op("vector", "tensor_tensor", VAR, MEAN, MEAN, ALU.mult)
        p.op("vector", "tensor_tensor", VAR, EX2, VAR, ALU.subtract, sem=cm)
        self.prev_stats_read = cm.v
        p.wait("scalar", cm, self.prev_stats_read)
        p.op("scalar", "activation", RSTD, VAR, AF.Ln, bias=self.cvec_t[0:1, 1:2])
        p.op("scalar", "activation", RSTD, RSTD, AF.Exp, scale=-0.5, sem=sc)
        p.wait("vector", sc)
        p.op("vector", "scalar_tensor_tensor", BB, MEAN, -1.0, RSTD, ALU.mult, ALU.mult, sem=cm)
        bb_v = cm.v
        p.wait("tensor", cm, bb_v)
        if self.prev_apply_done is not None:
            p.wait("tensor", cm, self.prev_apply_done)
        p.op("tensor", "matmul", self.ps_a[:], self.ones_rowf[:], RSTD, start=True, stop=True)
        p.op("tensor", "matmul", self.ps_b[:], self.ones_rowf[:], BB, start=True, stop=True, sem=mm)
        p.wait("vector", mm)
        if out_guard is not None:
            p.wait("vector", out_guard[0], out_guard[1])
        for oc in range(CH):
            p.op("vector", "tensor_tensor", out_t[:, oc, :], s_sub[:, oc, :], self.ps_a[:], ALU.mult)
            p.op("vector", "tensor_tensor", out_t[:, oc, :], out_t[:, oc, :], self.ps_b[:], ALU.add,
                 sem=(cm if oc == CH - 1 else None))
        self.prev_apply_done = cm.v
        return cm.v


def emit_C(p, nc, dram):
    es = contextlib.ExitStack()

    x1T = dram("x1T", [D, T], F32, "in")
    wN = {}
    for n in ("wfT", "wiT", "wcT", "woT", "uoT", "ufT", "uiT", "ucT"):
        wN[n] = nc.dram_tensor(n, [D, D], F32, kind="ExternalInput").ap()
    bias_fm = dram("bias_fm", [128, 4 * CH], F32, "in")
    inc_fm = dram("inc_fm", [128, CH], F32, "in")
    cvec = dram("cvec", [128, 8], F32, "in")
    o_x2 = dram("o_x2", [D, T], F32, "out")

    sb = lambda name, shape, dt=F32: es.enter_context(nc.sbuf_tensor(name, shape, dt, allow_name_mangling=True))
    ps = lambda name, shape, dt=F32: es.enter_context(nc.psum_tensor(name, shape, dt, allow_name_mangling=True))

    cvec_t = sb("cvec_t", [128, 8])
    bias_t = sb("bias_t", [128, 4 * CH])
    inc_t = sb("inc_t", [128, CH])
    inc_bf = sb("inc_bf", [128, CH], BF16)
    biasu = sb("biasu", [128, 3 * CH])
    w_g = {n: sb(n + "_s", [128, CH, D], BF16) for n in ("wfT", "wiT", "wcT", "woT", "uoT")}
    x1bf = sb("x1bf", [128, CH, SUB], BF16)
    x1f = sb("x1f", [128, CH, SUB])
    fg = sb("fg", [128, SUB])
    ig = sb("ig", [128, SUB])
    gg = sb("gg", [128, SUB])
    c_t = sb("c_t", [128, CH, SUB])
    c_bf = sb("c_bf", [128, CH, SUB], BF16)
    s_sub = sb("s_sub", [128, CH, SUB])
    sbfc = sb("sbfc", [128, CH, SUB], BF16)
    sqbf = sb("sqbf", [128, CH, SUB], BF16)

    ln = LNKit(p, sb, ps, cvec_t)

    es1 = contextlib.ExitStack()
    w_u = es1.enter_context(nc.sbuf_tensor("w_u", [128, CH, D], BF16, allow_name_mangling=True))
    ps_g = [ps("ps_g0", [128, SUB]), ps("ps_g1", [128, SUB])]
    ps_mv = ps("ps_mv", [128, 1])

    ld = p.sem("ld")
    ldg = p.sem("ldg")
    ldx = p.sem("ldx")
    cm = p.sem("cm")
    mm = p.sem("mm")
    sc = p.sem("sc")
    st = p.sem("st")

    p.dma("sync", cvec_t[:], cvec, ld)
    p.dma("sync", bias_t[:], bias_fm, ld)
    p.dma("sync", inc_t[:], inc_fm, ld)
    ld_small = ld.v

    ln.init_consts(cm)
    p.wait("vector", ld, ld_small)
    p.op("vector", "tensor_copy", inc_bf[:], inc_t[:], sem=cm)
    INCBF = cm.v

    p.wait("tensor", cm, INCBF)
    for ui, n in enumerate(("ufT", "uiT", "ucT")):
        if ui > 0:
            p.wait("gpsimd", mm, mm.v)
        p.dma("gpsimd", w_u[:], wN[n].rearrange("(c p) o -> p c o", p=128), ldg)
        p.wait("tensor", ldg, ldg.v)
        for oc in range(CH):
            for ic in range(CH):
                p.op("tensor", "matmul", ps_mv[:], w_u[:, ic, oc * 128:(oc + 1) * 128],
                     inc_bf[:, ic:ic + 1], start=(ic == 0), stop=(ic == CH - 1),
                     sem=(mm if ic == CH - 1 else None))
            p.wait("vector", mm)
            p.op("vector", "tensor_copy", biasu[:, ui * CH + oc:ui * CH + oc + 1], ps_mv[:], sem=cm)
            p.wait("tensor", cm)
    for n in ("wfT", "wiT", "wcT", "woT", "uoT"):
        p.dma("gpsimd", w_g[n][:], wN[n].rearrange("(c p) o -> p c o", p=128), ldg)
    ld_w = ldg.v
    es1.close()
    x2_t = sb("x2_t", [128, CH, SUB])
    p.op("vector", "tensor_tensor", biasu[:], biasu[:], bias_t[:, 0:3 * CH], ALU.add, sem=cm)
    BIASES = cm.v
    p.wait("scalar", cm, BIASES)

    gate_specs = [("wfT", AF.Sigmoid, fg, 0), ("wiT", AF.Sigmoid, ig, 1), ("wcT", AF.Tanh, gg, 2)]
    p.wait("tensor", ldg, ld_w)
    evacv = []
    kcnt = 0
    prev_g = {}
    last_c_done = None
    for s in range(NSUB):
        if s >= 1:
            p.wait("gpsimd", cm, evacv[-1])
            p.wait("sync", cm, evacv[-1])
        p.dma("gpsimd", x1bf[:], x1T.rearrange("(c p) t -> p c t", p=128)[:, :, s * SUB:(s + 1) * SUB], ldg)
        p.dma("sync", x1f[:], x1T.rearrange("(c p) t -> p c t", p=128)[:, :, s * SUB:(s + 1) * SUB], ldx)
        ldg_s, ldx_s = ldg.v, ldx.v
        p.wait("tensor", ldg, ldg_s)
        if s == 0:
            p.wait("tensor", cm, BIASES)
        for oc in range(CH):
            if last_c_done is not None:
                p.wait("scalar", cm, last_c_done)
            for (wn, act, gt, bi) in gate_specs:
                pg = ps_g[kcnt % 2]
                if kcnt >= 2:
                    p.wait("tensor", sc, prev_g[kcnt - 2])
                for ic in range(CH):
                    p.op("tensor", "matmul", pg[:], w_g[wn][:, ic, oc * 128:(oc + 1) * 128],
                         x1bf[:, ic, :], start=(ic == 0), stop=(ic == CH - 1),
                         sem=(mm if ic == CH - 1 else None))
                p.wait("scalar", mm)
                p.op("scalar", "activation", gt[:], pg[:], act,
                     bias=biasu[:, bi * CH + oc:bi * CH + oc + 1], sem=sc)
                prev_g[kcnt] = sc.v
                kcnt += 1
            p.wait("vector", sc)
            p.op("vector", "tensor_tensor", c_t[:, oc, :], ig[:], gg[:], ALU.mult)
            p.op("vector", "scalar_tensor_tensor", c_t[:, oc, :], fg[:], inc_t[:, oc:oc + 1],
                 c_t[:, oc, :], ALU.mult, ALU.add, sem=cm)
            last_c_done = cm.v
        p.op("vector", "tensor_copy", c_bf[:].rearrange("p c t -> p (c t)"),
             c_t[:].rearrange("p c t -> p (c t)"), sem=cm)
        CBF = cm.v
        p.wait("tensor", cm, CBF)
        p.wait("vector", ldx, ldx_s)
        for oc in range(CH):
            pg = ps_g[kcnt % 2]
            if kcnt >= 2:
                p.wait("tensor", sc, prev_g[kcnt - 2])
            for ic in range(CH):
                p.op("tensor", "matmul", pg[:], w_g["woT"][:, ic, oc * 128:(oc + 1) * 128],
                     x1bf[:, ic, :], start=(ic == 0), stop=False)
            for ic in range(CH):
                p.op("tensor", "matmul", pg[:], w_g["uoT"][:, ic, oc * 128:(oc + 1) * 128],
                     c_bf[:, ic, :], start=False, stop=(ic == CH - 1),
                     sem=(mm if ic == CH - 1 else None))
            p.wait("scalar", mm)
            p.op("scalar", "activation", s_sub[:, oc, :], pg[:], AF.Tanh,
                 bias=bias_t[:, 3 * CH + oc:3 * CH + oc + 1], sem=sc)
            prev_g[kcnt] = sc.v
            kcnt += 1
            p.wait("vector", sc)
            p.op("vector", "tensor_tensor", s_sub[:, oc, :], s_sub[:, oc, :], x1f[:, oc, :], ALU.add, sem=cm)
            p.op("vector", "tensor_copy", sbfc[:, oc, :], s_sub[:, oc, :])
        evacv.append(cm.v)
        out_guard = (st, stp) if s >= 1 else None
        ln.emit(s_sub, sbfc, sqbf, x2_t, cm, mm, sc, evacv[-1], out_guard=out_guard)
        p.wait("sync", cm, ln.prev_apply_done)
        p.dma("sync", o_x2.rearrange("(c p) t -> p c t", p=128)[:, :, s * SUB:(s + 1) * SUB], x2_t[:], st)
        stp = st.v
    p.wait("sync", st)
    es.close()


def host_inputs_C(inputs, core, b_results):
    cv = np.zeros((128, 8), dtype=np.float32)
    cv[:, 0] = PI / 2
    cv[:, 1] = 1e-5
    cv[:, 2] = 1e-6

    def fm(v):
        return np.ascontiguousarray(np.asarray(v).reshape(CH, 128).T.astype(np.float32))

    bias_fm = np.concatenate([fm(inputs["L0_ffn_bf"]), fm(inputs["L0_ffn_bi"]),
                              fm(inputs["L0_ffn_bc"]), fm(inputs["L0_ffn_bo"])], axis=1)
    d = {
        "x1T": np.ascontiguousarray(b_results[core]["o_x1"]),
        "bias_fm": np.ascontiguousarray(bias_fm),
        "inc_fm": fm(inputs["L0_ffn_in_c"]),
        "cvec": cv,
    }
    for n, key in (("wfT", "L0_ffn_wf"), ("wiT", "L0_ffn_wi"), ("wcT", "L0_ffn_wc"),
                   ("woT", "L0_ffn_wo"), ("uoT", "L0_ffn_uo"), ("ufT", "L0_ffn_uf"),
                   ("uiT", "L0_ffn_ui"), ("ucT", "L0_ffn_uc")):
        d[n] = np.ascontiguousarray(np.asarray(inputs[key]).T)
    return d



def emit_D(p, nc, dram):
    es = contextlib.ExitStack()

    x2T = dram("x2T", [D, T], F32, "in")
    wqkvT = dram("wqkvT", [D, 3 * D], F32, "in")
    blkb = dram("blkb", [128, CH * 16], BF16, "in")
    bc16f = dram("bc16f", [16, CH * 128], F32, "in")
    o_qfin = dram("o_qfin", [D, T], BF16, "out")
    o_kv = dram("o_kv", [64, 1024], F32, "out")
    o_ks = dram("o_ks", [128, 8], F32, "out")
    o_edge = dram("o_edge", [128, 16], BF16, "out")

    sb = lambda name, shape, dt=F32: es.enter_context(nc.sbuf_tensor(name, shape, dt, allow_name_mangling=True))
    ps = lambda name, shape, dt=F32: es.enter_context(nc.psum_tensor(name, shape, dt, allow_name_mangling=True))

    x2bf = sb("x2bf", [128, CH, T], BF16)
    wbuf = sb("wbuf", [128, CH, D], BF16)
    blk_t = sb("blk_t", [128, CH * 16], BF16)
    bc_t = sb("bc_t", [16, CH * 128])
    q3 = sb("q3", [128, CH, SUB])
    q3bf = sb("q3bf", [128, CH, SUB], BF16)
    qfin = sb("qfin", [128, CH, SUB], BF16)
    rcp_r = sb("rcp_r", [16, SUB])
    kt = sb("kt", [128, D])
    kt2 = sb("kt2", [128, D])
    ks16 = sb("ks16", [128, 16])
    k_tm = sb("k_tm", [128, 16, D], BF16)
    v_tm = sb("v_tm", [128, 16, D], BF16)
    kv_s = sb("kv_s", [64, 1024])
    ks_s = sb("ks_s", [128, 8])
    edge_t = sb("edge_t", [128, CH, 2], BF16)
    ones_b = sb("ones_b", [128, 1], BF16)

    ps_p = [ps("ps_p0", [128, SUB]), ps("ps_p1", [128, SUB])]
    ps_s3 = ps("ps_s3", [16, SUB])
    ps_bc = ps("ps_bc", [128, SUB])
    ps_kv = ps("ps_kv", [64, 1024])
    ps_ks = ps("ps_ks", [128, 8])

    ld = p.sem("ld")
    ldg = p.sem("ldg")
    cm = p.sem("cm")
    mm = p.sem("mm")
    sc = p.sem("sc")
    st = p.sem("st")

    p.dma("sync", blk_t[:], blkb, ld)
    p.dma("sync", bc_t[:], bc16f, ld)
    ld_small = ld.v
    p.dma("gpsimd", x2bf[:], x2T.rearrange("(c p) t -> p c t", p=128), ldg)
    p.dma("gpsimd", wbuf[:], wqkvT[:, 0:D].rearrange("(c p) o -> p c o", p=128), ldg)
    ld_q = ldg.v
    p.op("vector", "memset", ones_b[:], 1.0, sem=cm)
    ONES = cm.v

    p.wait("tensor", ldg, ld_q)
    p.wait("tensor", ld, ld_small)
    EPS = 1e-6
    evacv = []
    kcnt = 0
    prev_use = {}
    stq = []
    for s in range(NSUB):
        if s >= 1:
            p.wait("sync", cm, evacv[-1])
        for oc in range(CH):
            pg = ps_p[kcnt % 2]
            if kcnt >= 2:
                p.wait("tensor", sc, prev_use[kcnt - 2])
            for ic in range(CH):
                p.op("tensor", "matmul", pg[:], wbuf[:, ic, oc * 128:(oc + 1) * 128],
                     x2bf[:, ic, s * SUB:(s + 1) * SUB], start=(ic == 0), stop=(ic == CH - 1),
                     sem=(mm if ic == CH - 1 else None))
            p.wait("scalar", mm)
            if s >= 1 and oc == 0:
                p.wait("scalar", cm, evacv[-1])
            p.op("scalar", "activation", q3[:, oc, :], pg[:], AF.Relu, sem=sc)
            prev_use[kcnt] = sc.v
            kcnt += 1
            p.wait("vector", sc)
            p.op("vector", "tensor_scalar", q3[:, oc, :], q3[:, oc, :], EPS, None, ALU.add)
            p.op("vector", "tensor_tensor", kt[:, 0:SUB], q3[:, oc, :], q3[:, oc, :], ALU.mult)
            p.op("vector", "tensor_tensor", q3[:, oc, :], kt[:, 0:SUB], q3[:, oc, :], ALU.mult)
            p.op("vector", "tensor_copy", q3bf[:, oc, :], q3[:, oc, :], sem=(cm if oc == CH - 1 else None))
        Q3_DONE = cm.v
        p.wait("tensor", cm, Q3_DONE)
        for oc in range(CH):
            p.op("tensor", "matmul", ps_s3[:], blk_t[:, oc * 16:(oc + 1) * 16], q3bf[:, oc, :],
                 start=(oc == 0), stop=(oc == CH - 1), sem=(mm if oc == CH - 1 else None))
        p.wait("vector", mm)
        p.op("vector", "reciprocal", rcp_r[:], ps_s3[:], sem=cm)
        RCP_DONE = cm.v
        p.wait("tensor", cm, RCP_DONE)
        for oc in range(CH):
            p.op("tensor", "matmul", ps_bc[:], bc_t[:, oc * 128:(oc + 1) * 128], rcp_r[:],
                 start=True, stop=True, sem=mm)
            p.wait("vector", mm)
            if s >= 1 and oc == 0:
                p.wait("vector", st, stq[-1])
            p.op("vector", "tensor_tensor", qfin[:, oc, :], q3[:, oc, :], ps_bc[:], ALU.mult, sem=cm)
            if s == 0:
                p.op("vector", "tensor_copy", edge_t[:, oc, 0:1], qfin[:, oc, 0:1])
            if s == NSUB - 1:
                p.op("vector", "tensor_copy", edge_t[:, oc, 1:2], qfin[:, oc, SUB - 1:SUB])
            if oc < CH - 1:
                p.wait("tensor", cm, cm.v)
        p.wait("sync", cm)
        p.dma("sync", o_qfin.rearrange("(c p) t -> p c t", p=128)[:, :, s * SUB:(s + 1) * SUB], qfin[:], st)
        stq.append(st.v)
        evacv.append(cm.v)

    p.wait("gpsimd", mm, mm.v)
    p.dma("gpsimd", wbuf[:], wqkvT[:, D:2 * D].rearrange("(c p) o -> p c o", p=128), ldg)
    ld_k = ldg.v
    p.wait("tensor", ldg, ld_k)
    k_done = {}
    for tile in range(16):
        for half in range(2):
            pg = ps_p[kcnt % 2]
            if kcnt >= 2:
                p.wait("tensor", sc, prev_use[kcnt - 2])
            for ic in range(CH):
                p.op("tensor", "matmul", pg[:], x2bf[:, ic, tile * 128:(tile + 1) * 128],
                     wbuf[:, ic, half * SUB:(half + 1) * SUB], start=(ic == 0), stop=(ic == CH - 1),
                     sem=(mm if ic == CH - 1 else None))
            p.wait("scalar", mm)
            if half == 0:
                p.wait("scalar", cm, k_done[tile - 1] if tile > 0 else evacv[-1])
            p.op("scalar", "activation", kt[:, half * SUB:(half + 1) * SUB], pg[:], AF.Relu, sem=sc)
            prev_use[kcnt] = sc.v
            kcnt += 1
        p.wait("vector", sc)
        p.op("vector", "tensor_scalar", kt[:], kt[:], EPS, None, ALU.add)
        p.op("vector", "tensor_tensor", kt2[:], kt[:], kt[:], ALU.mult)
        p.op("vector", "tensor_tensor", kt[:], kt2[:], kt[:], ALU.mult)
        k3 = kt[:].rearrange("p (h e) -> p h e", e=64)
        p.op("vector", "tensor_reduce", ks16[:], k3, AX.X, ALU.add)
        p.op("vector", "reciprocal", ks16[:], ks16[:])
        p.op("vector", "tensor_tensor", k_tm[:, tile, :].rearrange("p (h e) -> p h e", e=64), k3,
             ks16[:].unsqueeze(2).broadcast_to([128, 16, 64]), ALU.mult, sem=cm)
        k_done[tile] = cm.v

    p.wait("gpsimd", mm, mm.v)
    p.dma("gpsimd", wbuf[:], wqkvT[:, 2 * D:3 * D].rearrange("(c p) o -> p c o", p=128), ldg)
    ld_v = ldg.v
    p.wait("tensor", ldg, ld_v)
    for tile in range(16):
        for half in range(2):
            pg = ps_p[kcnt % 2]
            if kcnt >= 2:
                p.wait("tensor", sc, prev_use[kcnt - 2])
            for ic in range(CH):
                p.op("tensor", "matmul", pg[:], x2bf[:, ic, tile * 128:(tile + 1) * 128],
                     wbuf[:, ic, half * SUB:(half + 1) * SUB], start=(ic == 0), stop=(ic == CH - 1),
                     sem=(mm if ic == CH - 1 else None))
            p.wait("scalar", mm)
            p.op("scalar", "copy", v_tm[:, tile, half * SUB:(half + 1) * SUB], pg[:], sem=sc)
            prev_use[kcnt] = sc.v
            kcnt += 1

    p.wait("tensor", sc)
    p.wait_last("tensor", "vector")
    p.wait("tensor", cm, ONES)
    for h in range(16):
        for tile in range(16):
            p.op("tensor", "matmul", ps_kv[0:64, h * 64:(h + 1) * 64],
                 k_tm[:, tile, h * 64:(h + 1) * 64], v_tm[:, tile, h * 64:(h + 1) * 64],
                 start=(tile == 0), stop=(tile == 15))
    for pr in range(CH):
        for tile in range(16):
            p.op("tensor", "matmul", ps_ks[:, pr:pr + 1],
                 k_tm[:, tile, pr * 128:(pr + 1) * 128], ones_b[:],
                 start=(tile == 0), stop=(tile == 15),
                 sem=(mm if (pr == CH - 1 and tile == 15) else None))
    p.wait("vector", mm)
    p.op("vector", "tensor_copy", kv_s[:], ps_kv[:])
    p.op("vector", "tensor_copy", ks_s[:], ps_ks[:], sem=cm)
    p.wait("sync", cm)
    p.dma("sync", o_kv, kv_s[:], st)
    p.dma("sync", o_ks, ks_s[:], st)
    p.dma("sync", o_edge, edge_t[:].rearrange("p c r -> p (c r)"), st)
    p.wait("sync", st)
    es.close()


def host_inputs_D(inputs, core, c_results):
    import ml_dtypes
    blk = np.zeros((128, CH * 16), dtype=np.float32)
    for oc in range(CH):
        for pp in range(128):
            blk[pp, oc * 16 + 2 * oc + pp // 64] = 1.0
    bc = np.zeros((16, CH * 128), dtype=np.float32)
    for oc in range(CH):
        for m in range(128):
            bc[2 * oc + m // 64, oc * 128 + m] = 1.0
    return {
        "x2T": np.ascontiguousarray(c_results[core]["o_x2"]),
        "wqkvT": np.ascontiguousarray(np.asarray(inputs["L1_attn_w_qkv"]).T),
        "blkb": blk.astype(ml_dtypes.bfloat16),
        "bc16f": np.ascontiguousarray(bc),
    }



def emit_E1(p, nc, dram, reduced_kv=False):
    es = contextlib.ExitStack()

    TE = T + 2
    x2T = dram("x2T", [D, T], F32, "in")
    qfT = dram("qfT", [D, T], BF16, "in")
    if reduced_kv:
        kv_red = dram("kv_red", [64, 1024], F32, "in")
        ks_red = dram("ks_red", [128, 8], F32, "in")
    else:
        kv_all = dram("kv_all", [NCORES, 64, 1024], F32, "in")
        ks_all = dram("ks_all", [NCORES, 128, 8], F32, "in")
    edges_all = dram("edges_all", [NCORES * 128, 16], BF16, "in")
    nbr_idx_d = dram("nbr_idx", [128, 2], U32, "in")
    wprojT = dram("wprojT", [D, D], F32, "in")
    bproj_fm = dram("bproj_fm", [128, CH], F32, "in")
    wdwc_fm = dram("wdwc_fm", [128, CH * 3], F32, "in")
    bdwc_fm = dram("bdwc_fm", [128, CH], F32, "in")
    blkb = dram("blkb", [128, CH * 16], BF16, "in")
    bc16f = dram("bc16f", [16, CH * 128], F32, "in")
    cvec = dram("cvec", [128, 8], F32, "in")
    o_x3 = dram("o_x3", [D, T], F32, "out")

    sb = lambda name, shape, dt=F32: es.enter_context(nc.sbuf_tensor(name, shape, dt, allow_name_mangling=True))
    ps = lambda name, shape, dt=F32: es.enter_context(nc.psum_tensor(name, shape, dt, allow_name_mangling=True))

    cvec_t = sb("cvec_t", [128, 8])
    q_bf = sb("q_bf", [128, CH, T], BF16)
    kvg = sb("kvg", [64, 1024])
    ksg = sb("ksg", [128, 8])
    nbr_t = sb("nbr_t", [128, CH, 2], BF16)
    nbrL_t = sb("nbrL_t", [128, 16], BF16)
    nbrR_t = sb("nbrR_t", [128, 16], BF16)
    nbr_idx = sb("nbr_idx_s", [128, 2], U32)
    bd = sb("bd", [128, CH * 128], BF16)
    oa_ext = sb("oa_ext", [128, CH, TE], BF16)
    tmpz = sb("tmpz", [128, CH, SUB], BF16)
    zrow = sb("zrow", [16, SUB])
    oat = sb("oat", [128, SUB])
    out2b = sb("out2b", [128, CH, SUB], BF16)
    wproj_s = sb("wproj_s", [128, CH, D], BF16)
    bproj_t = sb("bproj_t", [128, CH])
    wdwc_t = sb("wdwc_t", [128, CH * 3])
    wdw1p = sb("wdw1p", [128, CH])
    bdwc_t = sb("bdwc_t", [128, CH])
    blk_t = sb("blk_t", [128, CH * 16], BF16)
    bc_t = sb("bc_t", [16, CH * 128])
    x2f = sb("x2f", [128, CH, SUB])
    s_sub = sb("s_sub", [128, CH, SUB])
    sbfc = sb("sbfc", [128, CH, SUB], BF16)
    sqbf = sb("sqbf", [128, CH, SUB], BF16)
    x3_t = sb("x3_t", [128, CH, SUB])
    ln = LNKit(p, sb, ps, cvec_t)

    ps_oa = ps("ps_oa", [128, SUB])
    ps_zb = ps("ps_zb", [128, SUB])
    ps_z = ps("ps_z", [16, SUB])
    ps_pr = ps("ps_pr", [128, SUB])

    ld = p.sem("ld")
    ldg = p.sem("ldg")
    ldx = p.sem("ldx")
    cm = p.sem("cm")
    mm = p.sem("mm")
    sc = p.sem("sc")
    st = p.sem("st")

    p.dma("sync", cvec_t[:], cvec, ld)
    p.dma("sync", nbr_idx[:], nbr_idx_d, ld)
    p.dma("sync", bproj_t[:], bproj_fm, ld)
    p.dma("sync", wdwc_t[:], wdwc_fm, ld)
    p.dma("sync", bdwc_t[:], bdwc_fm, ld)
    p.dma("sync", blk_t[:], blkb, ld)
    p.dma("sync", bc_t[:], bc16f, ld)
    ld_small = ld.v
    ldkv = p.sem("ldkv")
    if reduced_kv:
        p.dma("gpsimd", kvg[:], kv_red, ldkv)
        p.dma("gpsimd", ksg[:], ks_red, ldkv)
    else:
        p.dma("gpsimd", kvg[:], kv_all[0], ldkv)
        p.dma("gpsimd", ksg[:], ks_all[0], ldkv)
        for j in range(1, NCORES):
            p.wait("gpsimd", ldkv, ldkv.v)
            p.dma("gpsimd", kvg[:], kv_all[j], ldkv, accum_op=ALU.add)
            p.wait("gpsimd", ldkv, ldkv.v)
            p.dma("gpsimd", ksg[:], ks_all[j], ldkv, accum_op=ALU.add)
    ld_kv = ldkv.v
    p.dma("gpsimd", q_bf[:], qfT.rearrange("(c p) t -> p c t", p=128), ldg)
    p.dma("gpsimd", wproj_s[:], wprojT.rearrange("(c p) o -> p c o", p=128), ldg)
    ld_big = ldg.v

    p.wait("vector", ld, ld_small)
    p.op("vector", "memset", nbrL_t[:], 0.0)
    p.op("vector", "memset", nbrR_t[:], 0.0, sem=cm)
    p.wait("gpsimd", cm, cm.v)
    gnb = p.sem("gnb")
    for (dst, col) in ((nbrL_t, 0), (nbrR_t, 1)):
        gnb.bump(16)
        p.emit("gpsimd", lambda e, o=dst[:], ix=nbr_idx[:, col:col + 1], s2=gnb.h, src_=edges_all:
               e.indirect_dma_start(out=o, out_offset=None, in_=src_,
                                    in_offset=bass.IndirectOffsetOnAxis(ap=ix, axis=0),
                                    bounds_check=NCORES * 128 - 1, oob_is_err=False).then_inc(s2, 16))
    p.wait("vector", gnb, gnb.v)
    nl3 = nbrL_t[:].rearrange("p (c r) -> p c r", r=2)
    nr3 = nbrR_t[:].rearrange("p (c r) -> p c r", r=2)
    p.op("vector", "tensor_copy", nbr_t[:, :, 0], nl3[:, :, 1])
    p.op("vector", "tensor_copy", nbr_t[:, :, 1], nr3[:, :, 0], sem=cm)

    p.wait("vector", ldkv, ld_kv)
    p.op("vector", "memset", bd[:], 0.0)
    for oc in range(CH):
        p.op("vector", "tensor_copy", bd[0:64, oc * 128:oc * 128 + 64], kvg[0:64, (2 * oc) * 64:(2 * oc + 1) * 64])
        p.op("vector", "tensor_copy", bd[64:128, oc * 128 + 64:oc * 128 + 128],
             kvg[0:64, (2 * oc + 1) * 64:(2 * oc + 2) * 64])
    wd3 = wdwc_t[:].rearrange("p (c k) -> p c k", k=3)
    p.op("vector", "tensor_scalar", wdw1p[:], wd3[:, :, 1], 1.0, None, ALU.add, sem=cm)
    BD_DONE = cm.v

    p.wait("tensor", cm, BD_DONE)
    p.wait("tensor", ldg, ld_big)
    zz = []
    p.wait("vector", ldg, ld_big)
    for s in range(NSUB):
        for oc in range(CH):
            p.op("vector", "tensor_scalar", tmpz[:, oc, :], q_bf[:, oc, s * SUB:(s + 1) * SUB],
                 ksg[:, oc:oc + 1], None, ALU.mult, sem=(cm if oc == CH - 1 else None))
        TMPZ = cm.v
        p.wait("tensor", cm, TMPZ)
        if s >= 1:
            p.wait("tensor", cm, zz[-1])
        for oc in range(CH):
            p.op("tensor", "matmul", ps_z[:], blk_t[:, oc * 16:(oc + 1) * 16], tmpz[:, oc, :],
                 start=(oc == 0), stop=(oc == CH - 1), sem=(mm if oc == CH - 1 else None))
        p.wait("vector", mm)
        p.op("vector", "tensor_scalar", zrow[:], ps_z[:], 1e-6, None, ALU.add)
        p.op("vector", "reciprocal", zrow[:], zrow[:], sem=cm)
        zz.append(cm.v)
        p.wait("tensor", cm, zz[-1])
        for oc in range(CH):
            p.op("tensor", "matmul", ps_oa[:], bd[:, oc * 128:(oc + 1) * 128],
                 q_bf[:, oc, s * SUB:(s + 1) * SUB], start=True, stop=True, sem=mm)
            p.op("tensor", "matmul", ps_zb[:], bc_t[:, oc * 128:(oc + 1) * 128], zrow[:],
                 start=True, stop=True, sem=mm)
            p.wait("scalar", mm)
            p.op("scalar", "copy", oat[:], ps_oa[:], sem=sc)
            p.wait("vector", sc)
            p.op("vector", "tensor_tensor", oa_ext[:, oc, 1 + s * SUB:1 + (s + 1) * SUB], oat[:],
                 ps_zb[:], ALU.mult, sem=cm)
            p.wait("tensor", cm, cm.v)
    for oc in range(CH):
        p.op("vector", "tensor_scalar", tmpz[:, oc, 0:2], nbr_t[:, oc, :], ksg[:, oc:oc + 1], None,
             ALU.mult, sem=(cm if oc == CH - 1 else None))
    TMPB = cm.v
    p.wait("tensor", cm, TMPB)
    for oc in range(CH):
        p.op("tensor", "matmul", ps_z[:, 0:2], blk_t[:, oc * 16:(oc + 1) * 16], tmpz[:, oc, 0:2],
             start=(oc == 0), stop=(oc == CH - 1), sem=(mm if oc == CH - 1 else None))
    p.wait("vector", mm)
    p.op("vector", "tensor_scalar", zrow[:, 0:2], ps_z[:, 0:2], 1e-6, None, ALU.add)
    p.op("vector", "reciprocal", zrow[:, 0:2], zrow[:, 0:2], sem=cm)
    ZB = cm.v
    p.wait("tensor", cm, ZB)
    for oc in range(CH):
        p.op("tensor", "matmul", ps_oa[:, 0:2], bd[:, oc * 128:(oc + 1) * 128], nbr_t[:, oc, :],
             start=True, stop=True, sem=mm)
        p.op("tensor", "matmul", ps_zb[:, 0:2], bc_t[:, oc * 128:(oc + 1) * 128], zrow[:, 0:2],
             start=True, stop=True, sem=mm)
        p.wait("scalar", mm)
        p.op("scalar", "copy", oat[:, 0:2], ps_oa[:, 0:2], sem=sc)
        p.wait("vector", sc)
        p.op("vector", "tensor_tensor", oat[:, 0:2], oat[:, 0:2], ps_zb[:, 0:2], ALU.mult)
        p.op("vector", "tensor_copy", oa_ext[:, oc, 0:1], oat[:, 0:1])
        p.op("vector", "tensor_copy", oa_ext[:, oc, TE - 1:TE], oat[:, 1:2], sem=cm)
        p.wait("tensor", cm, cm.v)
    OA_ALL = cm.v

    ln.init_consts(cm)
    prev_pr = {}
    kcnt = 0
    stv = []
    for s in range(NSUB):
        p.wait("sync", cm, OA_ALL if s == 0 else ln.prev_apply_done)
        p.dma("sync", x2f[:], x2T.rearrange("(c p) t -> p c t", p=128)[:, :, s * SUB:(s + 1) * SUB], ldx)
        ldx_s = ldx.v
        if s == 0:
            p.wait("vector", cm, OA_ALL)
        for oc in range(CH):
            mid = oa_ext[:, oc, 1 + s * SUB:1 + (s + 1) * SUB]
            left = oa_ext[:, oc, s * SUB:(s + 1) * SUB]
            right = oa_ext[:, oc, 2 + s * SUB:2 + (s + 1) * SUB]
            p.op("vector", "tensor_scalar", oat[:], mid, wdw1p[:, oc:oc + 1], None, ALU.mult)
            p.op("vector", "scalar_tensor_tensor", oat[:], left, wd3[:, oc, 0:1], oat[:], ALU.mult, ALU.add)
            p.op("vector", "scalar_tensor_tensor", oat[:], right, wd3[:, oc, 2:3], oat[:], ALU.mult, ALU.add)
            p.op("vector", "tensor_scalar", out2b[:, oc, :], oat[:], bdwc_t[:, oc:oc + 1], None, ALU.add,
                 sem=(cm if oc == CH - 1 else None))
        OUT2 = cm.v
        p.wait("tensor", cm, OUT2)
        for oc in range(CH):
            if kcnt >= 1:
                p.wait("tensor", cm, prev_pr[kcnt - 1])
            for ic in range(CH):
                p.op("tensor", "matmul", ps_pr[:], wproj_s[:, ic, oc * 128:(oc + 1) * 128],
                     out2b[:, ic, :], start=(ic == 0), stop=(ic == CH - 1),
                     sem=(mm if ic == CH - 1 else None))
            p.wait("scalar", mm)
            p.op("scalar", "copy", s_sub[:, oc, :], ps_pr[:], sem=sc)
            p.wait("vector", sc)
            if oc == 0:
                p.wait("vector", ldx, ldx_s)
            p.op("vector", "tensor_scalar", s_sub[:, oc, :], s_sub[:, oc, :], bproj_t[:, oc:oc + 1], None, ALU.add)
            p.op("vector", "tensor_tensor", s_sub[:, oc, :], s_sub[:, oc, :], x2f[:, oc, :], ALU.add, sem=cm)
            prev_pr[kcnt] = cm.v
            kcnt += 1
            p.op("vector", "tensor_copy", sbfc[:, oc, :], s_sub[:, oc, :])
        evd = cm.v
        out_guard = (st, stv[-1]) if s >= 1 else None
        ln.emit(s_sub, sbfc, sqbf, x3_t, cm, mm, sc, evd, out_guard=out_guard)
        p.wait("sync", cm, ln.prev_apply_done)
        p.dma("sync", o_x3.rearrange("(c p) t -> p c t", p=128)[:, :, s * SUB:(s + 1) * SUB], x3_t[:], st)
        stv.append(st.v)
    p.wait("sync", st)
    es.close()


def host_inputs_E1(inputs, core, c_results, d_results):
    import ml_dtypes
    cv = np.zeros((128, 8), dtype=np.float32)
    cv[:, 0] = PI / 2
    cv[:, 1] = 1e-5
    cv[:, 2] = 1e-6
    blk = np.zeros((128, CH * 16), dtype=np.float32)
    for oc in range(CH):
        for pp in range(128):
            blk[pp, oc * 16 + 2 * oc + pp // 64] = 1.0
    bc = np.zeros((16, CH * 128), dtype=np.float32)
    for oc in range(CH):
        for m in range(128):
            bc[2 * oc + m // 64, oc * 128 + m] = 1.0

    def fm(v):
        return np.ascontiguousarray(np.asarray(v).reshape(CH, 128).T.astype(np.float32))

    OOB = np.uint32(1 << 30)
    nbr_idx = np.full((128, 2), OOB, dtype=np.uint32)
    if core > 0:
        nbr_idx[:, 0] = (core - 1) * 128 + np.arange(128)
    if core < NCORES - 1:
        nbr_idx[:, 1] = (core + 1) * 128 + np.arange(128)
    edges_all = np.concatenate([np.asarray(d_results[j]["o_edge"]) for j in range(NCORES)], axis=0)
    wd = np.asarray(inputs["L1_attn_w_dwc"]).reshape(CH, 128, 3).transpose(1, 0, 2).reshape(128, CH * 3)
    return {
        "x2T": np.ascontiguousarray(c_results[core]["o_x2"]),
        "qfT": np.ascontiguousarray(d_results[core]["o_qfin"]),
        "kv_all": np.ascontiguousarray(np.stack([np.asarray(d_results[j]["o_kv"]) for j in range(NCORES)])),
        "ks_all": np.ascontiguousarray(np.stack([np.asarray(d_results[j]["o_ks"]) for j in range(NCORES)])),
        "edges_all": np.ascontiguousarray(edges_all),
        "nbr_idx": np.ascontiguousarray(nbr_idx),
        "wprojT": np.ascontiguousarray(np.asarray(inputs["L1_attn_w_proj"]).T),
        "bproj_fm": fm(inputs["L1_attn_b_proj"]),
        "wdwc_fm": np.ascontiguousarray(wd.astype(np.float32)),
        "bdwc_fm": fm(inputs["L1_attn_b_dwc"]),
        "blkb": blk.astype(ml_dtypes.bfloat16),
        "bc16f": np.ascontiguousarray(bc),
        "cvec": cv,
    }



CAP = 512
NTILE = T // 128


def emit_E2(p, nc, dram):
    es = contextlib.ExitStack()

    x3T = dram("x3T", [D, T], F32, "in")
    Wr_in = dram("Wr_in", [D, 8], F32, "in")
    W1_in = dram("W1_in", [8, D, 4 * D], F32, "in")
    W2_in = dram("W2_in", [8, 4 * D, D], F32, "in")
    tri_in = dram("tri_in", [128, 128], BF16, "in")
    ident_in = dram("ident_in", [128, 128], BF16, "in")
    iotaB = dram("iotaB", [128, 8], F32, "in")
    cvec = dram("cvec", [128, 8], F32, "in")
    o_out = dram("o_out", [D, T], F32, "out")
    xs_d = dram("xs_d", [8 * CAP, D], BF16, "int")
    ys_d = dram("ys_d", [8 * CAP, D], BF16, "int")

    sb = lambda name, shape, dt=F32: es.enter_context(nc.sbuf_tensor(name, shape, dt, allow_name_mangling=True))

    cvec_t = sb("cvec_t", [128, 8])
    x3bf = sb("x3bf", [128, CH, T], BF16)
    Wr_t = sb("Wr_t", [128, CH, 8])
    tri_t = sb("tri_t", [128, 128], BF16)
    id_t = sb("id_t", [128, 128], BF16)
    iotaB_t = sb("iotaB_t", [128, 8])
    pos_u = sb("pos_u", [128, NTILE], U32)
    w0_all = sb("w0_all", [128, NTILE])
    ones_colb = sb("ones_colb", [128, 1], BF16)
    ones_rowf = sb("ones_rowf", [1, 128])
    runtot = sb("runtot", [1, 8])
    lg_s = sb("lg_s", [128, 8])
    sel_a = sb("sel_a", [128, 8])
    sel_b = sb("sel_b", [128, 8])
    v0_c = sb("v0_c", [128, 1])
    v1_c = sb("v1_c", [128, 1])
    es_c = sb("es_c", [128, 1])
    rk_c = sb("rk_c", [128, 1])
    oh_bf = sb("oh_bf", [128, 8], BF16)
    x3f_t = sb("x3f_t", [128, CH, 128])
    zeros_b = sb("zeros_b", [128, D], BF16)
    tm_t = sb("tm_t", [128, D], BF16)

    ld = p.sem("ld")
    ldg = p.sem("ldg")
    ldx = p.sem("ldx")
    cm = p.sem("cm")
    mm = p.sem("mm")
    sc = p.sem("sc")
    st = p.sem("st")
    sg = p.sem("sg")
    gth = p.sem("gth")

    p.dma("sync", cvec_t[:], cvec, ld)
    p.dma("sync", tri_t[:], tri_in, ld)
    p.dma("sync", id_t[:], ident_in, ld)
    p.dma("sync", iotaB_t[:], iotaB, ld)
    p.dma("sync", Wr_t[:], Wr_in.rearrange("(c p) e -> p c e", p=128), ld)
    ld_small = ld.v
    p.dma("gpsimd", x3bf[:], x3T.rearrange("(c p) t -> p c t", p=128), ldg)
    ld_big = ldg.v

    p.op("vector", "memset", ones_colb[:], 1.0)
    p.op("vector", "memset", ones_rowf[:], 1.0)
    p.op("vector", "memset", runtot[:], 0.0)
    p.op("vector", "memset", zeros_b[:], 0.0, sem=cm)
    CONSTS = cm.v
    zf = p.sem("zf")
    p.wait("scalar", cm, CONSTS)
    for r in range(8 * CAP // 128):
        p.dma("scalar", xs_d[r * 128:(r + 1) * 128, :], zeros_b[:], zf)
    XS_ZERO = zf.v

    es_p1 = contextlib.ExitStack()
    psn = lambda name, shape, dt=F32: es_p1.enter_context(nc.psum_tensor(name, shape, dt, allow_name_mangling=True))
    ps_lg = psn("ps_lg", [128, 8])
    ps_rk = psn("ps_rk", [128, 8])
    ps_rt = psn("ps_rt", [128, 8])
    ps_ct = psn("ps_ct", [1, 8])

    p.wait("tensor", ld, ld_small)
    p.wait("tensor", cm, CONSTS)
    BIG = 1.0e6
    rt_read = None
    sig_prev = None
    for tile in range(NTILE):
        if sig_prev is not None:
            p.wait("vector", sc, sig_prev)
        p.wait("sync", cm, cm.v)
        p.dma("sync", x3f_t[:], x3T.rearrange("(c p) t -> p c t", p=128)[:, :, tile * 128:(tile + 1) * 128], ldx)
        p.wait("tensor", ldx, ldx.v)
        for ic in range(CH):
            p.op("tensor", "matmul", ps_lg[:], x3f_t[:, ic, :], Wr_t[:, ic, :],
                 start=(ic == 0), stop=(ic == CH - 1), sem=(mm if ic == CH - 1 else None))
        p.wait("vector", mm)
        p.op("vector", "tensor_copy", lg_s[:], ps_lg[:])
        p.op("vector", "tensor_reduce", v0_c[:], lg_s[:], AX.X, ALU.max)
        p.op("vector", "tensor_tensor", sel_a[:], lg_s[:], v0_c[:].broadcast_to([128, 8]), ALU.subtract)
        p.op("vector", "tensor_scalar", sel_a[:], sel_a[:], BIG, None, ALU.mult)
        p.op("vector", "tensor_scalar", sel_a[:], sel_a[:], -1.0, None, ALU.max)
        p.op("vector", "tensor_scalar", sel_a[:], sel_a[:], 1.0, None, ALU.add)
        p.op("vector", "scalar_tensor_tensor", sel_b[:], sel_a[:], -BIG, iotaB_t[:], ALU.mult, ALU.add)
        p.op("vector", "tensor_reduce", es_c[:], sel_b[:], AX.X, ALU.min)
        p.op("vector", "tensor_tensor", sel_b[:], sel_b[:], es_c[:].broadcast_to([128, 8]), ALU.subtract)
        p.op("vector", "tensor_scalar", sel_b[:], sel_b[:], 1.0, None, ALU.min)
        p.op("vector", "scalar_tensor_tensor", sel_b[:], sel_b[:], -1.0, cvec_t[:, 3:4].broadcast_to([128, 8]),
             ALU.mult, ALU.add)
        p.op("vector", "tensor_scalar", sel_b[:], sel_b[:], 1.0, None, ALU.add)
        p.op("vector", "scalar_tensor_tensor", sel_a[:], sel_b[:], -BIG, lg_s[:], ALU.mult, ALU.add)
        p.op("vector", "tensor_reduce", v1_c[:], sel_a[:], AX.X, ALU.max)
        p.op("vector", "tensor_tensor", v0_c[:], v0_c[:], v1_c[:], ALU.subtract, sem=cm)
        p.wait("scalar", cm)
        p.op("scalar", "activation", w0_all[:, tile:tile + 1], v0_c[:], AF.Sigmoid, sem=sc)
        sig_prev = sc.v
        p.op("vector", "tensor_copy", oh_bf[:], sel_b[:], sem=cm)
        OH = cm.v
        p.wait("tensor", cm, OH)
        if rt_read is not None:
            p.wait("tensor", cm, rt_read)
        p.op("tensor", "matmul", ps_rk[:], tri_t[:], oh_bf[:], start=True, stop=True)
        p.op("tensor", "matmul", ps_rt[:], ones_rowf[:], runtot[:], start=True, stop=True)
        p.op("tensor", "matmul", ps_ct[:], ones_colb[:], oh_bf[:], start=True, stop=True,
             sem=mm)
        p.wait("vector", mm)
        p.op("vector", "tensor_copy", sel_a[:], ps_rk[:])
        p.op("vector", "tensor_tensor", sel_a[:], sel_a[:], ps_rt[:], ALU.add)
        p.op("vector", "tensor_tensor", runtot[:], runtot[:], ps_ct[:], ALU.add, sem=cm)
        rt_read = cm.v
        p.op("vector", "tensor_tensor", sel_a[:], sel_a[:], sel_b[:], ALU.mult)
        p.op("vector", "tensor_reduce", rk_c[:], sel_a[:], AX.X, ALU.add)
        p.op("vector", "tensor_scalar", es_c[:], es_c[:], float(CAP), None, ALU.mult)
        p.op("vector", "tensor_tensor", es_c[:], es_c[:], rk_c[:], ALU.add)
        p.op("vector", "tensor_scalar", rk_c[:], rk_c[:], -float(CAP - 1), None, ALU.add)
        p.op("vector", "tensor_scalar", rk_c[:], rk_c[:], 0.0, None, ALU.max)
        p.op("vector", "tensor_scalar", rk_c[:], rk_c[:], 1.0, None, ALU.min)
        p.op("vector", "scalar_tensor_tensor", es_c[:], rk_c[:], 1.0e7, es_c[:], ALU.mult, ALU.add)
        p.op("vector", "tensor_copy", pos_u[:, tile:tile + 1], es_c[:], sem=cm)

    ps_tp = es_p1.enter_context(nc.psum_tensor("ps_tp", [128, 128], BF16, allow_name_mangling=True))
    p.wait("gpsimd", zf, XS_ZERO)
    p.wait("tensor", ldg, ld_big)
    for tile in range(NTILE):
        for ic in range(CH):
            if tile > 0 and ic == 0:
                p.wait("vector", sg, sg.v)
            p.transpose_to(ps_tp, x3bf[:, ic, tile * 128:(tile + 1) * 128], id_t,
                           tm_t[:, ic * 128:(ic + 1) * 128], mm, sem=(cm if ic == CH - 1 else None))
        p.wait("gpsimd", cm, cm.v)
        sg.bump(16)
        p.emit("gpsimd", lambda e, o=xs_d, i=tm_t[:], ix=pos_u[:, tile:tile + 1], s=sg.h:
               e.indirect_dma_start(out=o, out_offset=bass.IndirectOffsetOnAxis(ap=ix, axis=0),
                                    in_=i, in_offset=None, bounds_check=8 * CAP - 1,
                                    oob_is_err=False).then_inc(s, 16))
    SCATTERED = sg.v
    es_p1.close()

    es_p3 = contextlib.ExitStack()
    psn3 = lambda name, shape, dt=F32: es_p3.enter_context(nc.psum_tensor(name, shape, dt, allow_name_mangling=True))
    ps_h = [psn3("ps_h0", [128, CAP]), psn3("ps_h1", [128, CAP])]
    ps_tp2 = psn3("ps_tp2", [128, 128], BF16)

    sb3 = lambda name, shape, dt=F32: es_p3.enter_context(nc.sbuf_tensor(name, shape, dt, allow_name_mangling=True))
    w1s = [sb3(f"w1s{i}", [128, CH, D], BF16) for i in range(2)]
    w2s = [sb3(f"w2s{i}", [128, CH, D], BF16) for i in range(2)]
    xe_fm = sb3("xe_fm", [128, CH, CAP], BF16)
    h_bf = sb3("h_bf", [128, CH, CAP], BF16)
    y_acc = sb3("y_acc", [128, CH, CAP])
    y_bfc = sb3("y_bfc", [128, CH, CAP], BF16)

    p.wait("sync", sg, SCATTERED)
    ldw = [p.sem("ldw0"), p.sem("ldw1")]
    wload = {}
    wcons = {}

    def emit_wload(sl):
        e2, hs2 = sl // 4, sl % 4
        buf2 = sl % 2
        if sl >= 2:
            p.wait("gpsimd", mm, wcons[(sl - 2, "w1")])
        p.dma("gpsimd", w1s[buf2][:], W1_in[e2, :, hs2 * D:(hs2 + 1) * D].rearrange("(c p) h -> p c h", p=128), ldw[buf2])
        if sl >= 2:
            p.wait("gpsimd", mm, wcons[(sl - 2, "w2")])
        p.dma("gpsimd", w2s[buf2][:], W2_in[e2, hs2 * D:(hs2 + 1) * D, :].rearrange("(c p) o -> p c o", p=128), ldw[buf2])
        wload[sl] = (buf2, ldw[buf2].v)

    emit_wload(0)
    emit_wload(1)
    for e in range(8):
        for tt in range(4):
            if e > 0 and tt == 0:
                p.wait("sync", ldx, ygone)
            p.dma("sync", tm_t[:], xs_d[e * CAP + tt * 128: e * CAP + (tt + 1) * 128, :], ldx)
            p.wait("tensor", ldx, ldx.v)
            if e > 0 or tt > 0:
                p.wait("tensor", cm, xe_ev)
            for ic in range(CH):
                p.transpose_to(ps_tp2, tm_t[:, ic * 128:(ic + 1) * 128], id_t,
                               xe_fm[:, ic, tt * 128:(tt + 1) * 128], mm,
                               sem=(cm if (ic == CH - 1) else None))
            xe_ev = cm.v
            p.wait("sync", mm, mm.v)
        p.op("vector", "memset", y_acc[:].rearrange("p c t -> p (c t)"), 0.0, sem=cm)
        YZ = cm.v
        for hs in range(4):
            sl = e * 4 + hs
            buf = sl % 2
            p.wait("tensor", ldw[wload[sl][0]], wload[sl][1])
            p.wait("tensor", cm, xe_ev)
            hprev = {}
            for hc in range(CH):
                ph = ps_h[hc % 2]
                if hc >= 2:
                    p.wait("tensor", sc, hprev[hc - 2])
                for ic in range(CH):
                    p.op("tensor", "matmul", ph[:], w1s[buf][:, ic, hc * 128:(hc + 1) * 128],
                         xe_fm[:, ic, :], start=(ic == 0), stop=(ic == CH - 1),
                         sem=(mm if ic == CH - 1 else None))
                p.wait("scalar", mm)
                p.op("scalar", "activation", h_bf[:, hc, :], ph[:], AF.Gelu, sem=sc)
                hprev[hc] = sc.v
            wcons[(sl, "w1")] = mm.v
            p.wait("tensor", sc, hprev[CH - 1])
            p.wait("scalar", cm, YZ)
            for oc in range(CH):
                py = ps_h[oc % 2]
                for hc in range(CH):
                    p.op("tensor", "matmul", py[:], w2s[buf][:, hc, oc * 128:(oc + 1) * 128],
                         h_bf[:, hc, :], start=(hc == 0), stop=(hc == CH - 1),
                         sem=(mm if hc == CH - 1 else None))
                p.wait("vector", mm)
                p.op("vector", "tensor_tensor", y_acc[:, oc, :], y_acc[:, oc, :], py[:], ALU.add, sem=cm)
                p.wait("tensor", cm, cm.v)
            wcons[(sl, "w2")] = mm.v
            if sl + 2 < 32:
                emit_wload(sl + 2)
        p.op("vector", "tensor_copy", y_bfc[:].rearrange("p c t -> p (c t)"),
             y_acc[:].rearrange("p c t -> p (c t)"), sem=cm)
        YB = cm.v
        p.wait("tensor", cm, YB)
        for blk in range(4):
            if e > 0 or blk > 0:
                p.wait("vector", ldx, ygone)
            for oc in range(CH):
                p.transpose_to(ps_tp2, y_bfc[:, oc, blk * 128:(blk + 1) * 128], id_t,
                               tm_t[:, oc * 128:(oc + 1) * 128], mm,
                               sem=(cm if oc == CH - 1 else None))
            p.wait("sync", cm, cm.v)
            p.dma("sync", ys_d[e * CAP + blk * 128: e * CAP + (blk + 1) * 128, :], tm_t[:], ldx)
            ygone = ldx.v
    es_p3.close()

    es_p4 = contextlib.ExitStack()
    psn4 = lambda name, shape, dt=F32: es_p4.enter_context(nc.psum_tensor(name, shape, dt, allow_name_mangling=True))
    ps_tp3 = psn4("ps_tp3", [128, 128], BF16)
    ln = LNKit(p, sb, ps, cvec_t) if False else None
    ln = LNKit(p, sb, lambda n, s, dt=F32: es_p4.enter_context(nc.psum_tensor(n, s, dt, allow_name_mangling=True)), cvec_t)
    x3f_s = sb("x3f_s", [128, CH, SUB])
    s_sub = sb("s_sub", [128, CH, SUB])
    sbfc = sb("sbfc", [128, CH, SUB], BF16)
    sqbf = sb("sqbf", [128, CH, SUB], BF16)
    out_t = sb("out_t", [128, CH, SUB])
    y_tm = sb("y_tm", [128, D], BF16)

    ln.init_consts(cm)
    stv = []
    for s in range(NSUB):
        p.wait("sync", cm, cm.v)
        p.dma("sync", x3f_s[:], x3T.rearrange("(c p) t -> p c t", p=128)[:, :, s * SUB:(s + 1) * SUB], ldx)
        ldx_s = ldx.v
        p.wait("vector", ldx, ldx_s)
        for tt in range(4):
            tile = s * 4 + tt
            p.op("vector", "memset", y_tm[:], 0.0, sem=cm)
            p.wait("gpsimd", cm, cm.v)
            gth.bump(16)
            p.emit("gpsimd", lambda e, o=y_tm[:], i=ys_d, ix=pos_u[:, tile:tile + 1], s2=gth.h:
                   e.indirect_dma_start(out=o, out_offset=None, in_=i,
                                        in_offset=bass.IndirectOffsetOnAxis(ap=ix, axis=0),
                                        bounds_check=8 * CAP - 1, oob_is_err=False).then_inc(s2, 16))
            p.wait("vector", gth, gth.v)
            p.op("vector", "tensor_scalar", y_tm[:], y_tm[:], w0_all[:, tile:tile + 1], None, ALU.mult, sem=cm)
            YS = cm.v
            p.wait("tensor", cm, YS)
            for oc in range(CH):
                p.wait_last("tensor", "vector")
                p.op("tensor", "transpose", ps_tp3[:], y_tm[:, oc * 128:(oc + 1) * 128], id_t[:], sem=mm)
                p.wait("vector", mm)
                p.op("vector", "tensor_tensor", s_sub[:, oc, tt * 128:(tt + 1) * 128], ps_tp3[:],
                     x3f_s[:, oc, tt * 128:(tt + 1) * 128], ALU.add,
                     sem=(cm if oc == CH - 1 else None))
        for oc in range(CH):
            p.op("vector", "tensor_copy", sbfc[:, oc, :], s_sub[:, oc, :],
                 sem=(cm if oc == CH - 1 else None))
        evd = cm.v
        out_guard = (st, stv[-1]) if s >= 1 else None
        ln.emit(s_sub, sbfc, sqbf, out_t, cm, mm, sc, evd, out_guard=out_guard)
        p.wait("sync", cm, ln.prev_apply_done)
        p.dma("sync", o_out.rearrange("(c p) t -> p c t", p=128)[:, :, s * SUB:(s + 1) * SUB], out_t[:], st)
        stv.append(st.v)
    p.wait("sync", st)
    es_p4.close()
    es.close()


def host_inputs_E2(inputs, core, e1_results):
    import ml_dtypes
    cv = np.zeros((128, 8), dtype=np.float32)
    cv[:, 0] = PI / 2
    cv[:, 1] = 1e-5
    cv[:, 2] = 1e-6
    cv[:, 3] = 0.0
    tri = np.triu(np.ones((128, 128), dtype=np.float32), 1)
    iotab = np.tile(np.arange(8, dtype=np.float32) + 1.0e6, (128, 1))
    return {
        "x3T": np.ascontiguousarray(e1_results[core]["o_x3"]),
        "Wr_in": np.ascontiguousarray(np.asarray(inputs["L1_ffn_Wr"]).astype(np.float32)),
        "W1_in": np.ascontiguousarray(np.asarray(inputs["L1_ffn_W1"]).astype(np.float32)),
        "W2_in": np.ascontiguousarray(np.asarray(inputs["L1_ffn_W2"]).astype(np.float32)),
        "tri_in": tri.astype(ml_dtypes.bfloat16),
        "ident_in": np.eye(128).astype(ml_dtypes.bfloat16),
        "iotaB": np.ascontiguousarray(iotab),
        "cvec": cv,
    }




def _standalone(emit_fn):
    nc = bass.Bass("TRN2", target_bir_lowering=False)
    es = contextlib.ExitStack()
    p = Prog(nc, es)
    emit_fn(p, nc, make_dram(nc))
    p.finalize()
    es.close()
    return nc


def build_launch_A(stages=6):
    nc = bass.Bass("TRN2", target_bir_lowering=False)
    es = contextlib.ExitStack()
    p = Prog(nc, es)
    emit_A(p, nc, make_dram(nc), stages=stages)
    p.finalize()
    es.close()
    return nc


def build_launch_B():
    return _standalone(emit_B)


def build_launch_C():
    return _standalone(emit_C)


def build_launch_D():
    return _standalone(emit_D)


def build_launch_E1():
    return _standalone(emit_E1)


def build_launch_E2():
    return _standalone(emit_E2)


def lamp_slice(lam_t, which):
    return lam_t[:, 4 * which:4 * which + 4]


def host_inputs_A(inputs, core):
    x = np.asarray(inputs["L0_attn_x"])[0]
    xT = np.ascontiguousarray(np.asarray(x).T)
    shard = np.ascontiguousarray(xT[:, core * T:(core + 1) * T])
    BreT = np.ascontiguousarray(np.asarray(inputs["L0_attn_B_re"]).T)
    BimT = np.ascontiguousarray(np.asarray(inputs["L0_attn_B_im"]).T)
    lamp = np.concatenate([
        np.asarray(inputs["L0_attn_nu_log"]).reshape(NG, 128).T,
        np.asarray(inputs["L0_attn_theta_log"]).reshape(NG, 128).T,
        np.asarray(inputs["L0_attn_gamma_log"]).reshape(NG, 128).T,
    ], axis=1).astype(np.float32)
    iotaL = np.tile(np.arange(L, dtype=np.float32), (128, 1))
    cv = np.zeros((128, 8), dtype=np.float32)
    cv[:, 0] = PI / 2
    cv[:, 1] = 1e-5
    cv[:, 2] = 1e-6
    return {"xT": shard, "BreT": BreT, "BimT": BimT, "lamp": np.ascontiguousarray(lamp),
            "iotaL": np.ascontiguousarray(iotaL), "cvec": cv}


FUSED_INTERNAL = ("cum_d_re", "cum_d_im", "bsum_d", "ball", "x1_d", "x2_d",
                  "qfin_d", "kv_d", "ks_d", "kv_red", "ks_red", "edge_d",
                  "edges_ag", "x3_d", "xs_d", "ys_d")
RG = [list(range(NCORES))]


def build_fused(upto=6, use_cc=True, x3_external=False, ext_names=(), a_stages=6):
    nc = bass.Bass("TRN2", target_bir_lowering=False)
    es = contextlib.ExitStack()
    p = Prog(nc, es)
    cache = {}
    _ext = set(ext_names) | ({"x3_d"} if x3_external else set())
    internal = tuple(n for n in FUSED_INTERNAL if n not in _ext)

    def dr(rename):
        return make_dram(nc, rename=rename, internal=internal, cache=cache)

    cc = p.sem("cc")
    emit_A(p, nc, dr({"o_cre": "cum_d_re", "o_cim": "cum_d_im", "o_bsum": "bsum_d"}), stages=a_stages)
    if upto <= 1:
        p.finalize(); es.close(); return nc
    p.global_barrier()
    bsum_d = cache["bsum_d"]
    ball = nc.dram_tensor("ball", [NCORES * 128, 8], F32, kind="Internal").ap()
    cache["ball"] = ball
    if use_cc:
        cc.bump(16)
        p.emit("gpsimd", lambda e, i=bsum_d, o=ball, s=cc.h:
               e.collective_compute("AllGather", ALU.bypass, RG, ins=[i], outs=[o]).then_inc(s, 16))
        p.wait("sync", cc, cc.v)
    emit_B(p, nc, dr({"i_cre": "cum_d_re", "i_cim": "cum_d_im", "o_x1": "x1_d"}))
    if upto <= 2:
        p.finalize(); es.close(); return nc
    p.global_barrier()
    emit_C(p, nc, dr({"x1T": "x1_d", "o_x2": "x2_d"}))
    if upto <= 3:
        p.finalize(); es.close(); return nc
    p.global_barrier()
    emit_D(p, nc, dr({"x2T": "x2_d", "o_qfin": "qfin_d", "o_kv": "kv_d",
                      "o_ks": "ks_d", "o_edge": "edge_d"}))
    if upto <= 4:
        p.finalize(); es.close(); return nc
    p.global_barrier()
    kv_red = nc.dram_tensor("kv_red", [64, 1024], F32, kind="Internal").ap()
    ks_red = nc.dram_tensor("ks_red", [128, 8], F32, kind="Internal").ap()
    edges_ag = nc.dram_tensor("edges_ag", [NCORES * 128, 16], BF16, kind="Internal").ap()
    cache.update({"kv_red": kv_red, "ks_red": ks_red, "edges_ag": edges_ag})
    cc.bump(16)
    p.emit("gpsimd", lambda e, i=cache["kv_d"], o=kv_red, s=cc.h:
           e.collective_compute("AllReduce", ALU.add, RG, ins=[i], outs=[o]).then_inc(s, 16))
    cc.bump(16)
    p.emit("gpsimd", lambda e, i=cache["ks_d"], o=ks_red, s=cc.h:
           e.collective_compute("AllReduce", ALU.add, RG, ins=[i], outs=[o]).then_inc(s, 16))
    cc.bump(16)
    p.emit("gpsimd", lambda e, i=cache["edge_d"], o=edges_ag, s=cc.h:
           e.collective_compute("AllGather", ALU.bypass, RG, ins=[i], outs=[o]).then_inc(s, 16))
    p.wait("gpsimd", cc, cc.v)
    p.wait("sync", cc, cc.v)
    emit_E1(p, nc, dr({"x2T": "x2_d", "qfT": "qfin_d", "edges_all": "edges_ag",
                       "o_x3": "x3_d"}), reduced_kv=True)
    if upto <= 5:
        p.finalize(); es.close(); return nc
    p.global_barrier()
    emit_E2(p, nc, dr({"x3T": "x3_d"}))
    p.finalize()
    es.close()
    return nc


def host_inputs_fused(inputs, core):
    handover = {"i_cre", "i_cim", "ball", "x1T", "x2T", "x3T", "qfT",
                "kv_all", "ks_all", "edges_all"}
    fake_a = [{"o_cre": None, "o_cim": None, "o_bsum": None}] * NCORES
    d = {}
    d.update(host_inputs_A(inputs, core))
    b = host_inputs_B(inputs, core, [{"o_cre": np.zeros((128, NG * T), np.float32),
                                      "o_cim": np.zeros((128, NG * T), np.float32),
                                      "o_bsum": np.zeros((128, 8), np.float32)}] * NCORES)
    d.update({k: v for k, v in b.items() if k not in handover})
    c = host_inputs_C(inputs, core, [{"o_x1": np.zeros((D, T), np.float32)}] * NCORES)
    d.update({k: v for k, v in c.items() if k not in handover})
    dd = host_inputs_D(inputs, core, [{"o_x2": np.zeros((D, T), np.float32)}] * NCORES)
    d.update({k: v for k, v in dd.items() if k not in handover})
    import ml_dtypes
    e1 = host_inputs_E1(inputs, core,
                        [{"o_x2": np.zeros((D, T), np.float32)}] * NCORES,
                        [{"o_qfin": np.zeros((D, T), ml_dtypes.bfloat16),
                          "o_kv": np.zeros((64, 1024), np.float32),
                          "o_ks": np.zeros((128, 8), np.float32),
                          "o_edge": np.zeros((128, 16), ml_dtypes.bfloat16)}] * NCORES)
    d.update({k: v for k, v in e1.items() if k not in handover})
    e2 = host_inputs_E2(inputs, core, [{"o_x3": np.zeros((D, T), np.float32)}] * NCORES)
    d.update({k: v for k, v in e2.items() if k not in handover})
    return d


def kernel_fused(**inputs):
    cores = list(range(NCORES))
    nc = build_fused()
    res = run_bass_kernel_spmd(nc, [host_inputs_fused(inputs, c) for c in cores], core_ids=cores)
    out = np.concatenate(
        [np.asarray(res.results[c]["o_out"]).reshape(CH, 128, T).transpose(2, 0, 1).reshape(T, D)
         for c in cores], axis=0)
    return out.reshape(1, NCORES * T, D).astype(np.float32)


def kernel_unfused(**inputs):
    cores = list(range(NCORES))
    ra = run_bass_kernel_spmd(build_launch_A(), [host_inputs_A(inputs, c) for c in cores], core_ids=cores).results
    rb = run_bass_kernel_spmd(build_launch_B(), [host_inputs_B(inputs, c, ra) for c in cores], core_ids=cores).results
    rc = run_bass_kernel_spmd(build_launch_C(), [host_inputs_C(inputs, c, rb) for c in cores], core_ids=cores).results
    rd = run_bass_kernel_spmd(build_launch_D(), [host_inputs_D(inputs, c, rc) for c in cores], core_ids=cores).results
    re1 = run_bass_kernel_spmd(build_launch_E1(), [host_inputs_E1(inputs, c, rc, rd) for c in cores], core_ids=cores).results
    re2 = run_bass_kernel_spmd(build_launch_E2(), [host_inputs_E2(inputs, c, re1) for c in cores], core_ids=cores).results
    out = np.concatenate(
        [np.asarray(re2[c]["o_out"]).reshape(CH, 128, T).transpose(2, 0, 1).reshape(T, D) for c in cores], axis=0)
    return out.reshape(1, NCORES * T, D).astype(np.float32)


def build_fused_BE2():
    nc = bass.Bass("TRN2", target_bir_lowering=False)
    es = contextlib.ExitStack()
    p = Prog(nc, es)
    cache = {}
    internal = tuple(n for n in FUSED_INTERNAL
                     if n not in ("cum_d_re", "cum_d_im", "ball"))

    def dr(rename):
        return make_dram(nc, rename=rename, internal=internal, cache=cache)

    cc = p.sem("cc")
    emit_B(p, nc, dr({"i_cre": "cum_d_re", "i_cim": "cum_d_im", "o_x1": "x1_d"}))
    p.global_barrier()
    emit_C(p, nc, dr({"x1T": "x1_d", "o_x2": "x2_d"}))
    p.global_barrier()
    emit_D(p, nc, dr({"x2T": "x2_d", "o_qfin": "qfin_d", "o_kv": "kv_d",
                      "o_ks": "ks_d", "o_edge": "edge_d"}))
    p.global_barrier()
    kv_red = nc.dram_tensor("kv_red", [64, 1024], F32, kind="Internal").ap()
    ks_red = nc.dram_tensor("ks_red", [128, 8], F32, kind="Internal").ap()
    edges_ag = nc.dram_tensor("edges_ag", [NCORES * 128, 16], BF16, kind="Internal").ap()
    cache.update({"kv_red": kv_red, "ks_red": ks_red, "edges_ag": edges_ag})
    cc.bump(16)
    p.emit("gpsimd", lambda e, i=cache["kv_d"], o=kv_red, s=cc.h:
           e.collective_compute("AllReduce", ALU.add, RG, ins=[i], outs=[o]).then_inc(s, 16))
    cc.bump(16)
    p.emit("gpsimd", lambda e, i=cache["ks_d"], o=ks_red, s=cc.h:
           e.collective_compute("AllReduce", ALU.add, RG, ins=[i], outs=[o]).then_inc(s, 16))
    cc.bump(16)
    p.emit("gpsimd", lambda e, i=cache["edge_d"], o=edges_ag, s=cc.h:
           e.collective_compute("AllGather", ALU.bypass, RG, ins=[i], outs=[o]).then_inc(s, 16))
    p.wait("gpsimd", cc, cc.v)
    p.wait("sync", cc, cc.v)
    emit_E1(p, nc, dr({"x2T": "x2_d", "qfT": "qfin_d", "edges_all": "edges_ag",
                       "o_x3": "x3_d"}), reduced_kv=True)
    p.global_barrier()
    emit_E2(p, nc, dr({"x3T": "x3_d"}))
    p.finalize()
    es.close()
    return nc


def host_inputs_BE2(inputs, core, a_results):
    d = host_inputs_fused(inputs, core)
    d.pop("BreT", None); d.pop("BimT", None); d.pop("iotaL", None)
    d["cum_d_re"] = np.ascontiguousarray(np.asarray(a_results[core]["o_cre"]))
    d["cum_d_im"] = np.ascontiguousarray(np.asarray(a_results[core]["o_cim"]))
    d["ball"] = np.concatenate([np.asarray(a_results[j]["o_bsum"]) for j in range(NCORES)],
                               axis=0).astype(np.float32)
    return d


def build_fused_BCD():
    nc = bass.Bass("TRN2", target_bir_lowering=False)
    es = contextlib.ExitStack()
    p = Prog(nc, es)
    cache = {}
    internal = ("x1_d",)

    def dr(rename):
        return make_dram(nc, rename=rename, internal=internal, cache=cache)

    emit_B(p, nc, dr({"i_cre": "cum_d_re", "i_cim": "cum_d_im", "o_x1": "x1_d"}))
    p.global_barrier()
    emit_C(p, nc, dr({"x1T": "x1_d"}))
    p.global_barrier()
    emit_D(p, nc, dr({"x2T": "o_x2"}))
    p.finalize()
    es.close()
    return nc


def build_fused_E1E2():
    nc = bass.Bass("TRN2", target_bir_lowering=False)
    es = contextlib.ExitStack()
    p = Prog(nc, es)
    cache = {}
    internal = ("x3_d", "xs_d", "ys_d")

    def dr(rename):
        return make_dram(nc, rename=rename, internal=internal, cache=cache)

    emit_E1(p, nc, dr({"o_x3": "x3_d"}))
    p.global_barrier()
    emit_E2(p, nc, dr({"x3T": "x3_d"}))
    p.finalize()
    es.close()
    return nc


def kernel_3launch(**inputs):
    cores = list(range(NCORES))
    ra = run_bass_kernel_spmd(build_launch_A(), [host_inputs_A(inputs, c) for c in cores],
                              core_ids=cores).results
    nc2 = build_fused_BCD()
    im2 = []
    for c in cores:
        d = host_inputs_B(inputs, c, ra)
        d["i_cre"] = d.pop("i_cre") if "i_cre" in d else d.get("cum_d_re")
        dc = host_inputs_C(inputs, c, [{"o_x1": np.zeros((D, T), np.float32)}] * NCORES)
        dc.pop("x1T", None)
        d.update(dc)
        dd = host_inputs_D(inputs, c, [{"o_x2": np.zeros((D, T), np.float32)}] * NCORES)
        dd.pop("x2T", None)
        d.update(dd)
        d["cum_d_re"] = d.pop("i_cre")
        d["cum_d_im"] = d.pop("i_cim")
        im2.append(d)
    r2 = run_bass_kernel_spmd(nc2, im2, core_ids=cores).results
    nc3 = build_fused_E1E2()
    im3 = []
    for c in cores:
        d = host_inputs_E1(inputs, c, r2, r2)
        de = host_inputs_E2(inputs, c, [{"o_x3": np.zeros((D, T), np.float32)}] * NCORES)
        de.pop("x3T", None)
        d.update(de)
        im3.append(d)
    r3 = run_bass_kernel_spmd(nc3, im3, core_ids=cores).results
    out = np.concatenate(
        [np.asarray(r3[c]["o_out"]).reshape(CH, 128, T).transpose(2, 0, 1).reshape(T, D)
         for c in cores], axis=0)
    return out.reshape(1, NCORES * T, D).astype(np.float32)


def kernel_2launchC(**inputs):
    cores = list(range(NCORES))
    nc1 = build_fused(upto=4, ext_names=("qfin_d", "kv_d", "ks_d", "edge_d", "x2_d"))
    r1 = run_bass_kernel_spmd(nc1, [host_inputs_fused(inputs, c) for c in cores], core_ids=cores).results
    r1m = [{"o_x2": rr["x2_d"], "o_qfin": rr["qfin_d"], "o_kv": rr["kv_d"],
            "o_ks": rr["ks_d"], "o_edge": rr["edge_d"]} for rr in r1]
    nc3 = build_fused_E1E2()
    im3 = []
    for c in cores:
        d = host_inputs_E1(inputs, c, r1m, r1m)
        de = host_inputs_E2(inputs, c, [{"o_x3": np.zeros((D, T), np.float32)}] * NCORES)
        de.pop("x3T", None)
        d.update(de)
        im3.append(d)
    r3 = run_bass_kernel_spmd(nc3, im3, core_ids=cores).results
    out = np.concatenate(
        [np.asarray(r3[c]["o_out"]).reshape(CH, 128, T).transpose(2, 0, 1).reshape(T, D)
         for c in cores], axis=0)
    return out.reshape(1, NCORES * T, D).astype(np.float32)


def kernel_2launchB(**inputs):
    cores = list(range(NCORES))
    ra = run_bass_kernel_spmd(build_launch_A(), [host_inputs_A(inputs, c) for c in cores],
                              core_ids=cores).results
    nc2 = build_fused_BE2()
    res2 = run_bass_kernel_spmd(nc2, [host_inputs_BE2(inputs, c, ra) for c in cores], core_ids=cores)
    out = np.concatenate(
        [np.asarray(res2.results[c]["o_out"]).reshape(CH, 128, T).transpose(2, 0, 1).reshape(T, D)
         for c in cores], axis=0)
    return out.reshape(1, NCORES * T, D).astype(np.float32)


def kernel_2launch(**inputs):
    cores = list(range(NCORES))
    nc1 = build_fused(upto=5, x3_external=True)
    res1 = run_bass_kernel_spmd(nc1, [host_inputs_fused(inputs, c) for c in cores], core_ids=cores)
    r1 = res1.results
    e1r = [{"o_x3": np.asarray(r1[c]["x3_d"])} for c in cores]
    nc2 = build_launch_E2()
    res2 = run_bass_kernel_spmd(nc2, [host_inputs_E2(inputs, c, e1r) for c in cores], core_ids=cores)
    out = np.concatenate(
        [np.asarray(res2.results[c]["o_out"]).reshape(CH, 128, T).transpose(2, 0, 1).reshape(T, D)
         for c in cores], axis=0)
    return out.reshape(1, NCORES * T, D).astype(np.float32)


def kernel(**inputs):
    return kernel_3launch(**inputs)


# version 13
# speedup vs baseline: 1.0094x; 1.0094x over previous
import contextlib
import numpy as np
import concourse.bass as bass
import concourse.mybir as mybir
from concourse.bass_utils import run_bass_kernel_spmd

F32 = mybir.dt.float32
BF16 = mybir.dt.bfloat16
U32 = mybir.dt.uint32
AF = mybir.ActivationFunctionType
ALU = mybir.AluOpType
AX = mybir.AxisListType

NCORES = 8
T = 2048
D = 1024
CH = 8
SUB = 512
NSUB = T // SUB
N_SC = 512
NG = 4
L = 64
CJ = T // L
PI = float(np.pi)
TWO_PI = float(2 * np.pi)

ENGINES = ("sync", "tensor", "vector", "scalar", "gpsimd")


class Sem:
    def __init__(self, handle):
        self.h = handle
        self.v = 0

    def bump(self, n):
        self.v += n
        return self.v


class Prog:

    def __init__(self, nc, es):
        self.nc = nc
        self.es = es
        self.q = {e: [] for e in ENGINES}
        self._semnames = {}
        self._last = {}

    def sem(self, name):
        if name not in self._semnames:
            self._semnames[name] = Sem(self.es.enter_context(self.nc.semaphore(name)))
        return self._semnames[name]

    def emit(self, eng, fn):
        self.q[eng].append(fn)

    def wait_last(self, eng, producer):
        last = self._last.get(producer)
        if last is not None:
            self.wait(eng, last[0], last[1])

    def wait(self, eng, sem, v=None):
        if v is None:
            v = sem.v
        self.emit(eng, lambda e, s=sem.h, vv=v: e.wait_ge(s, vv))

    SERIAL = ("vector", "scalar", "gpsimd")

    def op(self, eng, name, *args, sem=None, inc=1, **kw):
        if eng in self.SERIAL:
            last = self._last.get(eng)
            if last is not None:
                self.wait(eng, last[0], last[1])
            if sem is None:
                sem = self.sem("_chain_" + eng)
                inc = 1
            sem.bump(inc)
            self._last[eng] = (sem, sem.v)

            def fn(e, n=name, a=args, k=kw, s=sem.h, i=inc):
                getattr(e, n)(*a, **k).then_inc(s, i)
        elif sem is not None:
            sem.bump(inc)

            def fn(e, n=name, a=args, k=kw, s=sem.h, i=inc):
                getattr(e, n)(*a, **k).then_inc(s, i)
        else:
            def fn(e, n=name, a=args, k=kw):
                getattr(e, n)(*a, **k)
        self.emit(eng, fn)

    def dma(self, eng, out, in_, sem, **kw):
        sem.bump(16)
        self.emit(eng, lambda e, o=out, i=in_, s=sem.h, k=kw: e.dma_start(out=o, in_=i, **k).then_inc(s, 16))

    def global_barrier(self):
        for eng in ENGINES:
            for s in self._semnames.values():
                if s.v > 0:
                    self.wait(eng, s, s.v)

    def transpose_to(self, ps_tile, in_ap, id_t, dest_ap, mm, sem=None):
        self.wait_last("tensor", "vector")
        self.op("tensor", "transpose", ps_tile[:], in_ap, id_t[:], sem=mm)
        self.wait("vector", mm)
        self.op("vector", "tensor_copy", dest_ap, ps_tile[:], sem=sem)

    def finalize(self):
        block = self.es.enter_context(self.nc.Block())
        for eng in ENGINES:
            getattr(block, eng)(lambda e, q=tuple(self.q[eng]): [f(e) for f in q])


def make_dram(nc, rename=None, internal=(), cache=None):
    cache = {} if cache is None else cache
    rename = rename or {}

    def dram(name, shape, dtype, io):
        canon = rename.get(name, name)
        if canon in cache:
            return cache[canon]
        if canon in internal:
            kind = "Internal"
        else:
            kind = "ExternalInput" if io == "in" else "ExternalOutput"
        ap = nc.dram_tensor(canon, list(shape), dtype, kind=kind).ap()
        cache[canon] = ap
        return ap

    return dram


def emit_A(p, nc, dram, stages=6):
    es = contextlib.ExitStack()

    xT = dram("xT", [D, T], F32, "in")
    BreT = dram("BreT", [D, N_SC], F32, "in")
    BimT = dram("BimT", [D, N_SC], F32, "in")
    lamp = dram("lamp", [128, 12], F32, "in")
    iotaL = dram("iotaL", [128, L], F32, "in")
    cvec = dram("cvec", [128, 8], F32, "in")
    o_cre = dram("o_cre", [128, NG * T], F32, "out")
    o_cim = dram("o_cim", [128, NG * T], F32, "out")
    o_bsum = dram("o_bsum", [128, 8], F32, "out")

    sb = lambda name, shape, dt=F32: es.enter_context(nc.sbuf_tensor(name, shape, dt, allow_name_mangling=True))
    ps = lambda name, shape, dt=F32: es.enter_context(nc.psum_tensor(name, shape, dt, allow_name_mangling=True))

    lam_t = sb("lam_t", [128, 12])
    cvec_t = sb("cvec_t", [128, 8])
    iotaL_t = sb("iotaL_t", [128, L])
    c4 = sb("c4", [128, 4 * 16])
    tabs = sb("tabs", [128, NG, 4 * L])
    ang = sb("ang", [128, NG, L])
    s_k = sb("s_k", [128, NG, L])
    c_k = sb("c_k", [128, NG, L])
    m_k = sb("m_k", [128, NG, L])
    t_k = sb("t_k", [128, NG, L])
    w_re = sb("w_re", [128, NG, T])
    w_im = sb("w_im", [128, NG, T])
    ev1 = sb("ev1", [128, SUB])
    ev2 = sb("ev2", [128, SUB])
    sA_re = sb("sA_re", [128, NG, CJ])
    sA_im = sb("sA_im", [128, NG, CJ])
    sB_re = sb("sB_re", [128, NG, CJ])
    sB_im = sb("sB_im", [128, NG, CJ])
    t32a = sb("t32a", [128, NG, CJ])
    t32b = sb("t32b", [128, NG, CJ])
    hsr = sb("hsr", [128, NG, 5])
    hsi = sb("hsi", [128, NG, 5])
    l63r_all = sb("l63r_all", [128, NG])
    l63i_all = sb("l63i_all", [128, NG])
    bsum_t = sb("bsum_t", [128, 8])
    qpr = sb("qpr", [128, NG, 6])
    qpi = sb("qpi", [128, NG, 6])
    qnr = sb("qnr", [128, NG, 6])
    qni = sb("qni", [128, NG, 6])

    es1 = contextlib.ExitStack()
    sb1 = lambda name, shape, dt=F32: es1.enter_context(nc.sbuf_tensor(name, shape, dt, allow_name_mangling=True))
    x_bf = sb1("x_bf", [128, CH, T], BF16)
    Bt_re = sb1("Bt_re", [128, CH, N_SC], BF16)
    Bt_im = sb1("Bt_im", [128, CH, N_SC], BF16)

    ps_re = ps("ps_re", [128, SUB])
    ps_im = ps("ps_im", [128, SUB])
    ps_re2 = ps("ps_re2", [128, SUB])
    ps_im2 = ps("ps_im2", [128, SUB])

    ld = p.sem("ld")
    cm = p.sem("cm")
    mm = p.sem("mm")
    sc = p.sem("sc")
    st = p.sem("st")

    p.dma("sync", lam_t[:], lamp, ld)
    p.dma("sync", iotaL_t[:], iotaL, ld)
    p.dma("sync", cvec_t[:], cvec, ld)
    ld_small = ld.v
    ldg = p.sem("ldg")
    p.dma("gpsimd", x_bf[:], xT.rearrange("(c p) t -> p c t", p=128), ldg)
    p.dma("gpsimd", Bt_re[:], BreT.rearrange("(c p) n -> p c n", p=128), ldg)
    p.dma("gpsimd", Bt_im[:], BimT.rearrange("(c p) n -> p c n", p=128), ldg)
    ld_big = ldg.v

    if stages < 2:
        p.wait("sync", ld, ld_small)
        p.wait("sync", ldg, ld_big)
        p.dma("sync", o_bsum, bsum_t[:], st)
        p.wait("sync", st)
        es1.close(); es.close(); return

    def slot(i):
        return c4[:, 4 * i:4 * i + 4]
    NU, TH, GA, ENU, RHO, THR, SINT, COST, LR, LI, LAr, LAi, TMP1, TMP2, TMP3, TMP4 = range(16)

    p.wait("scalar", ld, ld_small)
    p.op("scalar", "activation", slot(ENU), lamp_slice(lam_t, 0), AF.Exp)
    p.op("scalar", "activation", slot(RHO), slot(ENU), AF.Exp, scale=-1.0)
    p.op("scalar", "activation", slot(TH), lamp_slice(lam_t, 1), AF.Exp)
    p.op("scalar", "activation", slot(GA), lamp_slice(lam_t, 2), AF.Exp, sem=sc)
    p.wait("vector", sc)
    p.op("vector", "tensor_scalar", slot(TMP1), slot(TH), -PI, None, ALU.add)
    p.op("vector", "tensor_scalar", slot(TMP1), slot(TMP1), 1e9, None, ALU.mult)
    p.op("vector", "tensor_scalar", slot(TMP1), slot(TMP1), 0.0, None, ALU.max)
    p.op("vector", "tensor_scalar", slot(TMP1), slot(TMP1), 1.0, None, ALU.min)
    p.op("vector", "scalar_tensor_tensor", slot(THR), slot(TMP1), -TWO_PI, slot(TH), ALU.mult, ALU.add, sem=cm)
    p.wait("scalar", cm)
    p.op("scalar", "activation", slot(SINT), slot(THR), AF.Sin, sem=sc)
    p.wait("vector", sc)
    p.op("vector", "tensor_scalar", slot(TMP1), slot(THR), -1.0, None, ALU.mult)
    p.op("vector", "tensor_tensor", slot(TMP1), slot(TMP1), slot(THR), ALU.max, sem=cm)
    p.wait("scalar", cm)
    p.op("scalar", "activation", slot(COST), slot(TMP1), AF.Sin, scale=-1.0, bias=cvec_t[:, 0:1], sem=sc)
    p.wait("vector", sc)
    p.op("vector", "tensor_tensor", slot(LR), slot(RHO), slot(COST), ALU.mult)
    p.op("vector", "tensor_tensor", slot(LI), slot(RHO), slot(SINT), ALU.mult, sem=cm)

    def Tn_re(g):
        return tabs[:, g, 0 * L:1 * L]

    def Tn_im(g):
        return tabs[:, g, 1 * L:2 * L]

    def Tp_re(g):
        return tabs[:, g, 2 * L:3 * L]

    def Tp_im(g):
        return tabs[:, g, 3 * L:4 * L]

    NLr, NLi = NU, TH
    p.op("vector", "tensor_tensor", slot(TMP2), slot(RHO), slot(RHO), ALU.mult)
    p.op("vector", "reciprocal", slot(TMP2), slot(TMP2))
    p.op("vector", "tensor_tensor", slot(NLr), slot(LR), slot(TMP2), ALU.mult)
    p.op("vector", "tensor_tensor", slot(NLi), slot(LI), slot(TMP2), ALU.mult)
    p.op("vector", "tensor_scalar", slot(NLi), slot(NLi), -1.0, None, ALU.mult)

    def cplx_mulv(ar, ai, br, bi, outr, outi, tmp1, tmp2):
        p.op("vector", "tensor_tensor", tmp1, ar, br, ALU.mult)
        p.op("vector", "tensor_tensor", tmp2, ai, bi, ALU.mult)
        p.op("vector", "tensor_tensor", outr, tmp1, tmp2, ALU.subtract)
        p.op("vector", "tensor_tensor", tmp1, ar, bi, ALU.mult)
        p.op("vector", "tensor_tensor", tmp2, ai, br, ALU.mult)
        p.op("vector", "tensor_tensor", outi, tmp1, tmp2, ALU.add)

    p.op("vector", "tensor_copy", qpr[:, :, 0], slot(LR))
    p.op("vector", "tensor_copy", qpi[:, :, 0], slot(LI))
    p.op("vector", "tensor_copy", qnr[:, :, 0], slot(NLr))
    p.op("vector", "tensor_copy", qni[:, :, 0], slot(NLi))
    for m in range(1, 6):
        cplx_mulv(qpr[:, :, m - 1], qpi[:, :, m - 1], qpr[:, :, m - 1], qpi[:, :, m - 1],
                  qpr[:, :, m], qpi[:, :, m], slot(TMP1), slot(TMP2))
        cplx_mulv(qnr[:, :, m - 1], qni[:, :, m - 1], qnr[:, :, m - 1], qni[:, :, m - 1],
                  qnr[:, :, m], qni[:, :, m], slot(TMP1), slot(TMP2))
    p.op("vector", "memset", tabs[:, :, 0 * L:0 * L + 1], 1.0)
    p.op("vector", "memset", tabs[:, :, 1 * L:1 * L + 1], 0.0)
    p.op("vector", "memset", tabs[:, :, 2 * L:2 * L + 1], 1.0)
    p.op("vector", "memset", tabs[:, :, 3 * L:3 * L + 1], 0.0)
    for m in range(6):
        d = 1 << m
        for g in range(NG):
            for (tr, ti, qr_, qi_) in ((Tp_re(g), Tp_im(g), qpr, qpi), (Tn_re(g), Tn_im(g), qnr, qni)):
                t1 = ang[:, g, 0:d]
                t2 = s_k[:, g, 0:d]
                p.op("vector", "tensor_scalar", t1, tr[:, 0:d], qr_[:, g, m:m + 1], None, ALU.mult)
                p.op("vector", "tensor_scalar", t2, ti[:, 0:d], qi_[:, g, m:m + 1], None, ALU.mult)
                p.op("vector", "tensor_tensor", tr[:, d:2 * d], t1, t2, ALU.subtract)
                p.op("vector", "tensor_scalar", t1, tr[:, 0:d], qi_[:, g, m:m + 1], None, ALU.mult)
                p.op("vector", "tensor_scalar", t2, ti[:, 0:d], qr_[:, g, m:m + 1], None, ALU.mult)
                p.op("vector", "tensor_tensor", ti[:, d:2 * d], t1, t2, ALU.add)
    for g in range(NG):
        p.op("vector", "tensor_scalar", Tn_re(g), Tn_re(g), slot(GA)[:, g:g + 1], None, ALU.mult)
        p.op("vector", "tensor_scalar", Tn_im(g), Tn_im(g), slot(GA)[:, g:g + 1], None,
             ALU.mult, sem=(cm if g == NG - 1 else None))
    TABS_DONE = cm.v

    if stages < 3:
        p.wait("sync", cm, TABS_DONE)
        p.dma("sync", o_bsum, bsum_t[:], st)
        p.wait("sync", st)
        es1.close(); es.close(); return

    p.wait("tensor", ldg, ld_big)
    p.wait("tensor", cm, TABS_DONE)
    psums = [(ps_re, ps_im), (ps_re2, ps_im2)]
    n_evac = 0
    for g in range(NG):
        for s in range(NSUB):
            i = g * NSUB + s
            pre, pim = psums[i % 2]
            if i >= 2:
                p.wait("tensor", cm, TABS_DONE + i - 1)
            for ic in range(CH):
                p.op("tensor", "matmul", pre[:], Bt_re[:, ic, g * 128:(g + 1) * 128],
                     x_bf[:, ic, s * SUB:(s + 1) * SUB], start=(ic == 0), stop=(ic == CH - 1))
            for ic in range(CH):
                p.op("tensor", "matmul", pim[:], Bt_im[:, ic, g * 128:(g + 1) * 128],
                     x_bf[:, ic, s * SUB:(s + 1) * SUB], start=(ic == 0), stop=(ic == CH - 1),
                     sem=(mm if ic == CH - 1 else None))
            p.wait("vector", mm)
            pre3 = pre[:].rearrange("p (j k) -> p j k", k=L)
            pim3 = pim[:].rearrange("p (j k) -> p j k", k=L)
            ev1_3 = ev1[:].rearrange("p (j k) -> p j k", k=L)
            ev2_3 = ev2[:].rearrange("p (j k) -> p j k", k=L)
            wre3 = w_re[:, g, s * SUB:(s + 1) * SUB].rearrange("p (j k) -> p j k", k=L)
            wim3 = w_im[:, g, s * SUB:(s + 1) * SUB].rearrange("p (j k) -> p j k", k=L)
            tnr = Tn_re(g).unsqueeze(1).broadcast_to([128, SUB // L, L])
            tni = Tn_im(g).unsqueeze(1).broadcast_to([128, SUB // L, L])
            p.op("vector", "tensor_tensor", ev1_3, pre3, tnr, ALU.mult)
            p.op("vector", "tensor_tensor", ev2_3, pim3, tni, ALU.mult)
            p.op("vector", "tensor_tensor", wre3, ev1_3, ev2_3, ALU.subtract)
            p.op("vector", "tensor_tensor", ev1_3, pre3, tni, ALU.mult)
            p.op("vector", "tensor_tensor", ev2_3, pim3, tnr, ALU.mult)
            p.op("vector", "tensor_tensor", wim3, ev1_3, ev2_3, ALU.add, sem=cm)
            n_evac += 1
    es1.close()
    cum_re = sb("cum_re_s", [128, NG, T])
    cum_im = sb("cum_im_s", [128, NG, T])
    msk = sb("msk", [128, NG * T])

    if stages < 4:
        p.wait("sync", cm)
        p.dma("sync", o_cre, w_re[:].rearrange("p g t -> p (g t)"), st)
        p.dma("sync", o_cim, w_im[:].rearrange("p g t -> p (g t)"), st)
        p.dma("sync", o_cre[:, 5000:5064], c4[:], st)
        p.dma("sync", o_cre[:, 6000:6000 + NG * 4 * L], tabs[:].rearrange("p g k -> p (g k)"), st)
        p.dma("sync", o_cre[:, 7100:7100 + NG * 6], qnr[:].rearrange("p g m -> p (g m)"), st)
        p.dma("sync", o_cre[:, 7200:7200 + NG * 6], qpr[:].rearrange("p g m -> p (g m)"), st)
        p.dma("sync", o_bsum[:, 0:4], tabs[:, 0, 0:4], st)
        p.wait("sync", st)
        es1.close(); es.close(); return

    p.op("vector", "memset", msk[:], 1.0)
    p.op("vector", "memset", msk[:].rearrange("p (c k) -> p c k", k=L)[:, :, 0:1], 0.0)
    cre_f = cum_re[:].rearrange("p g t -> p (g t)")
    cim_f = cum_im[:].rearrange("p g t -> p (g t)")
    wre_f = w_re[:].rearrange("p g t -> p (g t)")
    wim_f = w_im[:].rearrange("p g t -> p (g t)")
    p.op("vector", "tensor_tensor_scan", cre_f, msk[:], wre_f, 0.0, ALU.mult, ALU.add)
    p.op("vector", "tensor_tensor_scan", cim_f, msk[:], wim_f, 0.0, ALU.mult, ALU.add, sem=cm)

    if stages < 5:
        p.wait("sync", cm)
        p.dma("sync", o_cre, cre_f, st)
        p.dma("sync", o_cim, cim_f, st)
        p.wait("sync", st)
        es1.close(); es.close(); return

    cre4 = cum_re[:].rearrange("p g (j k) -> p g j k", k=L)
    cim4 = cum_im[:].rearrange("p g (j k) -> p g j k", k=L)
    p.op("vector", "tensor_copy", sA_re[:], cre4[:, :, :, L - 1])
    p.op("vector", "tensor_copy", sA_im[:], cim4[:, :, :, L - 1])
    for g in range(NG):
        p.op("vector", "tensor_copy", l63r_all[:, g:g + 1], Tp_re(g)[:, L - 1:L])
        p.op("vector", "tensor_copy", l63i_all[:, g:g + 1], Tp_im(g)[:, L - 1:L])
    for g in range(NG):
        p.op("vector", "tensor_scalar", t32a[:, g, :], sA_re[:, g, :], l63r_all[:, g:g + 1], None, ALU.mult)
        p.op("vector", "tensor_scalar", t32b[:, g, :], sA_im[:, g, :], l63i_all[:, g:g + 1], None, ALU.mult)
        p.op("vector", "tensor_scalar", sB_re[:, g, :], sA_re[:, g, :], l63i_all[:, g:g + 1], None, ALU.mult)
        p.op("vector", "tensor_scalar", sB_im[:, g, :], sA_im[:, g, :], l63r_all[:, g:g + 1], None, ALU.mult)
    p.op("vector", "tensor_tensor", sA_re[:], t32a[:], t32b[:], ALU.subtract)
    p.op("vector", "tensor_tensor", sA_im[:], sB_re[:], sB_im[:], ALU.add)

    def cplx_mul(ar, ai, br, bi, outr, outi, tmp1, tmp2):
        p.op("vector", "tensor_tensor", tmp1, ar, br, ALU.mult)
        p.op("vector", "tensor_tensor", tmp2, ai, bi, ALU.mult)
        p.op("vector", "tensor_tensor", outr, tmp1, tmp2, ALU.subtract)
        p.op("vector", "tensor_tensor", tmp1, ar, bi, ALU.mult)
        p.op("vector", "tensor_tensor", tmp2, ai, br, ALU.mult)
        p.op("vector", "tensor_tensor", outi, tmp1, tmp2, ALU.add)

    cplx_mul(l63r_all[:], l63i_all[:], slot(LR), slot(LI), slot(LAr), slot(LAi), slot(TMP1), slot(TMP2))
    p.op("vector", "tensor_copy", hsr[:, :, 0], slot(LAr))
    p.op("vector", "tensor_copy", hsi[:, :, 0], slot(LAi))
    for i in range(1, 5):
        cplx_mul(hsr[:, :, i - 1], hsi[:, :, i - 1], hsr[:, :, i - 1], hsi[:, :, i - 1],
                 hsr[:, :, i], hsi[:, :, i], slot(TMP1), slot(TMP2))
    cur_r, cur_i, oth_r, oth_i = sA_re, sA_im, sB_re, sB_im
    for pw in range(5):
        d = 1 << pw
        mr = hsr[:, :, pw:pw + 1].broadcast_to([128, NG, CJ - d])
        mi = hsi[:, :, pw:pw + 1].broadcast_to([128, NG, CJ - d])
        p.op("vector", "tensor_copy", oth_r[:, :, 0:d], cur_r[:, :, 0:d])
        p.op("vector", "tensor_copy", oth_i[:, :, 0:d], cur_i[:, :, 0:d])
        p.op("vector", "tensor_tensor", t32a[:, :, 0:CJ - d], cur_r[:, :, 0:CJ - d], mr, ALU.mult)
        p.op("vector", "tensor_tensor", t32b[:, :, 0:CJ - d], cur_i[:, :, 0:CJ - d], mi, ALU.mult)
        p.op("vector", "tensor_tensor", t32a[:, :, 0:CJ - d], t32a[:, :, 0:CJ - d], t32b[:, :, 0:CJ - d], ALU.subtract)
        p.op("vector", "tensor_tensor", oth_r[:, :, d:], cur_r[:, :, d:], t32a[:, :, 0:CJ - d], ALU.add)
        p.op("vector", "tensor_tensor", t32a[:, :, 0:CJ - d], cur_r[:, :, 0:CJ - d], mi, ALU.mult)
        p.op("vector", "tensor_tensor", t32b[:, :, 0:CJ - d], cur_i[:, :, 0:CJ - d], mr, ALU.mult)
        p.op("vector", "tensor_tensor", t32a[:, :, 0:CJ - d], t32a[:, :, 0:CJ - d], t32b[:, :, 0:CJ - d], ALU.add)
        p.op("vector", "tensor_tensor", oth_i[:, :, d:], cur_i[:, :, d:], t32a[:, :, 0:CJ - d], ALU.add)
        cur_r, cur_i, oth_r, oth_i = oth_r, oth_i, cur_r, cur_i
    p.op("vector", "tensor_copy", bsum_t[:, 0:4], cur_r[:, :, CJ - 1])
    p.op("vector", "tensor_copy", bsum_t[:, 4:8], cur_i[:, :, CJ - 1])
    for g in range(NG):
        p.op("vector", "tensor_scalar", oth_r[:, g, :], cur_r[:, g, :], slot(LR)[:, g:g + 1], None, ALU.mult)
        p.op("vector", "tensor_scalar", oth_i[:, g, :], cur_i[:, g, :], slot(LR)[:, g:g + 1], None, ALU.mult)
        p.op("vector", "tensor_scalar", t32a[:, g, :], cur_i[:, g, :], slot(LI)[:, g:g + 1], None, ALU.mult)
        p.op("vector", "tensor_scalar", t32b[:, g, :], cur_r[:, g, :], slot(LI)[:, g:g + 1], None, ALU.mult)
    p.op("vector", "tensor_tensor", t32a[:], oth_r[:], t32a[:], ALU.subtract)
    p.op("vector", "tensor_tensor", t32b[:], t32b[:], oth_i[:], ALU.add)
    cre4w = cum_re[:].rearrange("p g (j k) -> p g j k", k=L)
    cim4w = cum_im[:].rearrange("p g (j k) -> p g j k", k=L)
    p.op("vector", "tensor_tensor", cre4w[:, :, 1:, :], cre4w[:, :, 1:, :],
         t32a[:, :, 0:CJ - 1].unsqueeze(3).broadcast_to([128, NG, CJ - 1, L]), ALU.add)
    p.op("vector", "tensor_tensor", cim4w[:, :, 1:, :], cim4w[:, :, 1:, :],
         t32b[:, :, 0:CJ - 1].unsqueeze(3).broadcast_to([128, NG, CJ - 1, L]), ALU.add, sem=cm)
    DONE = cm.v

    p.wait("sync", cm, DONE)
    p.dma("sync", o_cre, cre_f, st)
    p.dma("sync", o_cim, cim_f, st)
    p.dma("sync", o_bsum, bsum_t[:], st)
    p.wait("sync", st)

    es.close()



def emit_B(p, nc, dram):
    es = contextlib.ExitStack()

    xT = dram("xT", [D, T], F32, "in")
    i_cre = dram("i_cre", [128, NG * T], F32, "in")
    i_cim = dram("i_cim", [128, NG * T], F32, "in")
    ball = dram("ball", [NCORES * 128, 8], F32, "in")
    selmask = dram("selmask", [128, 8], F32, "in")
    lamp = dram("lamp", [128, 12], F32, "in")
    cvec = dram("cvec", [128, 8], F32, "in")
    CreT = dram("CreT", [N_SC, D], F32, "in")
    CimT = dram("CimT", [N_SC, D], F32, "in")
    DT = dram("DT", [D, D], F32, "in")
    o_x1 = dram("o_x1", [D, T], F32, "out")

    sb = lambda name, shape, dt=F32: es.enter_context(nc.sbuf_tensor(name, shape, dt, allow_name_mangling=True))
    ps = lambda name, shape, dt=F32: es.enter_context(nc.psum_tensor(name, shape, dt, allow_name_mangling=True))

    lam_t = sb("lam_t", [128, 12])
    cvec_t = sb("cvec_t", [128, 8])
    c4 = sb("c4", [128, 4 * 16])
    tp_t = sb("tp_t", [128, NG, L])
    tp_i = sb("tp_i", [128, NG, L])
    lj_r = sb("lj_r", [128, NG, CJ])
    lj_i = sb("lj_i", [128, NG, CJ])
    qpr = sb("qpr", [128, NG, 6])
    qpi = sb("qpi", [128, NG, 6])
    ball_t = sb("ball_t", [128, 64])
    mask_t = sb("mask_t", [128, 8])
    sgr = sb("sgr", [128, NG])
    sgi = sb("sgi", [128, NG])
    ggr = sb("ggr", [128, NG])
    ggi = sb("ggi", [128, NG])
    c1r = sb("c1r", [128, NG, CJ])
    c1i = sb("c1i", [128, NG, CJ])
    x_bf = sb("x_bf", [128, CH, T], BF16)
    Ct_re = sb("Ct_re", [128, NG, D], BF16)
    Ct_imn = sb("Ct_imn", [128, NG, D], BF16)
    Dt_bf = sb("Dt_bf", [128, CH, D], BF16)
    s_re_bf = sb("s_re_bf", [128, NG, T], BF16)
    s_im_bf = sb("s_im_bf", [128, NG, T], BF16)
    evA = sb("evA", [128, T])
    evB = sb("evB", [128, T])
    ones_colb = sb("ones_colb", [128, 1], BF16)
    ones_rowf = sb("ones_rowf", [1, 128])
    row_sc = sb("row_sc", [1, 8 * SUB])

    es1 = contextlib.ExitStack()
    cum_re = es1.enter_context(nc.sbuf_tensor("cum_re_s", [128, NG, T], F32, allow_name_mangling=True))
    cum_im = es1.enter_context(nc.sbuf_tensor("cum_im_s", [128, NG, T], F32, allow_name_mangling=True))

    ps_y = [ps("ps_y0", [128, SUB]), ps("ps_y1", [128, SUB])]
    ps_sum = ps("ps_sum", [1, SUB])
    ps_sq = ps("ps_sq", [1, SUB])
    ps_a = ps("ps_a", [128, SUB])
    ps_b = ps("ps_b", [128, SUB])

    ld = p.sem("ld")
    ldg = p.sem("ldg")
    ldx = p.sem("ldx")
    cm = p.sem("cm")
    mm = p.sem("mm")
    sc = p.sem("sc")
    st = p.sem("st")

    p.dma("sync", lam_t[:], lamp, ld)
    p.dma("sync", cvec_t[:], cvec, ld)
    p.dma("sync", ball_t[:].rearrange("p (j e) -> p j e", e=8), ball.rearrange("(j p) e -> p j e", p=128), ld)
    p.dma("sync", mask_t[:], selmask, ld)
    ld_small = ld.v
    ldc = p.sem("ldc")
    p.dma("sync", cum_re[:].rearrange("p g t -> p (g t)"), i_cre, ldc)
    p.dma("sync", cum_im[:].rearrange("p g t -> p (g t)"), i_cim, ldc)
    ld_cum = ldc.v
    p.dma("gpsimd", x_bf[:], xT.rearrange("(c p) t -> p c t", p=128), ldg)
    p.dma("gpsimd", Ct_re[:], CreT.rearrange("(g p) o -> p g o", p=128), ldg)
    p.dma("gpsimd", Ct_imn[:], CimT.rearrange("(g p) o -> p g o", p=128), ldg)
    p.dma("gpsimd", Dt_bf[:], DT.rearrange("(c p) o -> p c o", p=128), ldg)
    ld_big = ldg.v

    def slot(i):
        return c4[:, 4 * i:4 * i + 4]
    NU, TH, GA, ENU, RHO, THR, SINT, COST, LR, LI, LTr, LTi, TMP1, TMP2, TMP3, TMP4 = range(16)

    p.wait("vector", ldg, ld_big)
    p.op("vector", "tensor_scalar", Ct_imn[:].rearrange("p g o -> p (g o)"),
         Ct_imn[:].rearrange("p g o -> p (g o)"), -1.0, None, ALU.mult)

    p.wait("scalar", ld, ld_small)
    p.op("scalar", "activation", slot(ENU), lamp_slice(lam_t, 0), AF.Exp)
    p.op("scalar", "activation", slot(RHO), slot(ENU), AF.Exp, scale=-1.0)
    p.op("scalar", "activation", slot(TH), lamp_slice(lam_t, 1), AF.Exp, sem=sc)
    p.wait("vector", sc)
    p.op("vector", "tensor_scalar", slot(TMP1), slot(TH), -PI, None, ALU.add)
    p.op("vector", "tensor_scalar", slot(TMP1), slot(TMP1), 1e9, None, ALU.mult)
    p.op("vector", "tensor_scalar", slot(TMP1), slot(TMP1), 0.0, None, ALU.max)
    p.op("vector", "tensor_scalar", slot(TMP1), slot(TMP1), 1.0, None, ALU.min)
    p.op("vector", "scalar_tensor_tensor", slot(THR), slot(TMP1), -TWO_PI, slot(TH), ALU.mult, ALU.add, sem=cm)
    p.wait("scalar", cm)
    p.op("scalar", "activation", slot(SINT), slot(THR), AF.Sin, sem=sc)
    p.wait("vector", sc)
    p.op("vector", "tensor_scalar", slot(TMP1), slot(THR), -1.0, None, ALU.mult)
    p.op("vector", "tensor_tensor", slot(TMP1), slot(TMP1), slot(THR), ALU.max, sem=cm)
    p.wait("scalar", cm)
    p.op("scalar", "activation", slot(COST), slot(TMP1), AF.Sin, scale=-1.0, bias=cvec_t[:, 0:1], sem=sc)
    p.wait("vector", sc)
    p.op("vector", "tensor_tensor", slot(LR), slot(RHO), slot(COST), ALU.mult)
    p.op("vector", "tensor_tensor", slot(LI), slot(RHO), slot(SINT), ALU.mult)

    def cplx_mul(ar, ai, br, bi, outr, outi, tmp1, tmp2):
        p.op("vector", "tensor_tensor", tmp1, ar, br, ALU.mult)
        p.op("vector", "tensor_tensor", tmp2, ai, bi, ALU.mult)
        p.op("vector", "tensor_tensor", outr, tmp1, tmp2, ALU.subtract)
        p.op("vector", "tensor_tensor", tmp1, ar, bi, ALU.mult)
        p.op("vector", "tensor_tensor", tmp2, ai, br, ALU.mult)
        p.op("vector", "tensor_tensor", outi, tmp1, tmp2, ALU.add)

    def build_pow_table(tr, ti, base_r, base_i, n, nlev):
        p.op("vector", "tensor_copy", qpr[:, :, 0], base_r)
        p.op("vector", "tensor_copy", qpi[:, :, 0], base_i)
        for m in range(1, nlev):
            cplx_mul(qpr[:, :, m - 1], qpi[:, :, m - 1], qpr[:, :, m - 1], qpi[:, :, m - 1],
                     qpr[:, :, m], qpi[:, :, m], slot(TMP1), slot(TMP2))
        p.op("vector", "memset", tr[:, :, 0:1], 1.0)
        p.op("vector", "memset", ti[:, :, 0:1], 0.0)
        for m in range(nlev):
            d = 1 << m
            if d >= n:
                break
            w = min(d, n - d)
            for g in range(NG):
                t1 = evA[:, 0:w]
                t2 = evB[:, 0:w]
                p.op("vector", "tensor_scalar", t1, tr[:, g, 0:w], qpr[:, g, m:m + 1], None, ALU.mult)
                p.op("vector", "tensor_scalar", t2, ti[:, g, 0:w], qpi[:, g, m:m + 1], None, ALU.mult)
                p.op("vector", "tensor_tensor", tr[:, g, d:d + w], t1, t2, ALU.subtract)
                p.op("vector", "tensor_scalar", t1, tr[:, g, 0:w], qpi[:, g, m:m + 1], None, ALU.mult)
                p.op("vector", "tensor_scalar", t2, ti[:, g, 0:w], qpr[:, g, m:m + 1], None, ALU.mult)
                p.op("vector", "tensor_tensor", ti[:, g, d:d + w], t1, t2, ALU.add)

    build_pow_table(tp_t, tp_i, slot(LR), slot(LI), L, 6)
    cplx_mul(tp_t[:, :, L - 1], tp_i[:, :, L - 1], slot(LR), slot(LI), slot(LTr), slot(LTi), slot(TMP1), slot(TMP2))
    build_pow_table(lj_r, lj_i, slot(LTr), slot(LTi), CJ, 5)
    LTTr, LTTi = TMP3, TMP4
    cplx_mul(lj_r[:, :, 16], lj_i[:, :, 16], lj_r[:, :, 16], lj_i[:, :, 16], slot(LTTr), slot(LTTi), slot(TMP1), slot(TMP2))

    p.op("vector", "memset", sgr[:], 0.0)
    p.op("vector", "memset", sgi[:], 0.0)
    p.op("vector", "memset", ggr[:], 0.0)
    p.op("vector", "memset", ggi[:], 0.0)
    for j in range(NCORES):
        br = ball_t[:, 8 * j:8 * j + 4]
        bi = ball_t[:, 8 * j + 4:8 * j + 8]
        p.op("vector", "tensor_tensor", slot(TMP1), sgr[:], slot(LTTr), ALU.mult)
        p.op("vector", "tensor_tensor", slot(TMP2), sgi[:], slot(LTTi), ALU.mult)
        p.op("vector", "tensor_tensor", slot(TMP1), slot(TMP1), slot(TMP2), ALU.subtract)
        p.op("vector", "tensor_tensor", slot(TMP2), sgr[:], slot(LTTi), ALU.mult)
        p.op("vector", "tensor_tensor", sgi[:], sgi[:], slot(LTTr), ALU.mult)
        p.op("vector", "tensor_tensor", sgi[:], sgi[:], slot(TMP2), ALU.add)
        p.op("vector", "tensor_tensor", sgi[:], sgi[:], bi, ALU.add)
        p.op("vector", "tensor_tensor", sgr[:], slot(TMP1), br, ALU.add)
        p.op("vector", "tensor_scalar", slot(TMP1), sgr[:], mask_t[:, j:j + 1], None, ALU.mult)
        p.op("vector", "tensor_tensor", ggr[:], ggr[:], slot(TMP1), ALU.add)
        p.op("vector", "tensor_scalar", slot(TMP1), sgi[:], mask_t[:, j:j + 1], None, ALU.mult)
        p.op("vector", "tensor_tensor", ggi[:], ggi[:], slot(TMP1), ALU.add)
    glr, gli = sgr, sgi
    cplx_mul(ggr[:], ggi[:], slot(LR), slot(LI), glr[:], gli[:], slot(TMP1), slot(TMP2))
    for g in range(NG):
        p.op("vector", "tensor_scalar", c1r[:, g, :], lj_r[:, g, :], glr[:, g:g + 1], None, ALU.mult)
        p.op("vector", "tensor_scalar", evA[:, 0:CJ], lj_i[:, g, :], gli[:, g:g + 1], None, ALU.mult)
        p.op("vector", "tensor_tensor", c1r[:, g, :], c1r[:, g, :], evA[:, 0:CJ], ALU.subtract)
        p.op("vector", "tensor_scalar", c1i[:, g, :], lj_r[:, g, :], gli[:, g:g + 1], None, ALU.mult)
        p.op("vector", "tensor_scalar", evA[:, 0:CJ], lj_i[:, g, :], glr[:, g:g + 1], None, ALU.mult)
        p.op("vector", "tensor_tensor", c1i[:, g, :], c1i[:, g, :], evA[:, 0:CJ], ALU.add)

    p.wait("vector", ldc, ld_cum)
    cre4 = cum_re[:].rearrange("p g (j k) -> p g j k", k=L)
    cim4 = cum_im[:].rearrange("p g (j k) -> p g j k", k=L)
    p.op("vector", "tensor_tensor", cre4, cre4, c1r[:].unsqueeze(3).broadcast_to([128, NG, CJ, L]), ALU.add)
    p.op("vector", "tensor_tensor", cim4, cim4, c1i[:].unsqueeze(3).broadcast_to([128, NG, CJ, L]), ALU.add)
    for g in range(NG):
        cr = cum_re[:, g, :].rearrange("p (j k) -> p j k", k=L)
        ci = cum_im[:, g, :].rearrange("p (j k) -> p j k", k=L)
        tpr = tp_t[:, g, :].unsqueeze(1).broadcast_to([128, CJ, L])
        tpi = tp_i[:, g, :].unsqueeze(1).broadcast_to([128, CJ, L])
        eA = evA[:].rearrange("p (j k) -> p j k", k=L)
        eB = evB[:].rearrange("p (j k) -> p j k", k=L)
        p.op("vector", "tensor_tensor", eA, cr, tpr, ALU.mult)
        p.op("vector", "tensor_tensor", eB, ci, tpi, ALU.mult)
        p.op("vector", "tensor_tensor", s_re_bf[:, g, :].rearrange("p (j k) -> p j k", k=L), eA, eB, ALU.subtract)
        p.op("vector", "tensor_tensor", eA, cr, tpi, ALU.mult)
        p.op("vector", "tensor_tensor", eB, ci, tpr, ALU.mult)
        p.op("vector", "tensor_tensor", s_im_bf[:, g, :].rearrange("p (j k) -> p j k", k=L), eA, eB,
             ALU.add, sem=(cm if g == NG - 1 else None))
    STATES_DONE = cm.v
    es1.close()
    s_sub = sb("s_sub", [128, CH, SUB])
    x_sub = sb("x_sub", [128, CH, SUB])
    sbfc = sb("sbfc", [128, CH, SUB], BF16)
    sqbf = sb("sqbf", [128, CH, SUB], BF16)
    x1_t = sb("x1_t", [128, CH, SUB])

    p.op("vector", "memset", ones_colb[:], 1.0)
    p.op("vector", "memset", ones_rowf[:], 1.0, sem=cm)
    CONSTS_DONE = cm.v

    p.wait("tensor", ldg, ld_big)
    p.wait("tensor", cm, STATES_DONE)
    evac_done = [0] * NSUB
    stats_read = [0] * NSUB
    apply_done = [0] * NSUB
    sq_done = [0] * NSUB
    store_done = [0] * NSUB
    evacv = []
    for s in range(NSUB):
        p.wait("sync", cm, evac_done[s - 1] if s >= 1 else STATES_DONE)
        p.dma("sync", x_sub[:], xT.rearrange("(c p) t -> p c t", p=128)[:, :, s * SUB:(s + 1) * SUB], ldx)
        ldx_s = ldx.v
        for oc in range(CH):
            k = s * CH + oc
            pyt = ps_y[k % 2]
            if k >= 2:
                p.wait("tensor", cm, evacv[k - 2])
            first = True
            for g in range(NG):
                p.op("tensor", "matmul", pyt[:], Ct_re[:, g, oc * 128:(oc + 1) * 128],
                     s_re_bf[:, g, s * SUB:(s + 1) * SUB], start=first, stop=False)
                first = False
            for g in range(NG):
                p.op("tensor", "matmul", pyt[:], Ct_imn[:, g, oc * 128:(oc + 1) * 128],
                     s_im_bf[:, g, s * SUB:(s + 1) * SUB], start=False, stop=False)
            for ic in range(CH):
                p.op("tensor", "matmul", pyt[:], Dt_bf[:, ic, oc * 128:(oc + 1) * 128],
                     x_bf[:, ic, s * SUB:(s + 1) * SUB], start=False, stop=(ic == CH - 1),
                     sem=(mm if ic == CH - 1 else None))
            mm_oc = mm.v
            p.wait("vector", mm, mm_oc)
            if oc == 0:
                p.wait("vector", ldx, ldx_s)
            p.op("vector", "tensor_tensor", s_sub[:, oc, :], pyt[:], x_sub[:, oc, :], ALU.add, sem=cm)
            evacv.append(cm.v)
            p.op("vector", "tensor_copy", sbfc[:, oc, :], s_sub[:, oc, :])
        evac_done[s] = evacv[-1]
        p.wait("scalar", cm, evac_done[s])
        for oc in range(CH):
            p.op("scalar", "activation", sqbf[:, oc, :], s_sub[:, oc, :], AF.Square,
                 sem=(sc if oc == CH - 1 else None))
        sq_done[s] = sc.v
        p.wait("tensor", sc, sq_done[s])
        p.wait_last("tensor", "vector")
        if s >= 1:
            p.wait("tensor", cm, stats_read[s - 1])
        for oc in range(CH):
            p.op("tensor", "matmul", ps_sum[:], ones_colb[:], sbfc[:, oc, :], start=(oc == 0), stop=(oc == CH - 1))
        for oc in range(CH):
            p.op("tensor", "matmul", ps_sq[:], ones_colb[:], sqbf[:, oc, :], start=(oc == 0), stop=(oc == CH - 1),
                 sem=(mm if oc == CH - 1 else None))
        MEAN = row_sc[0:1, 0:SUB]
        EX2 = row_sc[0:1, SUB:2 * SUB]
        VAR = row_sc[0:1, 2 * SUB:3 * SUB]
        RSTD = row_sc[0:1, 3 * SUB:4 * SUB]
        BB = row_sc[0:1, 4 * SUB:5 * SUB]
        p.wait("vector", mm)
        p.op("vector", "tensor_scalar", MEAN, ps_sum[:], 1.0 / D, None, ALU.mult)
        p.op("vector", "tensor_scalar", EX2, ps_sq[:], 1.0 / D, None, ALU.mult)
        p.op("vector", "tensor_tensor", VAR, MEAN, MEAN, ALU.mult)
        p.op("vector", "tensor_tensor", VAR, EX2, VAR, ALU.subtract, sem=cm)
        stats_read[s] = cm.v
        p.wait("scalar", cm, stats_read[s])
        p.op("scalar", "activation", RSTD, VAR, AF.Ln, bias=cvec_t[0:1, 1:2])
        p.op("scalar", "activation", RSTD, RSTD, AF.Exp, scale=-0.5, sem=sc)
        p.wait("vector", sc)
        p.op("vector", "scalar_tensor_tensor", BB, MEAN, -1.0, RSTD, ALU.mult, ALU.mult, sem=cm)
        bb_done = cm.v
        p.wait("tensor", cm, bb_done)
        if s >= 1:
            p.wait("tensor", cm, apply_done[s - 1])
        p.op("tensor", "matmul", ps_a[:], ones_rowf[:], RSTD, start=True, stop=True)
        p.op("tensor", "matmul", ps_b[:], ones_rowf[:], BB, start=True, stop=True, sem=mm)
        p.wait("vector", mm)
        if s >= 1:
            p.wait("vector", st, store_done[s - 1])
        for oc in range(CH):
            p.op("vector", "tensor_tensor", x1_t[:, oc, :], s_sub[:, oc, :], ps_a[:], ALU.mult)
            p.op("vector", "tensor_tensor", x1_t[:, oc, :], x1_t[:, oc, :], ps_b[:], ALU.add,
                 sem=(cm if oc == CH - 1 else None))
        apply_done[s] = cm.v
        p.wait("sync", cm, apply_done[s])
        p.dma("sync", o_x1.rearrange("(c p) t -> p c t", p=128)[:, :, s * SUB:(s + 1) * SUB], x1_t[:], st)
        store_done[s] = st.v
    p.wait("sync", st)

    es.close()


def host_inputs_B(inputs, core, a_results):
    x = np.asarray(inputs["L0_attn_x"])[0]
    xT = np.ascontiguousarray(np.asarray(x).T)
    shard = np.ascontiguousarray(xT[:, core * T:(core + 1) * T])
    lamp = np.concatenate([
        np.asarray(inputs["L0_attn_nu_log"]).reshape(NG, 128).T,
        np.asarray(inputs["L0_attn_theta_log"]).reshape(NG, 128).T,
        np.asarray(inputs["L0_attn_gamma_log"]).reshape(NG, 128).T,
    ], axis=1).astype(np.float32)
    cv = np.zeros((128, 8), dtype=np.float32)
    cv[:, 0] = PI / 2
    cv[:, 1] = 1e-5
    cv[:, 2] = 1e-6
    ball = np.concatenate([a_results[j]["o_bsum"] for j in range(NCORES)], axis=0).astype(np.float32)
    selmask = np.zeros((128, 8), dtype=np.float32)
    if core > 0:
        selmask[:, core - 1] = 1.0
    return {
        "xT": shard,
        "i_cre": np.ascontiguousarray(a_results[core]["o_cre"]),
        "i_cim": np.ascontiguousarray(a_results[core]["o_cim"]),
        "ball": np.ascontiguousarray(ball),
        "selmask": selmask,
        "lamp": np.ascontiguousarray(lamp),
        "cvec": cv,
        "CreT": np.ascontiguousarray(np.asarray(inputs["L0_attn_C_re"]).T),
        "CimT": np.ascontiguousarray(np.asarray(inputs["L0_attn_C_im"]).T),
        "DT": np.ascontiguousarray(np.asarray(inputs["L0_attn_D"]).T),
    }



class LNKit:
    def __init__(self, p, sb, ps, cvec_t):
        self.p = p
        self.cvec_t = cvec_t
        self.ones_colb = sb("ln_ones_colb", [128, 1], BF16)
        self.ones_rowf = sb("ln_ones_rowf", [1, 128])
        self.rows = [sb(f"ln_row{i}", [1, SUB]) for i in range(5)]
        self.ps_sum = ps("ln_ps_sum", [1, SUB])
        self.ps_sq = ps("ln_ps_sq", [1, SUB])
        self.ps_a = ps("ln_ps_a", [128, SUB])
        self.ps_b = ps("ln_ps_b", [128, SUB])
        self.prev_stats_read = None
        self.prev_apply_done = None
        self.inited = False

    def init_consts(self, cm):
        p = self.p
        p.op("vector", "memset", self.ones_colb[:], 1.0)
        p.op("vector", "memset", self.ones_rowf[:], 1.0, sem=cm)
        self.inited = True

    def emit(self, s_sub, sbfc, sqbf, out_t, cm, mm, sc, evac_done_v, out_guard=None):
        p = self.p
        p.wait("scalar", cm, evac_done_v)
        for oc in range(CH):
            p.op("scalar", "activation", sqbf[:, oc, :], s_sub[:, oc, :], AF.Square,
                 sem=(sc if oc == CH - 1 else None))
        sq_v = sc.v
        p.wait("tensor", sc, sq_v)
        p.wait_last("tensor", "vector")
        if self.prev_stats_read is not None:
            p.wait("tensor", cm, self.prev_stats_read)
        for oc in range(CH):
            p.op("tensor", "matmul", self.ps_sum[:], self.ones_colb[:], sbfc[:, oc, :],
                 start=(oc == 0), stop=(oc == CH - 1))
        for oc in range(CH):
            p.op("tensor", "matmul", self.ps_sq[:], self.ones_colb[:], sqbf[:, oc, :],
                 start=(oc == 0), stop=(oc == CH - 1), sem=(mm if oc == CH - 1 else None))
        MEAN = self.rows[0][:]
        EX2 = self.rows[1][:]
        VAR = self.rows[2][:]
        RSTD = self.rows[3][:]
        BB = self.rows[4][:]
        p.wait("vector", mm)
        p.op("vector", "tensor_scalar", MEAN, self.ps_sum[:], 1.0 / D, None, ALU.mult)
        p.op("vector", "tensor_scalar", EX2, self.ps_sq[:], 1.0 / D, None, ALU.mult)
        p.op("vector", "tensor_tensor", VAR, MEAN, MEAN, ALU.mult)
        p.op("vector", "tensor_tensor", VAR, EX2, VAR, ALU.subtract, sem=cm)
        self.prev_stats_read = cm.v
        p.wait("scalar", cm, self.prev_stats_read)
        p.op("scalar", "activation", RSTD, VAR, AF.Ln, bias=self.cvec_t[0:1, 1:2])
        p.op("scalar", "activation", RSTD, RSTD, AF.Exp, scale=-0.5, sem=sc)
        p.wait("vector", sc)
        p.op("vector", "scalar_tensor_tensor", BB, MEAN, -1.0, RSTD, ALU.mult, ALU.mult, sem=cm)
        bb_v = cm.v
        p.wait("tensor", cm, bb_v)
        if self.prev_apply_done is not None:
            p.wait("tensor", cm, self.prev_apply_done)
        p.op("tensor", "matmul", self.ps_a[:], self.ones_rowf[:], RSTD, start=True, stop=True)
        p.op("tensor", "matmul", self.ps_b[:], self.ones_rowf[:], BB, start=True, stop=True, sem=mm)
        p.wait("vector", mm)
        if out_guard is not None:
            p.wait("vector", out_guard[0], out_guard[1])
        for oc in range(CH):
            p.op("vector", "tensor_tensor", out_t[:, oc, :], s_sub[:, oc, :], self.ps_a[:], ALU.mult)
            p.op("vector", "tensor_tensor", out_t[:, oc, :], out_t[:, oc, :], self.ps_b[:], ALU.add,
                 sem=(cm if oc == CH - 1 else None))
        self.prev_apply_done = cm.v
        return cm.v


def emit_C(p, nc, dram):
    es = contextlib.ExitStack()

    x1T = dram("x1T", [D, T], F32, "in")
    wN = {}
    for n in ("wfT", "wiT", "wcT", "woT", "uoT", "ufT", "uiT", "ucT"):
        wN[n] = nc.dram_tensor(n, [D, D], F32, kind="ExternalInput").ap()
    bias_fm = dram("bias_fm", [128, 4 * CH], F32, "in")
    inc_fm = dram("inc_fm", [128, CH], F32, "in")
    cvec = dram("cvec", [128, 8], F32, "in")
    o_x2 = dram("o_x2", [D, T], F32, "out")

    sb = lambda name, shape, dt=F32: es.enter_context(nc.sbuf_tensor(name, shape, dt, allow_name_mangling=True))
    ps = lambda name, shape, dt=F32: es.enter_context(nc.psum_tensor(name, shape, dt, allow_name_mangling=True))

    cvec_t = sb("cvec_t", [128, 8])
    bias_t = sb("bias_t", [128, 4 * CH])
    inc_t = sb("inc_t", [128, CH])
    inc_bf = sb("inc_bf", [128, CH], BF16)
    biasu = sb("biasu", [128, 3 * CH])
    w_g = {n: sb(n + "_s", [128, CH, D], BF16) for n in ("wfT", "wiT", "wcT", "woT", "uoT")}
    x1bf = sb("x1bf", [128, CH, SUB], BF16)
    x1f = sb("x1f", [128, CH, SUB])
    fg = sb("fg", [128, SUB])
    ig = sb("ig", [128, SUB])
    gg = sb("gg", [128, SUB])
    c_t = sb("c_t", [128, CH, SUB])
    c_bf = sb("c_bf", [128, CH, SUB], BF16)
    s_sub = sb("s_sub", [128, CH, SUB])
    sbfc = sb("sbfc", [128, CH, SUB], BF16)
    sqbf = sb("sqbf", [128, CH, SUB], BF16)

    ln = LNKit(p, sb, ps, cvec_t)

    es1 = contextlib.ExitStack()
    w_u = es1.enter_context(nc.sbuf_tensor("w_u", [128, CH, D], BF16, allow_name_mangling=True))
    ps_g = [ps("ps_g0", [128, SUB]), ps("ps_g1", [128, SUB])]
    ps_mv = ps("ps_mv", [128, 1])

    ld = p.sem("ld")
    ldg = p.sem("ldg")
    ldx = p.sem("ldx")
    cm = p.sem("cm")
    mm = p.sem("mm")
    sc = p.sem("sc")
    st = p.sem("st")

    p.dma("sync", cvec_t[:], cvec, ld)
    p.dma("sync", bias_t[:], bias_fm, ld)
    p.dma("sync", inc_t[:], inc_fm, ld)
    ld_small = ld.v

    ln.init_consts(cm)
    p.wait("vector", ld, ld_small)
    p.op("vector", "tensor_copy", inc_bf[:], inc_t[:], sem=cm)
    INCBF = cm.v

    p.wait("tensor", cm, INCBF)
    for ui, n in enumerate(("ufT", "uiT", "ucT")):
        if ui > 0:
            p.wait("gpsimd", mm, mm.v)
        p.dma("gpsimd", w_u[:], wN[n].rearrange("(c p) o -> p c o", p=128), ldg)
        p.wait("tensor", ldg, ldg.v)
        for oc in range(CH):
            for ic in range(CH):
                p.op("tensor", "matmul", ps_mv[:], w_u[:, ic, oc * 128:(oc + 1) * 128],
                     inc_bf[:, ic:ic + 1], start=(ic == 0), stop=(ic == CH - 1),
                     sem=(mm if ic == CH - 1 else None))
            p.wait("vector", mm)
            p.op("vector", "tensor_copy", biasu[:, ui * CH + oc:ui * CH + oc + 1], ps_mv[:], sem=cm)
            p.wait("tensor", cm)
    for n in ("wfT", "wiT", "wcT", "woT", "uoT"):
        p.dma("gpsimd", w_g[n][:], wN[n].rearrange("(c p) o -> p c o", p=128), ldg)
    ld_w = ldg.v
    es1.close()
    x2_t = sb("x2_t", [128, CH, SUB])
    p.op("vector", "tensor_tensor", biasu[:], biasu[:], bias_t[:, 0:3 * CH], ALU.add, sem=cm)
    BIASES = cm.v
    p.wait("scalar", cm, BIASES)

    gate_specs = [("wfT", AF.Sigmoid, fg, 0), ("wiT", AF.Sigmoid, ig, 1), ("wcT", AF.Tanh, gg, 2)]
    p.wait("tensor", ldg, ld_w)
    evacv = []
    kcnt = 0
    prev_g = {}
    last_c_done = None
    for s in range(NSUB):
        if s >= 1:
            p.wait("gpsimd", cm, evacv[-1])
            p.wait("sync", cm, evacv[-1])
        p.dma("gpsimd", x1bf[:], x1T.rearrange("(c p) t -> p c t", p=128)[:, :, s * SUB:(s + 1) * SUB], ldg)
        p.dma("sync", x1f[:], x1T.rearrange("(c p) t -> p c t", p=128)[:, :, s * SUB:(s + 1) * SUB], ldx)
        ldg_s, ldx_s = ldg.v, ldx.v
        p.wait("tensor", ldg, ldg_s)
        if s == 0:
            p.wait("tensor", cm, BIASES)
        for oc in range(CH):
            if last_c_done is not None:
                p.wait("scalar", cm, last_c_done)
            for (wn, act, gt, bi) in gate_specs:
                pg = ps_g[kcnt % 2]
                if kcnt >= 2:
                    p.wait("tensor", sc, prev_g[kcnt - 2])
                for ic in range(CH):
                    p.op("tensor", "matmul", pg[:], w_g[wn][:, ic, oc * 128:(oc + 1) * 128],
                         x1bf[:, ic, :], start=(ic == 0), stop=(ic == CH - 1),
                         sem=(mm if ic == CH - 1 else None))
                p.wait("scalar", mm)
                p.op("scalar", "activation", gt[:], pg[:], act,
                     bias=biasu[:, bi * CH + oc:bi * CH + oc + 1], sem=sc)
                prev_g[kcnt] = sc.v
                kcnt += 1
            p.wait("vector", sc)
            p.op("vector", "tensor_tensor", c_t[:, oc, :], ig[:], gg[:], ALU.mult)
            p.op("vector", "scalar_tensor_tensor", c_t[:, oc, :], fg[:], inc_t[:, oc:oc + 1],
                 c_t[:, oc, :], ALU.mult, ALU.add, sem=cm)
            last_c_done = cm.v
        p.op("vector", "tensor_copy", c_bf[:].rearrange("p c t -> p (c t)"),
             c_t[:].rearrange("p c t -> p (c t)"), sem=cm)
        CBF = cm.v
        p.wait("tensor", cm, CBF)
        p.wait("vector", ldx, ldx_s)
        for oc in range(CH):
            pg = ps_g[kcnt % 2]
            if kcnt >= 2:
                p.wait("tensor", sc, prev_g[kcnt - 2])
            for ic in range(CH):
                p.op("tensor", "matmul", pg[:], w_g["woT"][:, ic, oc * 128:(oc + 1) * 128],
                     x1bf[:, ic, :], start=(ic == 0), stop=False)
            for ic in range(CH):
                p.op("tensor", "matmul", pg[:], w_g["uoT"][:, ic, oc * 128:(oc + 1) * 128],
                     c_bf[:, ic, :], start=False, stop=(ic == CH - 1),
                     sem=(mm if ic == CH - 1 else None))
            p.wait("scalar", mm)
            p.op("scalar", "activation", s_sub[:, oc, :], pg[:], AF.Tanh,
                 bias=bias_t[:, 3 * CH + oc:3 * CH + oc + 1], sem=sc)
            prev_g[kcnt] = sc.v
            kcnt += 1
            p.wait("vector", sc)
            p.op("vector", "tensor_tensor", s_sub[:, oc, :], s_sub[:, oc, :], x1f[:, oc, :], ALU.add, sem=cm)
            p.op("vector", "tensor_copy", sbfc[:, oc, :], s_sub[:, oc, :])
        evacv.append(cm.v)
        out_guard = (st, stp) if s >= 1 else None
        ln.emit(s_sub, sbfc, sqbf, x2_t, cm, mm, sc, evacv[-1], out_guard=out_guard)
        p.wait("sync", cm, ln.prev_apply_done)
        p.dma("sync", o_x2.rearrange("(c p) t -> p c t", p=128)[:, :, s * SUB:(s + 1) * SUB], x2_t[:], st)
        stp = st.v
    p.wait("sync", st)
    es.close()


def host_inputs_C(inputs, core, b_results):
    cv = np.zeros((128, 8), dtype=np.float32)
    cv[:, 0] = PI / 2
    cv[:, 1] = 1e-5
    cv[:, 2] = 1e-6

    def fm(v):
        return np.ascontiguousarray(np.asarray(v).reshape(CH, 128).T.astype(np.float32))

    bias_fm = np.concatenate([fm(inputs["L0_ffn_bf"]), fm(inputs["L0_ffn_bi"]),
                              fm(inputs["L0_ffn_bc"]), fm(inputs["L0_ffn_bo"])], axis=1)
    d = {
        "x1T": np.ascontiguousarray(b_results[core]["o_x1"]),
        "bias_fm": np.ascontiguousarray(bias_fm),
        "inc_fm": fm(inputs["L0_ffn_in_c"]),
        "cvec": cv,
    }
    for n, key in (("wfT", "L0_ffn_wf"), ("wiT", "L0_ffn_wi"), ("wcT", "L0_ffn_wc"),
                   ("woT", "L0_ffn_wo"), ("uoT", "L0_ffn_uo"), ("ufT", "L0_ffn_uf"),
                   ("uiT", "L0_ffn_ui"), ("ucT", "L0_ffn_uc")):
        d[n] = np.ascontiguousarray(np.asarray(inputs[key]).T)
    return d



def emit_D(p, nc, dram):
    es = contextlib.ExitStack()

    x2T = dram("x2T", [D, T], F32, "in")
    wqkvT = dram("wqkvT", [D, 3 * D], F32, "in")
    blkb = dram("blkb", [128, CH * 16], BF16, "in")
    bc16f = dram("bc16f", [16, CH * 128], F32, "in")
    o_qfin = dram("o_qfin", [D, T], BF16, "out")
    o_kv = dram("o_kv", [64, 1024], F32, "out")
    o_ks = dram("o_ks", [128, 8], F32, "out")
    o_edge = dram("o_edge", [128, 16], BF16, "out")

    sb = lambda name, shape, dt=F32: es.enter_context(nc.sbuf_tensor(name, shape, dt, allow_name_mangling=True))
    ps = lambda name, shape, dt=F32: es.enter_context(nc.psum_tensor(name, shape, dt, allow_name_mangling=True))

    x2bf = sb("x2bf", [128, CH, T], BF16)
    wbuf = sb("wbuf", [128, CH, D], BF16)
    blk_t = sb("blk_t", [128, CH * 16], BF16)
    bc_t = sb("bc_t", [16, CH * 128])
    q3 = sb("q3", [128, CH, SUB])
    q3bf = sb("q3bf", [128, CH, SUB], BF16)
    qfin = sb("qfin", [128, CH, SUB], BF16)
    rcp_r = sb("rcp_r", [16, SUB])
    kt = sb("kt", [128, D])
    kt2 = sb("kt2", [128, D])
    ks16 = sb("ks16", [128, 16])
    k_tm = sb("k_tm", [128, 16, D], BF16)
    v_tm = sb("v_tm", [128, 16, D], BF16)
    kv_s = sb("kv_s", [64, 1024])
    ks_s = sb("ks_s", [128, 8])
    edge_t = sb("edge_t", [128, CH, 2], BF16)
    ones_b = sb("ones_b", [128, 1], BF16)

    ps_p = [ps("ps_p0", [128, SUB]), ps("ps_p1", [128, SUB])]
    ps_s3 = ps("ps_s3", [16, SUB])
    ps_bc = ps("ps_bc", [128, SUB])
    ps_kv = ps("ps_kv", [64, 1024])
    ps_ks = ps("ps_ks", [128, 8])

    ld = p.sem("ld")
    ldg = p.sem("ldg")
    cm = p.sem("cm")
    mm = p.sem("mm")
    sc = p.sem("sc")
    st = p.sem("st")

    p.dma("sync", blk_t[:], blkb, ld)
    p.dma("sync", bc_t[:], bc16f, ld)
    ld_small = ld.v
    p.dma("gpsimd", x2bf[:], x2T.rearrange("(c p) t -> p c t", p=128), ldg)
    p.dma("gpsimd", wbuf[:], wqkvT[:, 0:D].rearrange("(c p) o -> p c o", p=128), ldg)
    ld_q = ldg.v
    p.op("vector", "memset", ones_b[:], 1.0, sem=cm)
    ONES = cm.v

    p.wait("tensor", ldg, ld_q)
    p.wait("tensor", ld, ld_small)
    EPS = 1e-6
    evacv = []
    kcnt = 0
    prev_use = {}
    stq = []
    for s in range(NSUB):
        if s >= 1:
            p.wait("sync", cm, evacv[-1])
        for oc in range(CH):
            pg = ps_p[kcnt % 2]
            if kcnt >= 2:
                p.wait("tensor", sc, prev_use[kcnt - 2])
            for ic in range(CH):
                p.op("tensor", "matmul", pg[:], wbuf[:, ic, oc * 128:(oc + 1) * 128],
                     x2bf[:, ic, s * SUB:(s + 1) * SUB], start=(ic == 0), stop=(ic == CH - 1),
                     sem=(mm if ic == CH - 1 else None))
            p.wait("scalar", mm)
            if s >= 1 and oc == 0:
                p.wait("scalar", cm, evacv[-1])
            p.op("scalar", "activation", q3[:, oc, :], pg[:], AF.Relu, sem=sc)
            prev_use[kcnt] = sc.v
            kcnt += 1
            p.wait("vector", sc)
            p.op("vector", "tensor_scalar", q3[:, oc, :], q3[:, oc, :], EPS, None, ALU.add)
            p.op("vector", "tensor_tensor", kt[:, 0:SUB], q3[:, oc, :], q3[:, oc, :], ALU.mult)
            p.op("vector", "tensor_tensor", q3[:, oc, :], kt[:, 0:SUB], q3[:, oc, :], ALU.mult)
            p.op("vector", "tensor_copy", q3bf[:, oc, :], q3[:, oc, :], sem=(cm if oc == CH - 1 else None))
        Q3_DONE = cm.v
        p.wait("tensor", cm, Q3_DONE)
        for oc in range(CH):
            p.op("tensor", "matmul", ps_s3[:], blk_t[:, oc * 16:(oc + 1) * 16], q3bf[:, oc, :],
                 start=(oc == 0), stop=(oc == CH - 1), sem=(mm if oc == CH - 1 else None))
        p.wait("vector", mm)
        p.op("vector", "reciprocal", rcp_r[:], ps_s3[:], sem=cm)
        RCP_DONE = cm.v
        p.wait("tensor", cm, RCP_DONE)
        for oc in range(CH):
            p.op("tensor", "matmul", ps_bc[:], bc_t[:, oc * 128:(oc + 1) * 128], rcp_r[:],
                 start=True, stop=True, sem=mm)
            p.wait("vector", mm)
            if s >= 1 and oc == 0:
                p.wait("vector", st, stq[-1])
            p.op("vector", "tensor_tensor", qfin[:, oc, :], q3[:, oc, :], ps_bc[:], ALU.mult, sem=cm)
            if s == 0:
                p.op("vector", "tensor_copy", edge_t[:, oc, 0:1], qfin[:, oc, 0:1])
            if s == NSUB - 1:
                p.op("vector", "tensor_copy", edge_t[:, oc, 1:2], qfin[:, oc, SUB - 1:SUB])
            if oc < CH - 1:
                p.wait("tensor", cm, cm.v)
        p.wait("sync", cm)
        p.dma("sync", o_qfin.rearrange("(c p) t -> p c t", p=128)[:, :, s * SUB:(s + 1) * SUB], qfin[:], st)
        stq.append(st.v)
        evacv.append(cm.v)

    p.wait("gpsimd", mm, mm.v)
    p.dma("gpsimd", wbuf[:], wqkvT[:, D:2 * D].rearrange("(c p) o -> p c o", p=128), ldg)
    ld_k = ldg.v
    p.wait("tensor", ldg, ld_k)
    k_done = {}
    for tile in range(16):
        for half in range(2):
            pg = ps_p[kcnt % 2]
            if kcnt >= 2:
                p.wait("tensor", sc, prev_use[kcnt - 2])
            for ic in range(CH):
                p.op("tensor", "matmul", pg[:], x2bf[:, ic, tile * 128:(tile + 1) * 128],
                     wbuf[:, ic, half * SUB:(half + 1) * SUB], start=(ic == 0), stop=(ic == CH - 1),
                     sem=(mm if ic == CH - 1 else None))
            p.wait("scalar", mm)
            if half == 0:
                p.wait("scalar", cm, k_done[tile - 1] if tile > 0 else evacv[-1])
            p.op("scalar", "activation", kt[:, half * SUB:(half + 1) * SUB], pg[:], AF.Relu, sem=sc)
            prev_use[kcnt] = sc.v
            kcnt += 1
        p.wait("vector", sc)
        p.op("vector", "tensor_scalar", kt[:], kt[:], EPS, None, ALU.add)
        p.op("vector", "tensor_tensor", kt2[:], kt[:], kt[:], ALU.mult)
        p.op("vector", "tensor_tensor", kt[:], kt2[:], kt[:], ALU.mult)
        k3 = kt[:].rearrange("p (h e) -> p h e", e=64)
        p.op("vector", "tensor_reduce", ks16[:], k3, AX.X, ALU.add)
        p.op("vector", "reciprocal", ks16[:], ks16[:])
        p.op("vector", "tensor_tensor", k_tm[:, tile, :].rearrange("p (h e) -> p h e", e=64), k3,
             ks16[:].unsqueeze(2).broadcast_to([128, 16, 64]), ALU.mult, sem=cm)
        k_done[tile] = cm.v

    p.wait("gpsimd", mm, mm.v)
    p.dma("gpsimd", wbuf[:], wqkvT[:, 2 * D:3 * D].rearrange("(c p) o -> p c o", p=128), ldg)
    ld_v = ldg.v
    p.wait("tensor", ldg, ld_v)
    for tile in range(16):
        for half in range(2):
            pg = ps_p[kcnt % 2]
            if kcnt >= 2:
                p.wait("tensor", sc, prev_use[kcnt - 2])
            for ic in range(CH):
                p.op("tensor", "matmul", pg[:], x2bf[:, ic, tile * 128:(tile + 1) * 128],
                     wbuf[:, ic, half * SUB:(half + 1) * SUB], start=(ic == 0), stop=(ic == CH - 1),
                     sem=(mm if ic == CH - 1 else None))
            p.wait("scalar", mm)
            p.op("scalar", "copy", v_tm[:, tile, half * SUB:(half + 1) * SUB], pg[:], sem=sc)
            prev_use[kcnt] = sc.v
            kcnt += 1

    p.wait("tensor", sc)
    p.wait_last("tensor", "vector")
    p.wait("tensor", cm, ONES)
    for h in range(16):
        for tile in range(16):
            p.op("tensor", "matmul", ps_kv[0:64, h * 64:(h + 1) * 64],
                 k_tm[:, tile, h * 64:(h + 1) * 64], v_tm[:, tile, h * 64:(h + 1) * 64],
                 start=(tile == 0), stop=(tile == 15))
    for pr in range(CH):
        for tile in range(16):
            p.op("tensor", "matmul", ps_ks[:, pr:pr + 1],
                 k_tm[:, tile, pr * 128:(pr + 1) * 128], ones_b[:],
                 start=(tile == 0), stop=(tile == 15),
                 sem=(mm if (pr == CH - 1 and tile == 15) else None))
    p.wait("vector", mm)
    p.op("vector", "tensor_copy", kv_s[:], ps_kv[:])
    p.op("vector", "tensor_copy", ks_s[:], ps_ks[:], sem=cm)
    p.wait("sync", cm)
    p.dma("sync", o_kv, kv_s[:], st)
    p.dma("sync", o_ks, ks_s[:], st)
    p.dma("sync", o_edge, edge_t[:].rearrange("p c r -> p (c r)"), st)
    p.wait("sync", st)
    es.close()


def host_inputs_D(inputs, core, c_results):
    import ml_dtypes
    blk = np.zeros((128, CH * 16), dtype=np.float32)
    for oc in range(CH):
        for pp in range(128):
            blk[pp, oc * 16 + 2 * oc + pp // 64] = 1.0
    bc = np.zeros((16, CH * 128), dtype=np.float32)
    for oc in range(CH):
        for m in range(128):
            bc[2 * oc + m // 64, oc * 128 + m] = 1.0
    return {
        "x2T": np.ascontiguousarray(c_results[core]["o_x2"]),
        "wqkvT": np.ascontiguousarray(np.asarray(inputs["L1_attn_w_qkv"]).T),
        "blkb": blk.astype(ml_dtypes.bfloat16),
        "bc16f": np.ascontiguousarray(bc),
    }



def emit_E1(p, nc, dram, reduced_kv=False):
    es = contextlib.ExitStack()

    TE = T + 2
    x2T = dram("x2T", [D, T], F32, "in")
    qfT = dram("qfT", [D, T], BF16, "in")
    if reduced_kv:
        kv_red = dram("kv_red", [64, 1024], F32, "in")
        ks_red = dram("ks_red", [128, 8], F32, "in")
    else:
        kv_all = dram("kv_all", [NCORES, 64, 1024], F32, "in")
        ks_all = dram("ks_all", [NCORES, 128, 8], F32, "in")
    edges_all = dram("edges_all", [NCORES * 128, 16], BF16, "in")
    nbr_idx_d = dram("nbr_idx", [128, 2], U32, "in")
    wprojT = dram("wprojT", [D, D], F32, "in")
    bproj_fm = dram("bproj_fm", [128, CH], F32, "in")
    wdwc_fm = dram("wdwc_fm", [128, CH * 3], F32, "in")
    bdwc_fm = dram("bdwc_fm", [128, CH], F32, "in")
    blkb = dram("blkb", [128, CH * 16], BF16, "in")
    bc16f = dram("bc16f", [16, CH * 128], F32, "in")
    cvec = dram("cvec", [128, 8], F32, "in")
    o_x3 = dram("o_x3", [D, T], F32, "out")

    sb = lambda name, shape, dt=F32: es.enter_context(nc.sbuf_tensor(name, shape, dt, allow_name_mangling=True))
    ps = lambda name, shape, dt=F32: es.enter_context(nc.psum_tensor(name, shape, dt, allow_name_mangling=True))

    cvec_t = sb("cvec_t", [128, 8])
    q_bf = sb("q_bf", [128, CH, T], BF16)
    kvg = sb("kvg", [64, 1024])
    ksg = sb("ksg", [128, 8])
    nbr_t = sb("nbr_t", [128, CH, 2], BF16)
    nbrL_t = sb("nbrL_t", [128, 16], BF16)
    nbrR_t = sb("nbrR_t", [128, 16], BF16)
    nbr_idx = sb("nbr_idx_s", [128, 2], U32)
    bd = sb("bd", [128, CH * 128], BF16)
    oa_ext = sb("oa_ext", [128, CH, TE], BF16)
    tmpz = sb("tmpz", [128, CH, SUB], BF16)
    zrow = sb("zrow", [16, SUB])
    oat = sb("oat", [128, SUB])
    out2b = sb("out2b", [128, CH, SUB], BF16)
    wproj_s = sb("wproj_s", [128, CH, D], BF16)
    bproj_t = sb("bproj_t", [128, CH])
    wdwc_t = sb("wdwc_t", [128, CH * 3])
    wdw1p = sb("wdw1p", [128, CH])
    bdwc_t = sb("bdwc_t", [128, CH])
    blk_t = sb("blk_t", [128, CH * 16], BF16)
    bc_t = sb("bc_t", [16, CH * 128])
    x2f = sb("x2f", [128, CH, SUB])
    s_sub = sb("s_sub", [128, CH, SUB])
    sbfc = sb("sbfc", [128, CH, SUB], BF16)
    sqbf = sb("sqbf", [128, CH, SUB], BF16)
    x3_t = sb("x3_t", [128, CH, SUB])
    ln = LNKit(p, sb, ps, cvec_t)

    ps_oa = ps("ps_oa", [128, SUB])
    ps_zb = ps("ps_zb", [128, SUB])
    ps_z = ps("ps_z", [16, SUB])
    ps_pr = ps("ps_pr", [128, SUB])

    ld = p.sem("ld")
    ldg = p.sem("ldg")
    ldx = p.sem("ldx")
    cm = p.sem("cm")
    mm = p.sem("mm")
    sc = p.sem("sc")
    st = p.sem("st")

    p.dma("sync", cvec_t[:], cvec, ld)
    p.dma("sync", nbr_idx[:], nbr_idx_d, ld)
    p.dma("sync", bproj_t[:], bproj_fm, ld)
    p.dma("sync", wdwc_t[:], wdwc_fm, ld)
    p.dma("sync", bdwc_t[:], bdwc_fm, ld)
    p.dma("sync", blk_t[:], blkb, ld)
    p.dma("sync", bc_t[:], bc16f, ld)
    ld_small = ld.v
    ldkv = p.sem("ldkv")
    if reduced_kv:
        p.dma("gpsimd", kvg[:], kv_red, ldkv)
        p.dma("gpsimd", ksg[:], ks_red, ldkv)
    else:
        p.dma("gpsimd", kvg[:], kv_all[0], ldkv)
        p.dma("gpsimd", ksg[:], ks_all[0], ldkv)
        for j in range(1, NCORES):
            p.wait("gpsimd", ldkv, ldkv.v)
            p.dma("gpsimd", kvg[:], kv_all[j], ldkv, accum_op=ALU.add)
            p.wait("gpsimd", ldkv, ldkv.v)
            p.dma("gpsimd", ksg[:], ks_all[j], ldkv, accum_op=ALU.add)
    ld_kv = ldkv.v
    p.dma("gpsimd", q_bf[:], qfT.rearrange("(c p) t -> p c t", p=128), ldg)
    p.dma("gpsimd", wproj_s[:], wprojT.rearrange("(c p) o -> p c o", p=128), ldg)
    ld_big = ldg.v

    p.wait("vector", ld, ld_small)
    p.op("vector", "memset", nbrL_t[:], 0.0)
    p.op("vector", "memset", nbrR_t[:], 0.0, sem=cm)
    p.wait("gpsimd", cm, cm.v)
    gnb = p.sem("gnb")
    for (dst, col) in ((nbrL_t, 0), (nbrR_t, 1)):
        gnb.bump(16)
        p.emit("gpsimd", lambda e, o=dst[:], ix=nbr_idx[:, col:col + 1], s2=gnb.h, src_=edges_all:
               e.indirect_dma_start(out=o, out_offset=None, in_=src_,
                                    in_offset=bass.IndirectOffsetOnAxis(ap=ix, axis=0),
                                    bounds_check=NCORES * 128 - 1, oob_is_err=False).then_inc(s2, 16))
    p.wait("vector", gnb, gnb.v)
    nl3 = nbrL_t[:].rearrange("p (c r) -> p c r", r=2)
    nr3 = nbrR_t[:].rearrange("p (c r) -> p c r", r=2)
    p.op("vector", "tensor_copy", nbr_t[:, :, 0], nl3[:, :, 1])
    p.op("vector", "tensor_copy", nbr_t[:, :, 1], nr3[:, :, 0], sem=cm)

    p.wait("vector", ldkv, ld_kv)
    p.op("vector", "memset", bd[:], 0.0)
    for oc in range(CH):
        p.op("vector", "tensor_copy", bd[0:64, oc * 128:oc * 128 + 64], kvg[0:64, (2 * oc) * 64:(2 * oc + 1) * 64])
        p.op("vector", "tensor_copy", bd[64:128, oc * 128 + 64:oc * 128 + 128],
             kvg[0:64, (2 * oc + 1) * 64:(2 * oc + 2) * 64])
    wd3 = wdwc_t[:].rearrange("p (c k) -> p c k", k=3)
    p.op("vector", "tensor_scalar", wdw1p[:], wd3[:, :, 1], 1.0, None, ALU.add, sem=cm)
    BD_DONE = cm.v

    p.wait("tensor", cm, BD_DONE)
    p.wait("tensor", ldg, ld_big)
    zz = []
    p.wait("vector", ldg, ld_big)
    for s in range(NSUB):
        for oc in range(CH):
            p.op("vector", "tensor_scalar", tmpz[:, oc, :], q_bf[:, oc, s * SUB:(s + 1) * SUB],
                 ksg[:, oc:oc + 1], None, ALU.mult, sem=(cm if oc == CH - 1 else None))
        TMPZ = cm.v
        p.wait("tensor", cm, TMPZ)
        if s >= 1:
            p.wait("tensor", cm, zz[-1])
        for oc in range(CH):
            p.op("tensor", "matmul", ps_z[:], blk_t[:, oc * 16:(oc + 1) * 16], tmpz[:, oc, :],
                 start=(oc == 0), stop=(oc == CH - 1), sem=(mm if oc == CH - 1 else None))
        p.wait("vector", mm)
        p.op("vector", "tensor_scalar", zrow[:], ps_z[:], 1e-6, None, ALU.add)
        p.op("vector", "reciprocal", zrow[:], zrow[:], sem=cm)
        zz.append(cm.v)
        p.wait("tensor", cm, zz[-1])
        for oc in range(CH):
            p.op("tensor", "matmul", ps_oa[:], bd[:, oc * 128:(oc + 1) * 128],
                 q_bf[:, oc, s * SUB:(s + 1) * SUB], start=True, stop=True, sem=mm)
            p.op("tensor", "matmul", ps_zb[:], bc_t[:, oc * 128:(oc + 1) * 128], zrow[:],
                 start=True, stop=True, sem=mm)
            p.wait("scalar", mm)
            p.op("scalar", "copy", oat[:], ps_oa[:], sem=sc)
            p.wait("vector", sc)
            p.op("vector", "tensor_tensor", oa_ext[:, oc, 1 + s * SUB:1 + (s + 1) * SUB], oat[:],
                 ps_zb[:], ALU.mult, sem=cm)
            p.wait("tensor", cm, cm.v)
    for oc in range(CH):
        p.op("vector", "tensor_scalar", tmpz[:, oc, 0:2], nbr_t[:, oc, :], ksg[:, oc:oc + 1], None,
             ALU.mult, sem=(cm if oc == CH - 1 else None))
    TMPB = cm.v
    p.wait("tensor", cm, TMPB)
    for oc in range(CH):
        p.op("tensor", "matmul", ps_z[:, 0:2], blk_t[:, oc * 16:(oc + 1) * 16], tmpz[:, oc, 0:2],
             start=(oc == 0), stop=(oc == CH - 1), sem=(mm if oc == CH - 1 else None))
    p.wait("vector", mm)
    p.op("vector", "tensor_scalar", zrow[:, 0:2], ps_z[:, 0:2], 1e-6, None, ALU.add)
    p.op("vector", "reciprocal", zrow[:, 0:2], zrow[:, 0:2], sem=cm)
    ZB = cm.v
    p.wait("tensor", cm, ZB)
    for oc in range(CH):
        p.op("tensor", "matmul", ps_oa[:, 0:2], bd[:, oc * 128:(oc + 1) * 128], nbr_t[:, oc, :],
             start=True, stop=True, sem=mm)
        p.op("tensor", "matmul", ps_zb[:, 0:2], bc_t[:, oc * 128:(oc + 1) * 128], zrow[:, 0:2],
             start=True, stop=True, sem=mm)
        p.wait("scalar", mm)
        p.op("scalar", "copy", oat[:, 0:2], ps_oa[:, 0:2], sem=sc)
        p.wait("vector", sc)
        p.op("vector", "tensor_tensor", oat[:, 0:2], oat[:, 0:2], ps_zb[:, 0:2], ALU.mult)
        p.op("vector", "tensor_copy", oa_ext[:, oc, 0:1], oat[:, 0:1])
        p.op("vector", "tensor_copy", oa_ext[:, oc, TE - 1:TE], oat[:, 1:2], sem=cm)
        p.wait("tensor", cm, cm.v)
    OA_ALL = cm.v

    ln.init_consts(cm)
    prev_pr = {}
    kcnt = 0
    stv = []
    for s in range(NSUB):
        p.wait("sync", cm, OA_ALL if s == 0 else ln.prev_apply_done)
        p.dma("sync", x2f[:], x2T.rearrange("(c p) t -> p c t", p=128)[:, :, s * SUB:(s + 1) * SUB], ldx)
        ldx_s = ldx.v
        if s == 0:
            p.wait("vector", cm, OA_ALL)
        for oc in range(CH):
            mid = oa_ext[:, oc, 1 + s * SUB:1 + (s + 1) * SUB]
            left = oa_ext[:, oc, s * SUB:(s + 1) * SUB]
            right = oa_ext[:, oc, 2 + s * SUB:2 + (s + 1) * SUB]
            p.op("vector", "tensor_scalar", oat[:], mid, wdw1p[:, oc:oc + 1], None, ALU.mult)
            p.op("vector", "scalar_tensor_tensor", oat[:], left, wd3[:, oc, 0:1], oat[:], ALU.mult, ALU.add)
            p.op("vector", "scalar_tensor_tensor", oat[:], right, wd3[:, oc, 2:3], oat[:], ALU.mult, ALU.add)
            p.op("vector", "tensor_scalar", out2b[:, oc, :], oat[:], bdwc_t[:, oc:oc + 1], None, ALU.add,
                 sem=(cm if oc == CH - 1 else None))
        OUT2 = cm.v
        p.wait("tensor", cm, OUT2)
        for oc in range(CH):
            if kcnt >= 1:
                p.wait("tensor", cm, prev_pr[kcnt - 1])
            for ic in range(CH):
                p.op("tensor", "matmul", ps_pr[:], wproj_s[:, ic, oc * 128:(oc + 1) * 128],
                     out2b[:, ic, :], start=(ic == 0), stop=(ic == CH - 1),
                     sem=(mm if ic == CH - 1 else None))
            p.wait("scalar", mm)
            p.op("scalar", "copy", s_sub[:, oc, :], ps_pr[:], sem=sc)
            p.wait("vector", sc)
            if oc == 0:
                p.wait("vector", ldx, ldx_s)
            p.op("vector", "tensor_scalar", s_sub[:, oc, :], s_sub[:, oc, :], bproj_t[:, oc:oc + 1], None, ALU.add)
            p.op("vector", "tensor_tensor", s_sub[:, oc, :], s_sub[:, oc, :], x2f[:, oc, :], ALU.add, sem=cm)
            prev_pr[kcnt] = cm.v
            kcnt += 1
            p.op("vector", "tensor_copy", sbfc[:, oc, :], s_sub[:, oc, :])
        evd = cm.v
        out_guard = (st, stv[-1]) if s >= 1 else None
        ln.emit(s_sub, sbfc, sqbf, x3_t, cm, mm, sc, evd, out_guard=out_guard)
        p.wait("sync", cm, ln.prev_apply_done)
        p.dma("sync", o_x3.rearrange("(c p) t -> p c t", p=128)[:, :, s * SUB:(s + 1) * SUB], x3_t[:], st)
        stv.append(st.v)
    p.wait("sync", st)
    es.close()


def host_inputs_E1(inputs, core, c_results, d_results):
    import ml_dtypes
    cv = np.zeros((128, 8), dtype=np.float32)
    cv[:, 0] = PI / 2
    cv[:, 1] = 1e-5
    cv[:, 2] = 1e-6
    blk = np.zeros((128, CH * 16), dtype=np.float32)
    for oc in range(CH):
        for pp in range(128):
            blk[pp, oc * 16 + 2 * oc + pp // 64] = 1.0
    bc = np.zeros((16, CH * 128), dtype=np.float32)
    for oc in range(CH):
        for m in range(128):
            bc[2 * oc + m // 64, oc * 128 + m] = 1.0

    def fm(v):
        return np.ascontiguousarray(np.asarray(v).reshape(CH, 128).T.astype(np.float32))

    OOB = np.uint32(1 << 30)
    nbr_idx = np.full((128, 2), OOB, dtype=np.uint32)
    if core > 0:
        nbr_idx[:, 0] = (core - 1) * 128 + np.arange(128)
    if core < NCORES - 1:
        nbr_idx[:, 1] = (core + 1) * 128 + np.arange(128)
    edges_all = np.concatenate([np.asarray(d_results[j]["o_edge"]) for j in range(NCORES)], axis=0)
    wd = np.asarray(inputs["L1_attn_w_dwc"]).reshape(CH, 128, 3).transpose(1, 0, 2).reshape(128, CH * 3)
    return {
        "x2T": np.ascontiguousarray(c_results[core]["o_x2"]),
        "qfT": np.ascontiguousarray(d_results[core]["o_qfin"]),
        "kv_all": np.ascontiguousarray(np.stack([np.asarray(d_results[j]["o_kv"]) for j in range(NCORES)])),
        "ks_all": np.ascontiguousarray(np.stack([np.asarray(d_results[j]["o_ks"]) for j in range(NCORES)])),
        "edges_all": np.ascontiguousarray(edges_all),
        "nbr_idx": np.ascontiguousarray(nbr_idx),
        "wprojT": np.ascontiguousarray(np.asarray(inputs["L1_attn_w_proj"]).T),
        "bproj_fm": fm(inputs["L1_attn_b_proj"]),
        "wdwc_fm": np.ascontiguousarray(wd.astype(np.float32)),
        "bdwc_fm": fm(inputs["L1_attn_b_dwc"]),
        "blkb": blk.astype(ml_dtypes.bfloat16),
        "bc16f": np.ascontiguousarray(bc),
        "cvec": cv,
    }



CAP = 512
NTILE = T // 128


def emit_E2(p, nc, dram):
    es = contextlib.ExitStack()

    x3T = dram("x3T", [D, T], F32, "in")
    Wr_in = dram("Wr_in", [D, 8], F32, "in")
    W1_in = dram("W1_in", [8, D, 4 * D], F32, "in")
    W2_in = dram("W2_in", [8, 4 * D, D], F32, "in")
    tri_in = dram("tri_in", [128, 128], BF16, "in")
    ident_in = dram("ident_in", [128, 128], BF16, "in")
    iotaB = dram("iotaB", [128, 8], F32, "in")
    cvec = dram("cvec", [128, 8], F32, "in")
    o_out = dram("o_out", [D, T], F32, "out")
    xs_d = dram("xs_d", [8 * CAP, D], BF16, "int")
    ys_d = dram("ys_d", [8 * CAP, D], BF16, "int")

    sb = lambda name, shape, dt=F32: es.enter_context(nc.sbuf_tensor(name, shape, dt, allow_name_mangling=True))

    cvec_t = sb("cvec_t", [128, 8])
    x3bf = sb("x3bf", [128, CH, T], BF16)
    Wr_t = sb("Wr_t", [128, CH, 8])
    tri_t = sb("tri_t", [128, 128], BF16)
    id_t = sb("id_t", [128, 128], BF16)
    iotaB_t = sb("iotaB_t", [128, 8])
    pos_u = sb("pos_u", [128, NTILE], U32)
    w0_all = sb("w0_all", [128, NTILE])
    ones_colb = sb("ones_colb", [128, 1], BF16)
    ones_rowf = sb("ones_rowf", [1, 128])
    runtot = sb("runtot", [1, 8])
    lg_s = sb("lg_s", [128, 8])
    sel_a = sb("sel_a", [128, 8])
    sel_b = sb("sel_b", [128, 8])
    v0_c = sb("v0_c", [128, 1])
    v1_c = sb("v1_c", [128, 1])
    es_c = sb("es_c", [128, 1])
    rk_c = sb("rk_c", [128, 1])
    oh_bf = sb("oh_bf", [128, 8], BF16)
    x3f_t = sb("x3f_t", [128, CH, 128])
    zeros_b = sb("zeros_b", [128, D], BF16)
    tm_t = sb("tm_t", [128, D], BF16)

    ld = p.sem("ld")
    ldg = p.sem("ldg")
    ldx = p.sem("ldx")
    cm = p.sem("cm")
    mm = p.sem("mm")
    sc = p.sem("sc")
    st = p.sem("st")
    sg = p.sem("sg")
    gth = p.sem("gth")

    p.dma("sync", cvec_t[:], cvec, ld)
    p.dma("sync", tri_t[:], tri_in, ld)
    p.dma("sync", id_t[:], ident_in, ld)
    p.dma("sync", iotaB_t[:], iotaB, ld)
    p.dma("sync", Wr_t[:], Wr_in.rearrange("(c p) e -> p c e", p=128), ld)
    ld_small = ld.v
    p.dma("gpsimd", x3bf[:], x3T.rearrange("(c p) t -> p c t", p=128), ldg)
    ld_big = ldg.v

    p.op("vector", "memset", ones_colb[:], 1.0)
    p.op("vector", "memset", ones_rowf[:], 1.0)
    p.op("vector", "memset", runtot[:], 0.0)
    p.op("vector", "memset", zeros_b[:], 0.0, sem=cm)
    CONSTS = cm.v
    zf = p.sem("zf")
    p.wait("scalar", cm, CONSTS)
    for r in range(8 * CAP // 128):
        p.dma("scalar", xs_d[r * 128:(r + 1) * 128, :], zeros_b[:], zf)
    XS_ZERO = zf.v

    es_p1 = contextlib.ExitStack()
    psn = lambda name, shape, dt=F32: es_p1.enter_context(nc.psum_tensor(name, shape, dt, allow_name_mangling=True))
    ps_lg = psn("ps_lg", [128, 8])
    ps_rk = psn("ps_rk", [128, 8])
    ps_rt = psn("ps_rt", [128, 8])
    ps_ct = psn("ps_ct", [1, 8])

    p.wait("tensor", ld, ld_small)
    p.wait("tensor", cm, CONSTS)
    BIG = 1.0e6
    rt_read = None
    sig_prev = None
    for tile in range(NTILE):
        if sig_prev is not None:
            p.wait("vector", sc, sig_prev)
        p.wait("sync", cm, cm.v)
        p.dma("sync", x3f_t[:], x3T.rearrange("(c p) t -> p c t", p=128)[:, :, tile * 128:(tile + 1) * 128], ldx)
        p.wait("tensor", ldx, ldx.v)
        for ic in range(CH):
            p.op("tensor", "matmul", ps_lg[:], x3f_t[:, ic, :], Wr_t[:, ic, :],
                 start=(ic == 0), stop=(ic == CH - 1), sem=(mm if ic == CH - 1 else None))
        p.wait("vector", mm)
        p.op("vector", "tensor_copy", lg_s[:], ps_lg[:])
        p.op("vector", "tensor_reduce", v0_c[:], lg_s[:], AX.X, ALU.max)
        p.op("vector", "tensor_tensor", sel_a[:], lg_s[:], v0_c[:].broadcast_to([128, 8]), ALU.subtract)
        p.op("vector", "tensor_scalar", sel_a[:], sel_a[:], BIG, None, ALU.mult)
        p.op("vector", "tensor_scalar", sel_a[:], sel_a[:], -1.0, None, ALU.max)
        p.op("vector", "tensor_scalar", sel_a[:], sel_a[:], 1.0, None, ALU.add)
        p.op("vector", "scalar_tensor_tensor", sel_b[:], sel_a[:], -BIG, iotaB_t[:], ALU.mult, ALU.add)
        p.op("vector", "tensor_reduce", es_c[:], sel_b[:], AX.X, ALU.min)
        p.op("vector", "tensor_tensor", sel_b[:], sel_b[:], es_c[:].broadcast_to([128, 8]), ALU.subtract)
        p.op("vector", "tensor_scalar", sel_b[:], sel_b[:], 1.0, None, ALU.min)
        p.op("vector", "scalar_tensor_tensor", sel_b[:], sel_b[:], -1.0, cvec_t[:, 3:4].broadcast_to([128, 8]),
             ALU.mult, ALU.add)
        p.op("vector", "tensor_scalar", sel_b[:], sel_b[:], 1.0, None, ALU.add)
        p.op("vector", "scalar_tensor_tensor", sel_a[:], sel_b[:], -BIG, lg_s[:], ALU.mult, ALU.add)
        p.op("vector", "tensor_reduce", v1_c[:], sel_a[:], AX.X, ALU.max)
        p.op("vector", "tensor_tensor", v0_c[:], v0_c[:], v1_c[:], ALU.subtract, sem=cm)
        p.wait("scalar", cm)
        p.op("scalar", "activation", w0_all[:, tile:tile + 1], v0_c[:], AF.Sigmoid, sem=sc)
        sig_prev = sc.v
        p.op("vector", "tensor_copy", oh_bf[:], sel_b[:], sem=cm)
        OH = cm.v
        p.wait("tensor", cm, OH)
        if rt_read is not None:
            p.wait("tensor", cm, rt_read)
        p.op("tensor", "matmul", ps_rk[:], tri_t[:], oh_bf[:], start=True, stop=True)
        p.op("tensor", "matmul", ps_rt[:], ones_rowf[:], runtot[:], start=True, stop=True)
        p.op("tensor", "matmul", ps_ct[:], ones_colb[:], oh_bf[:], start=True, stop=True,
             sem=mm)
        p.wait("vector", mm)
        p.op("vector", "tensor_copy", sel_a[:], ps_rk[:])
        p.op("vector", "tensor_tensor", sel_a[:], sel_a[:], ps_rt[:], ALU.add)
        p.op("vector", "tensor_tensor", runtot[:], runtot[:], ps_ct[:], ALU.add, sem=cm)
        rt_read = cm.v
        p.op("vector", "tensor_tensor", sel_a[:], sel_a[:], sel_b[:], ALU.mult)
        p.op("vector", "tensor_reduce", rk_c[:], sel_a[:], AX.X, ALU.add)
        p.op("vector", "tensor_scalar", es_c[:], es_c[:], float(CAP), None, ALU.mult)
        p.op("vector", "tensor_tensor", es_c[:], es_c[:], rk_c[:], ALU.add)
        p.op("vector", "tensor_scalar", rk_c[:], rk_c[:], -float(CAP - 1), None, ALU.add)
        p.op("vector", "tensor_scalar", rk_c[:], rk_c[:], 0.0, None, ALU.max)
        p.op("vector", "tensor_scalar", rk_c[:], rk_c[:], 1.0, None, ALU.min)
        p.op("vector", "scalar_tensor_tensor", es_c[:], rk_c[:], 1.0e7, es_c[:], ALU.mult, ALU.add)
        p.op("vector", "tensor_copy", pos_u[:, tile:tile + 1], es_c[:], sem=cm)

    ps_tp = es_p1.enter_context(nc.psum_tensor("ps_tp", [128, 128], BF16, allow_name_mangling=True))
    p.wait("gpsimd", zf, XS_ZERO)
    p.wait("tensor", ldg, ld_big)
    for tile in range(NTILE):
        for ic in range(CH):
            if tile > 0 and ic == 0:
                p.wait("vector", sg, sg.v)
            p.transpose_to(ps_tp, x3bf[:, ic, tile * 128:(tile + 1) * 128], id_t,
                           tm_t[:, ic * 128:(ic + 1) * 128], mm, sem=(cm if ic == CH - 1 else None))
        p.wait("gpsimd", cm, cm.v)
        sg.bump(16)
        p.emit("gpsimd", lambda e, o=xs_d, i=tm_t[:], ix=pos_u[:, tile:tile + 1], s=sg.h:
               e.indirect_dma_start(out=o, out_offset=bass.IndirectOffsetOnAxis(ap=ix, axis=0),
                                    in_=i, in_offset=None, bounds_check=8 * CAP - 1,
                                    oob_is_err=False).then_inc(s, 16))
    SCATTERED = sg.v
    es_p1.close()

    es_p3 = contextlib.ExitStack()
    psn3 = lambda name, shape, dt=F32: es_p3.enter_context(nc.psum_tensor(name, shape, dt, allow_name_mangling=True))
    ps_h = [psn3("ps_h0", [128, CAP]), psn3("ps_h1", [128, CAP])]
    ps_tp2 = psn3("ps_tp2", [128, 128], BF16)

    sb3 = lambda name, shape, dt=F32: es_p3.enter_context(nc.sbuf_tensor(name, shape, dt, allow_name_mangling=True))
    w1s = [sb3(f"w1s{i}", [128, CH, D], BF16) for i in range(2)]
    w2s = [sb3(f"w2s{i}", [128, CH, D], BF16) for i in range(2)]
    xe_fm = sb3("xe_fm", [128, CH, CAP], BF16)
    h_bf = sb3("h_bf", [128, CH, CAP], BF16)
    y_acc = sb3("y_acc", [128, CH, CAP])
    y_bfc = sb3("y_bfc", [128, CH, CAP], BF16)

    p.wait("sync", sg, SCATTERED)
    ldw = [p.sem("ldw0"), p.sem("ldw1")]
    wload = {}
    wcons = {}

    def emit_wload(sl):
        e2, hs2 = sl // 4, sl % 4
        buf2 = sl % 2
        if sl >= 2:
            p.wait("gpsimd", mm, wcons[(sl - 2, "w1")])
        p.dma("gpsimd", w1s[buf2][:], W1_in[e2, :, hs2 * D:(hs2 + 1) * D].rearrange("(c p) h -> p c h", p=128), ldw[buf2])
        if sl >= 2:
            p.wait("gpsimd", mm, wcons[(sl - 2, "w2")])
        p.dma("gpsimd", w2s[buf2][:], W2_in[e2, hs2 * D:(hs2 + 1) * D, :].rearrange("(c p) o -> p c o", p=128), ldw[buf2])
        wload[sl] = (buf2, ldw[buf2].v)

    emit_wload(0)
    emit_wload(1)
    for e in range(8):
        for tt in range(4):
            if e > 0 and tt == 0:
                p.wait("sync", ldx, ygone)
            p.dma("sync", tm_t[:], xs_d[e * CAP + tt * 128: e * CAP + (tt + 1) * 128, :], ldx)
            p.wait("tensor", ldx, ldx.v)
            if e > 0 or tt > 0:
                p.wait("tensor", cm, xe_ev)
            for ic in range(CH):
                p.transpose_to(ps_tp2, tm_t[:, ic * 128:(ic + 1) * 128], id_t,
                               xe_fm[:, ic, tt * 128:(tt + 1) * 128], mm,
                               sem=(cm if (ic == CH - 1) else None))
            xe_ev = cm.v
            p.wait("sync", mm, mm.v)
        p.op("vector", "memset", y_acc[:].rearrange("p c t -> p (c t)"), 0.0, sem=cm)
        YZ = cm.v
        for hs in range(4):
            sl = e * 4 + hs
            buf = sl % 2
            p.wait("tensor", ldw[wload[sl][0]], wload[sl][1])
            p.wait("tensor", cm, xe_ev)
            hprev = {}
            for hc in range(CH):
                ph = ps_h[hc % 2]
                if hc >= 2:
                    p.wait("tensor", sc, hprev[hc - 2])
                for ic in range(CH):
                    p.op("tensor", "matmul", ph[:], w1s[buf][:, ic, hc * 128:(hc + 1) * 128],
                         xe_fm[:, ic, :], start=(ic == 0), stop=(ic == CH - 1),
                         sem=(mm if ic == CH - 1 else None))
                p.wait("scalar", mm)
                p.op("scalar", "activation", h_bf[:, hc, :], ph[:], AF.Gelu, sem=sc)
                hprev[hc] = sc.v
            wcons[(sl, "w1")] = mm.v
            p.wait("tensor", sc, hprev[CH - 1])
            p.wait("scalar", cm, YZ)
            for oc in range(CH):
                py = ps_h[oc % 2]
                for hc in range(CH):
                    p.op("tensor", "matmul", py[:], w2s[buf][:, hc, oc * 128:(oc + 1) * 128],
                         h_bf[:, hc, :], start=(hc == 0), stop=(hc == CH - 1),
                         sem=(mm if hc == CH - 1 else None))
                p.wait("vector", mm)
                p.op("vector", "tensor_tensor", y_acc[:, oc, :], y_acc[:, oc, :], py[:], ALU.add, sem=cm)
                p.wait("tensor", cm, cm.v)
            wcons[(sl, "w2")] = mm.v
            if sl + 2 < 32:
                emit_wload(sl + 2)
        p.op("vector", "tensor_copy", y_bfc[:].rearrange("p c t -> p (c t)"),
             y_acc[:].rearrange("p c t -> p (c t)"), sem=cm)
        YB = cm.v
        p.wait("tensor", cm, YB)
        for blk in range(4):
            if e > 0 or blk > 0:
                p.wait("vector", ldx, ygone)
            for oc in range(CH):
                p.transpose_to(ps_tp2, y_bfc[:, oc, blk * 128:(blk + 1) * 128], id_t,
                               tm_t[:, oc * 128:(oc + 1) * 128], mm,
                               sem=(cm if oc == CH - 1 else None))
            p.wait("sync", cm, cm.v)
            p.dma("sync", ys_d[e * CAP + blk * 128: e * CAP + (blk + 1) * 128, :], tm_t[:], ldx)
            ygone = ldx.v
    es_p3.close()

    es_p4 = contextlib.ExitStack()
    psn4 = lambda name, shape, dt=F32: es_p4.enter_context(nc.psum_tensor(name, shape, dt, allow_name_mangling=True))
    ps_tp3 = psn4("ps_tp3", [128, 128], BF16)
    ln = LNKit(p, sb, ps, cvec_t) if False else None
    ln = LNKit(p, sb, lambda n, s, dt=F32: es_p4.enter_context(nc.psum_tensor(n, s, dt, allow_name_mangling=True)), cvec_t)
    x3f_s = sb("x3f_s", [128, CH, SUB])
    s_sub = sb("s_sub", [128, CH, SUB])
    sbfc = sb("sbfc", [128, CH, SUB], BF16)
    sqbf = sb("sqbf", [128, CH, SUB], BF16)
    out_t = sb("out_t", [128, CH, SUB])
    y_tm = sb("y_tm", [128, D], BF16)

    ln.init_consts(cm)
    stv = []
    for s in range(NSUB):
        p.wait("sync", cm, cm.v)
        p.dma("sync", x3f_s[:], x3T.rearrange("(c p) t -> p c t", p=128)[:, :, s * SUB:(s + 1) * SUB], ldx)
        ldx_s = ldx.v
        p.wait("vector", ldx, ldx_s)
        for tt in range(4):
            tile = s * 4 + tt
            p.op("vector", "memset", y_tm[:], 0.0, sem=cm)
            p.wait("gpsimd", cm, cm.v)
            gth.bump(16)
            p.emit("gpsimd", lambda e, o=y_tm[:], i=ys_d, ix=pos_u[:, tile:tile + 1], s2=gth.h:
                   e.indirect_dma_start(out=o, out_offset=None, in_=i,
                                        in_offset=bass.IndirectOffsetOnAxis(ap=ix, axis=0),
                                        bounds_check=8 * CAP - 1, oob_is_err=False).then_inc(s2, 16))
            p.wait("vector", gth, gth.v)
            p.op("vector", "tensor_scalar", y_tm[:], y_tm[:], w0_all[:, tile:tile + 1], None, ALU.mult, sem=cm)
            YS = cm.v
            p.wait("tensor", cm, YS)
            for oc in range(CH):
                p.wait_last("tensor", "vector")
                p.op("tensor", "transpose", ps_tp3[:], y_tm[:, oc * 128:(oc + 1) * 128], id_t[:], sem=mm)
                p.wait("vector", mm)
                p.op("vector", "tensor_tensor", s_sub[:, oc, tt * 128:(tt + 1) * 128], ps_tp3[:],
                     x3f_s[:, oc, tt * 128:(tt + 1) * 128], ALU.add,
                     sem=(cm if oc == CH - 1 else None))
        for oc in range(CH):
            p.op("vector", "tensor_copy", sbfc[:, oc, :], s_sub[:, oc, :],
                 sem=(cm if oc == CH - 1 else None))
        evd = cm.v
        out_guard = (st, stv[-1]) if s >= 1 else None
        ln.emit(s_sub, sbfc, sqbf, out_t, cm, mm, sc, evd, out_guard=out_guard)
        p.wait("sync", cm, ln.prev_apply_done)
        p.dma("sync", o_out.rearrange("(c p) t -> p c t", p=128)[:, :, s * SUB:(s + 1) * SUB], out_t[:], st)
        stv.append(st.v)
    p.wait("sync", st)
    es_p4.close()
    es.close()


def host_inputs_E2(inputs, core, e1_results):
    import ml_dtypes
    cv = np.zeros((128, 8), dtype=np.float32)
    cv[:, 0] = PI / 2
    cv[:, 1] = 1e-5
    cv[:, 2] = 1e-6
    cv[:, 3] = 0.0
    tri = np.triu(np.ones((128, 128), dtype=np.float32), 1)
    iotab = np.tile(np.arange(8, dtype=np.float32) + 1.0e6, (128, 1))
    return {
        "x3T": np.ascontiguousarray(e1_results[core]["o_x3"]),
        "Wr_in": np.ascontiguousarray(np.asarray(inputs["L1_ffn_Wr"]).astype(np.float32)),
        "W1_in": np.ascontiguousarray(np.asarray(inputs["L1_ffn_W1"]).astype(np.float32)),
        "W2_in": np.ascontiguousarray(np.asarray(inputs["L1_ffn_W2"]).astype(np.float32)),
        "tri_in": tri.astype(ml_dtypes.bfloat16),
        "ident_in": np.eye(128).astype(ml_dtypes.bfloat16),
        "iotaB": np.ascontiguousarray(iotab),
        "cvec": cv,
    }




def _standalone(emit_fn):
    nc = bass.Bass("TRN2", target_bir_lowering=False)
    es = contextlib.ExitStack()
    p = Prog(nc, es)
    emit_fn(p, nc, make_dram(nc))
    p.finalize()
    es.close()
    return nc


def build_launch_A(stages=6):
    nc = bass.Bass("TRN2", target_bir_lowering=False)
    es = contextlib.ExitStack()
    p = Prog(nc, es)
    emit_A(p, nc, make_dram(nc), stages=stages)
    p.finalize()
    es.close()
    return nc


def build_launch_B():
    return _standalone(emit_B)


def build_launch_C():
    return _standalone(emit_C)


def build_launch_D():
    return _standalone(emit_D)


def build_launch_E1():
    return _standalone(emit_E1)


def build_launch_E2():
    return _standalone(emit_E2)


def lamp_slice(lam_t, which):
    return lam_t[:, 4 * which:4 * which + 4]


def host_inputs_A(inputs, core):
    x = np.asarray(inputs["L0_attn_x"])[0]
    xT = np.ascontiguousarray(np.asarray(x).T)
    shard = np.ascontiguousarray(xT[:, core * T:(core + 1) * T])
    BreT = np.ascontiguousarray(np.asarray(inputs["L0_attn_B_re"]).T)
    BimT = np.ascontiguousarray(np.asarray(inputs["L0_attn_B_im"]).T)
    lamp = np.concatenate([
        np.asarray(inputs["L0_attn_nu_log"]).reshape(NG, 128).T,
        np.asarray(inputs["L0_attn_theta_log"]).reshape(NG, 128).T,
        np.asarray(inputs["L0_attn_gamma_log"]).reshape(NG, 128).T,
    ], axis=1).astype(np.float32)
    iotaL = np.tile(np.arange(L, dtype=np.float32), (128, 1))
    cv = np.zeros((128, 8), dtype=np.float32)
    cv[:, 0] = PI / 2
    cv[:, 1] = 1e-5
    cv[:, 2] = 1e-6
    return {"xT": shard, "BreT": BreT, "BimT": BimT, "lamp": np.ascontiguousarray(lamp),
            "iotaL": np.ascontiguousarray(iotaL), "cvec": cv}


FUSED_INTERNAL = ("cum_d_re", "cum_d_im", "bsum_d", "ball", "x1_d", "x2_d",
                  "qfin_d", "kv_d", "ks_d", "kv_red", "ks_red", "edge_d",
                  "edges_ag", "x3_d", "xs_d", "ys_d")
RG = [list(range(NCORES))]


def build_fused(upto=6, use_cc=True, x3_external=False, ext_names=(), a_stages=6):
    nc = bass.Bass("TRN2", target_bir_lowering=False)
    es = contextlib.ExitStack()
    p = Prog(nc, es)
    cache = {}
    _ext = set(ext_names) | ({"x3_d"} if x3_external else set())
    internal = tuple(n for n in FUSED_INTERNAL if n not in _ext)

    def dr(rename):
        return make_dram(nc, rename=rename, internal=internal, cache=cache)

    cc = p.sem("cc")
    emit_A(p, nc, dr({"o_cre": "cum_d_re", "o_cim": "cum_d_im", "o_bsum": "bsum_d"}), stages=a_stages)
    if upto <= 1:
        p.finalize(); es.close(); return nc
    p.global_barrier()
    bsum_d = cache["bsum_d"]
    ball = nc.dram_tensor("ball", [NCORES * 128, 8], F32, kind="Internal").ap()
    cache["ball"] = ball
    if use_cc:
        cc.bump(16)
        p.emit("gpsimd", lambda e, i=bsum_d, o=ball, s=cc.h:
               e.collective_compute("AllGather", ALU.bypass, RG, ins=[i], outs=[o]).then_inc(s, 16))
        p.wait("sync", cc, cc.v)
    emit_B(p, nc, dr({"i_cre": "cum_d_re", "i_cim": "cum_d_im", "o_x1": "x1_d"}))
    if upto <= 2:
        p.finalize(); es.close(); return nc
    p.global_barrier()
    emit_C(p, nc, dr({"x1T": "x1_d", "o_x2": "x2_d"}))
    if upto <= 3:
        p.finalize(); es.close(); return nc
    p.global_barrier()
    emit_D(p, nc, dr({"x2T": "x2_d", "o_qfin": "qfin_d", "o_kv": "kv_d",
                      "o_ks": "ks_d", "o_edge": "edge_d"}))
    if upto <= 4:
        p.finalize(); es.close(); return nc
    p.global_barrier()
    kv_red = nc.dram_tensor("kv_red", [64, 1024], F32, kind="Internal").ap()
    ks_red = nc.dram_tensor("ks_red", [128, 8], F32, kind="Internal").ap()
    edges_ag = nc.dram_tensor("edges_ag", [NCORES * 128, 16], BF16, kind="Internal").ap()
    cache.update({"kv_red": kv_red, "ks_red": ks_red, "edges_ag": edges_ag})
    cc.bump(16)
    p.emit("gpsimd", lambda e, i=cache["kv_d"], o=kv_red, s=cc.h:
           e.collective_compute("AllReduce", ALU.add, RG, ins=[i], outs=[o]).then_inc(s, 16))
    cc.bump(16)
    p.emit("gpsimd", lambda e, i=cache["ks_d"], o=ks_red, s=cc.h:
           e.collective_compute("AllReduce", ALU.add, RG, ins=[i], outs=[o]).then_inc(s, 16))
    cc.bump(16)
    p.emit("gpsimd", lambda e, i=cache["edge_d"], o=edges_ag, s=cc.h:
           e.collective_compute("AllGather", ALU.bypass, RG, ins=[i], outs=[o]).then_inc(s, 16))
    p.wait("gpsimd", cc, cc.v)
    p.wait("sync", cc, cc.v)
    emit_E1(p, nc, dr({"x2T": "x2_d", "qfT": "qfin_d", "edges_all": "edges_ag",
                       "o_x3": "x3_d"}), reduced_kv=True)
    if upto <= 5:
        p.finalize(); es.close(); return nc
    p.global_barrier()
    emit_E2(p, nc, dr({"x3T": "x3_d"}))
    p.finalize()
    es.close()
    return nc


def host_inputs_fused(inputs, core):
    handover = {"i_cre", "i_cim", "ball", "x1T", "x2T", "x3T", "qfT",
                "kv_all", "ks_all", "edges_all"}
    fake_a = [{"o_cre": None, "o_cim": None, "o_bsum": None}] * NCORES
    d = {}
    d.update(host_inputs_A(inputs, core))
    b = host_inputs_B(inputs, core, [{"o_cre": np.zeros((128, NG * T), np.float32),
                                      "o_cim": np.zeros((128, NG * T), np.float32),
                                      "o_bsum": np.zeros((128, 8), np.float32)}] * NCORES)
    d.update({k: v for k, v in b.items() if k not in handover})
    c = host_inputs_C(inputs, core, [{"o_x1": np.zeros((D, T), np.float32)}] * NCORES)
    d.update({k: v for k, v in c.items() if k not in handover})
    dd = host_inputs_D(inputs, core, [{"o_x2": np.zeros((D, T), np.float32)}] * NCORES)
    d.update({k: v for k, v in dd.items() if k not in handover})
    import ml_dtypes
    e1 = host_inputs_E1(inputs, core,
                        [{"o_x2": np.zeros((D, T), np.float32)}] * NCORES,
                        [{"o_qfin": np.zeros((D, T), ml_dtypes.bfloat16),
                          "o_kv": np.zeros((64, 1024), np.float32),
                          "o_ks": np.zeros((128, 8), np.float32),
                          "o_edge": np.zeros((128, 16), ml_dtypes.bfloat16)}] * NCORES)
    d.update({k: v for k, v in e1.items() if k not in handover})
    e2 = host_inputs_E2(inputs, core, [{"o_x3": np.zeros((D, T), np.float32)}] * NCORES)
    d.update({k: v for k, v in e2.items() if k not in handover})
    return d


def kernel_fused(**inputs):
    cores = list(range(NCORES))
    nc = build_fused()
    res = run_bass_kernel_spmd(nc, [host_inputs_fused(inputs, c) for c in cores], core_ids=cores)
    out = np.concatenate(
        [np.asarray(res.results[c]["o_out"]).reshape(CH, 128, T).transpose(2, 0, 1).reshape(T, D)
         for c in cores], axis=0)
    return out.reshape(1, NCORES * T, D).astype(np.float32)


def kernel_unfused(**inputs):
    cores = list(range(NCORES))
    ra = run_bass_kernel_spmd(build_launch_A(), [host_inputs_A(inputs, c) for c in cores], core_ids=cores).results
    rb = run_bass_kernel_spmd(build_launch_B(), [host_inputs_B(inputs, c, ra) for c in cores], core_ids=cores).results
    rc = run_bass_kernel_spmd(build_launch_C(), [host_inputs_C(inputs, c, rb) for c in cores], core_ids=cores).results
    rd = run_bass_kernel_spmd(build_launch_D(), [host_inputs_D(inputs, c, rc) for c in cores], core_ids=cores).results
    re1 = run_bass_kernel_spmd(build_launch_E1(), [host_inputs_E1(inputs, c, rc, rd) for c in cores], core_ids=cores).results
    re2 = run_bass_kernel_spmd(build_launch_E2(), [host_inputs_E2(inputs, c, re1) for c in cores], core_ids=cores).results
    out = np.concatenate(
        [np.asarray(re2[c]["o_out"]).reshape(CH, 128, T).transpose(2, 0, 1).reshape(T, D) for c in cores], axis=0)
    return out.reshape(1, NCORES * T, D).astype(np.float32)


def build_fused_BE2():
    nc = bass.Bass("TRN2", target_bir_lowering=False)
    es = contextlib.ExitStack()
    p = Prog(nc, es)
    cache = {}
    internal = tuple(n for n in FUSED_INTERNAL
                     if n not in ("cum_d_re", "cum_d_im", "ball"))

    def dr(rename):
        return make_dram(nc, rename=rename, internal=internal, cache=cache)

    cc = p.sem("cc")
    emit_B(p, nc, dr({"i_cre": "cum_d_re", "i_cim": "cum_d_im", "o_x1": "x1_d"}))
    p.global_barrier()
    emit_C(p, nc, dr({"x1T": "x1_d", "o_x2": "x2_d"}))
    p.global_barrier()
    emit_D(p, nc, dr({"x2T": "x2_d", "o_qfin": "qfin_d", "o_kv": "kv_d",
                      "o_ks": "ks_d", "o_edge": "edge_d"}))
    p.global_barrier()
    kv_red = nc.dram_tensor("kv_red", [64, 1024], F32, kind="Internal").ap()
    ks_red = nc.dram_tensor("ks_red", [128, 8], F32, kind="Internal").ap()
    edges_ag = nc.dram_tensor("edges_ag", [NCORES * 128, 16], BF16, kind="Internal").ap()
    cache.update({"kv_red": kv_red, "ks_red": ks_red, "edges_ag": edges_ag})
    cc.bump(16)
    p.emit("gpsimd", lambda e, i=cache["kv_d"], o=kv_red, s=cc.h:
           e.collective_compute("AllReduce", ALU.add, RG, ins=[i], outs=[o]).then_inc(s, 16))
    cc.bump(16)
    p.emit("gpsimd", lambda e, i=cache["ks_d"], o=ks_red, s=cc.h:
           e.collective_compute("AllReduce", ALU.add, RG, ins=[i], outs=[o]).then_inc(s, 16))
    cc.bump(16)
    p.emit("gpsimd", lambda e, i=cache["edge_d"], o=edges_ag, s=cc.h:
           e.collective_compute("AllGather", ALU.bypass, RG, ins=[i], outs=[o]).then_inc(s, 16))
    p.wait("gpsimd", cc, cc.v)
    p.wait("sync", cc, cc.v)
    emit_E1(p, nc, dr({"x2T": "x2_d", "qfT": "qfin_d", "edges_all": "edges_ag",
                       "o_x3": "x3_d"}), reduced_kv=True)
    p.global_barrier()
    emit_E2(p, nc, dr({"x3T": "x3_d"}))
    p.finalize()
    es.close()
    return nc


def host_inputs_BE2(inputs, core, a_results):
    d = host_inputs_fused(inputs, core)
    d.pop("BreT", None); d.pop("BimT", None); d.pop("iotaL", None)
    d["cum_d_re"] = np.ascontiguousarray(np.asarray(a_results[core]["o_cre"]))
    d["cum_d_im"] = np.ascontiguousarray(np.asarray(a_results[core]["o_cim"]))
    d["ball"] = np.concatenate([np.asarray(a_results[j]["o_bsum"]) for j in range(NCORES)],
                               axis=0).astype(np.float32)
    return d


def build_fused_BCD():
    nc = bass.Bass("TRN2", target_bir_lowering=False)
    es = contextlib.ExitStack()
    p = Prog(nc, es)
    cache = {}
    internal = ("x1_d",)

    def dr(rename):
        return make_dram(nc, rename=rename, internal=internal, cache=cache)

    emit_B(p, nc, dr({"i_cre": "cum_d_re", "i_cim": "cum_d_im", "o_x1": "x1_d"}))
    p.global_barrier()
    emit_C(p, nc, dr({"x1T": "x1_d"}))
    p.global_barrier()
    emit_D(p, nc, dr({"x2T": "o_x2"}))
    p.finalize()
    es.close()
    return nc


def build_fused_E1E2():
    nc = bass.Bass("TRN2", target_bir_lowering=False)
    es = contextlib.ExitStack()
    p = Prog(nc, es)
    cache = {}
    internal = ("x3_d", "xs_d", "ys_d")

    def dr(rename):
        return make_dram(nc, rename=rename, internal=internal, cache=cache)

    emit_E1(p, nc, dr({"o_x3": "x3_d"}))
    p.global_barrier()
    emit_E2(p, nc, dr({"x3T": "x3_d"}))
    p.finalize()
    es.close()
    return nc


def kernel_3launch(**inputs):
    cores = list(range(NCORES))
    ra = run_bass_kernel_spmd(build_launch_A(), [host_inputs_A(inputs, c) for c in cores],
                              core_ids=cores).results
    nc2 = build_fused_BCD()
    im2 = []
    for c in cores:
        d = host_inputs_B(inputs, c, ra)
        d["i_cre"] = d.pop("i_cre") if "i_cre" in d else d.get("cum_d_re")
        dc = host_inputs_C(inputs, c, [{"o_x1": np.zeros((D, T), np.float32)}] * NCORES)
        dc.pop("x1T", None)
        d.update(dc)
        dd = host_inputs_D(inputs, c, [{"o_x2": np.zeros((D, T), np.float32)}] * NCORES)
        dd.pop("x2T", None)
        d.update(dd)
        d["cum_d_re"] = d.pop("i_cre")
        d["cum_d_im"] = d.pop("i_cim")
        im2.append(d)
    r2 = run_bass_kernel_spmd(nc2, im2, core_ids=cores).results
    nc3 = build_fused_E1E2()
    im3 = []
    for c in cores:
        d = host_inputs_E1(inputs, c, r2, r2)
        de = host_inputs_E2(inputs, c, [{"o_x3": np.zeros((D, T), np.float32)}] * NCORES)
        de.pop("x3T", None)
        d.update(de)
        im3.append(d)
    r3 = run_bass_kernel_spmd(nc3, im3, core_ids=cores).results
    out = np.concatenate(
        [np.asarray(r3[c]["o_out"]).reshape(CH, 128, T).transpose(2, 0, 1).reshape(T, D)
         for c in cores], axis=0)
    return out.reshape(1, NCORES * T, D).astype(np.float32)


def kernel_2launchC(**inputs):
    cores = list(range(NCORES))
    nc1 = build_fused(upto=4, ext_names=("qfin_d", "kv_d", "ks_d", "edge_d", "x2_d"))
    r1 = run_bass_kernel_spmd(nc1, [host_inputs_fused(inputs, c) for c in cores], core_ids=cores).results
    r1m = [{"o_x2": rr["x2_d"], "o_qfin": rr["qfin_d"], "o_kv": rr["kv_d"],
            "o_ks": rr["ks_d"], "o_edge": rr["edge_d"]} for rr in r1]
    nc3 = build_fused_E1E2()
    im3 = []
    for c in cores:
        d = host_inputs_E1(inputs, c, r1m, r1m)
        de = host_inputs_E2(inputs, c, [{"o_x3": np.zeros((D, T), np.float32)}] * NCORES)
        de.pop("x3T", None)
        d.update(de)
        im3.append(d)
    r3 = run_bass_kernel_spmd(nc3, im3, core_ids=cores).results
    out = np.concatenate(
        [np.asarray(r3[c]["o_out"]).reshape(CH, 128, T).transpose(2, 0, 1).reshape(T, D)
         for c in cores], axis=0)
    return out.reshape(1, NCORES * T, D).astype(np.float32)


def kernel_2launchB(**inputs):
    cores = list(range(NCORES))
    ra = run_bass_kernel_spmd(build_launch_A(), [host_inputs_A(inputs, c) for c in cores],
                              core_ids=cores).results
    nc2 = build_fused_BE2()
    res2 = run_bass_kernel_spmd(nc2, [host_inputs_BE2(inputs, c, ra) for c in cores], core_ids=cores)
    out = np.concatenate(
        [np.asarray(res2.results[c]["o_out"]).reshape(CH, 128, T).transpose(2, 0, 1).reshape(T, D)
         for c in cores], axis=0)
    return out.reshape(1, NCORES * T, D).astype(np.float32)


def kernel_2launch(**inputs):
    cores = list(range(NCORES))
    nc1 = build_fused(upto=5, x3_external=True)
    res1 = run_bass_kernel_spmd(nc1, [host_inputs_fused(inputs, c) for c in cores], core_ids=cores)
    r1 = res1.results
    e1r = [{"o_x3": np.asarray(r1[c]["x3_d"])} for c in cores]
    nc2 = build_launch_E2()
    res2 = run_bass_kernel_spmd(nc2, [host_inputs_E2(inputs, c, e1r) for c in cores], core_ids=cores)
    out = np.concatenate(
        [np.asarray(res2.results[c]["o_out"]).reshape(CH, 128, T).transpose(2, 0, 1).reshape(T, D)
         for c in cores], axis=0)
    return out.reshape(1, NCORES * T, D).astype(np.float32)


def kernel(**inputs):
    return kernel_3launch(**inputs)
